# version 12
# baseline (speedup 1.0000x reference)
import numpy as np
import ml_dtypes
from contextlib import ExitStack

import concourse.bass as bass
import concourse.mybir as mybir
from concourse.bass_utils import run_bass_kernel_spmd

F32 = mybir.dt.float32
BF16 = mybir.dt.bfloat16
I8 = mybir.dt.int8
U8 = mybir.dt.uint8
AF = mybir.ActivationFunctionType
ALU = mybir.AluOpType
AX = mybir.AxisListType
BF = ml_dtypes.bfloat16

KB = 1024

FULL = dict(D=2048, T=2048, NH=16, HD=128, DQ=1536, DKV=512,
            E0=8192, E1=5632, KVG=4, TQ=512)
MINI = dict(D=512, T=512, NH=4, HD=128, DQ=384, DKV=128,
            E0=1024, E1=768, KVG=2, TQ=512)


def derive(cfg):
    c = dict(cfg)
    c["Dc"] = c["D"] // 128
    c["DQc"] = c["DQ"] // 128
    c["DKVc"] = c["DKV"] // 128
    c["E0c"] = c["E0"] // 128
    c["E1c"] = c["E1"] // 128
    c["NT"] = c["T"] // c["TQ"]
    c["NKT"] = c["T"] // 128
    c["KVD"] = c["KVG"] * c["HD"]
    c["KVDc"] = c["KVD"] // 128
    c["T2"] = c["T"] // 2
    c["NOFF"] = c["TQ"] // c["HD"]
    return c



def rope_tables(D, T):
    exps = np.arange(0, D, 2, dtype=np.float32)
    with np.errstate(over="ignore"):
        tn = np.float32(1.0) / (np.float32(10000.0) ** exps / np.float32(D))
    m = np.arange(T, dtype=np.float32)
    ang = np.outer(m, tn).astype(np.float32)
    cos = np.cos(ang).astype(np.float32)
    sin = np.sin(ang).astype(np.float32)
    d = np.arange(D)
    return (np.ascontiguousarray(cos[:, d // 2].T),
            np.ascontiguousarray(sin[:, d // 2].T))


def diag_masks(NOFF, TQ):
    k = np.arange(128)[:, None]
    q = np.arange(TQ)[None, :]
    out = np.zeros((128, NOFF * TQ), np.float32)
    for o in range(NOFF):
        out[:, o * TQ:(o + 1) * TQ] = np.where(o * 128 + k > q, -1.0e30, 0.0)
    return out.astype(BF)


def rope_perm():
    P = np.zeros((128, 128), np.float32)
    i = np.arange(64)
    P[2 * i, 2 * i + 1] = -1.0
    P[2 * i + 1, 2 * i] = 1.0
    return np.ascontiguousarray(P.T).astype(BF)


def pack_bias(b):
    b = np.asarray(b, np.float32)
    return np.ascontiguousarray(b.reshape(-1, 128).T)


def host_inputs(c, inp):
    t = lambda a: np.ascontiguousarray(np.asarray(a, np.float32).T)
    m = {}
    m["wdq"] = t(inp["L0_attn_w_down_q"])
    m["wuq"] = t(inp["L0_attn_w_up_q"])
    m["wqr"] = t(inp["L0_attn_wq_r"])
    m["wdkv"] = t(inp["L0_attn_w_down_kv"])
    m["wuk"] = t(inp["L0_attn_w_up_k"])
    m["wuv"] = t(inp["L0_attn_w_up_v"])
    m["wkr"] = t(inp["L0_attn_wk_r"])
    m["wo0"] = t(inp["L0_attn_wo"])
    m["bias"] = np.concatenate([
        pack_bias(inp["L0_attn_b_down_q"]),
        pack_bias(inp["L0_attn_b_up_q"]),
        pack_bias(inp["L0_attn_bq_r"]),
        pack_bias(inp["L0_attn_b_down_kv"]),
        pack_bias(inp["L0_attn_b_up_k"]),
        pack_bias(inp["L0_attn_bk_r"]),
        pack_bias(inp["L0_attn_bo"]),
        pack_bias(inp["L1_attn_bo"]),
    ], axis=1)
    m["bvrow"] = np.ascontiguousarray(
        np.asarray(inp["L0_attn_b_up_v"], np.float32)[None, :])
    m["wup"] = t(inp["L0_ffn_W_up"])
    m["wgt"] = t(inp["L0_ffn_W_gate"])
    m["wdn"] = t(inp["L0_ffn_W_down"])
    m["wq1"] = np.ascontiguousarray(np.asarray(inp["L1_attn_Wq"], np.float32))
    m["wk1"] = np.ascontiguousarray(np.asarray(inp["L1_attn_Wk"], np.float32))
    m["wv1"] = np.ascontiguousarray(np.asarray(inp["L1_attn_Wv"], np.float32))
    m["wo1"] = np.ascontiguousarray(np.asarray(inp["L1_attn_Wo"], np.float32))
    m["ww1"] = t(inp["L1_ffn_w1"])
    m["ww2"] = t(inp["L1_ffn_w2"])
    m["ww3"] = t(inp["L1_ffn_w3"])
    cexp, sexp = rope_tables(c["D"], c["T"])
    m["cexp"] = cexp.astype(BF)
    m["sexp"] = sexp.astype(BF)
    m["masks"] = diag_masks(c["NOFF"], c["TQ"])
    m["identb"] = np.eye(128, dtype=np.float32).astype(BF)
    m["identf"] = np.eye(128, dtype=np.float32)
    m["ptrope"] = rope_perm()
    return m



ENGINES = ("sync", "tensor", "vector", "scalar", "gpsimd")


class Chan:
    __slots__ = ("name", "sem", "n")

    def __init__(self, name, sem):
        self.name = name
        self.sem = sem
        self.n = 0


class Prog:
    def __init__(self, nc, stack):
        self.nc = nc
        self.stack = stack
        self.ops = {e: [] for e in ENGINES}
        self.chans = {}

    def chan(self, name):
        ch = self.chans.get(name)
        if ch is None:
            sem = self.stack.enter_context(self.nc.semaphore(name))
            ch = Chan(name, sem)
            self.chans[name] = ch
        return ch

    def wait(self, engine, ch, val=None, tag=""):
        if isinstance(ch, str):
            ch = self.chan(ch)
        v = ch.n if val is None else val
        if v <= 0:
            return 0
        self.ops[engine].append(("wait", ch, v, tag))
        return v

    def op(self, engine, fn, inc=None, inc_n=1, tag=""):
        if isinstance(inc, str):
            inc = self.chan(inc)
        self.ops[engine].append(("op", fn, inc, inc_n, tag))
        if inc is not None:
            inc.n += inc_n
        return inc.n if inc is not None else None

    def dma(self, out, in_, ch, q="sync", tag=""):
        if isinstance(ch, str):
            ch = self.chan(ch)
        if ch.n >= 16:
            self.wait(q, ch, ch.n, tag=tag + ":pace")
        return self.op(q, lambda eng, o=out, i=in_: eng.dma_start(o, i),
                       inc=ch, inc_n=16, tag=tag)

    def check(self):
        vals = {name: 0 for name in self.chans}
        ptr = {e: 0 for e in ENGINES}
        total = sum(len(v) for v in self.ops.values())
        done = 0
        progress = True
        while progress:
            progress = False
            for e in ENGINES:
                while ptr[e] < len(self.ops[e]):
                    item = self.ops[e][ptr[e]]
                    if item[0] == "wait":
                        _, ch, v, tag = item
                        if vals[ch.name] < v:
                            break
                    else:
                        _, fn, inc, inc_n, tag = item
                        if inc is not None:
                            vals[inc.name] += inc_n
                    ptr[e] += 1
                    done += 1
                    progress = True
        if done != total:
            lines = []
            for e in ENGINES:
                if ptr[e] < len(self.ops[e]):
                    item = self.ops[e][ptr[e]]
                    if item[0] == "wait":
                        _, ch, v, tag = item
                        lines.append(f"{e}[{ptr[e]}] waits {ch.name}>={v} "
                                     f"(have {vals[ch.name]}) tag={tag}")
            raise RuntimeError("deadlock in program build:\n" + "\n".join(lines))

    def emit(self):
        self.check()
        nc = self.nc
        own = {"vector": "v", "scalar": "s", "gpsimd": "g"}
        with nc.Block() as blk:
            def runner(name):
                def _run(eng):
                    och = self.chans.get(own.get(name, ""), None)
                    cnt = 0
                    for item in self.ops[name]:
                        if item[0] == "wait":
                            _, ch, v, tag = item
                            eng.wait_ge(ch.sem, v)
                        else:
                            _, fn, inc, inc_n, tag = item
                            if och is not None and cnt > 0:
                                eng.wait_ge(och.sem, cnt)
                            ins = fn(eng)
                            if inc is not None:
                                ins.then_inc(inc.sem, inc_n)
                            if inc is och:
                                cnt += inc_n
                    return
                return _run
            for name, deco in (("sync", blk.sync), ("tensor", blk.tensor),
                               ("vector", blk.vector), ("scalar", blk.scalar),
                               ("gpsimd", blk.gpsimd)):
                if self.ops[name]:
                    deco(runner(name))


class Arena:
    def __init__(self, nc, stack, size_bytes):
        self.nc = nc
        self.size = size_bytes
        self.back = stack.enter_context(
            nc.sbuf_tensor("arena", [128, size_bytes], U8))
        self.base = nc.lookup_mloc(self.back).addr
        self.names = {}

    def at(self, name, off_bytes, shape, dtype):
        key = (name, off_bytes)
        if key in self.names:
            return self.names[key]
        per_part = int(np.prod(shape[1:])) * mybir.dt.size(dtype)
        assert off_bytes + per_part <= self.size, \
            f"{name}: {off_bytes}+{per_part} > {self.size}"
        name = "sb_" + name
        hnd = self.nc.virtual_sbuf_tensor(name, shape, dtype)
        self.nc._tensor(hnd.name, shape, dtype, type="SB")
        mloc = self.nc.lookup_mloc(hnd)
        mloc.allocated = True
        mloc.addr = self.base + off_bytes
        self.names[key] = hnd
        return hnd



class MK:
    def __init__(self, cfg, debug_outputs=()):
        self.c = derive(cfg)
        self.stack = ExitStack()
        self.nc = bass.Bass("TRN2", target_bir_lowering=False)
        self.p = Prog(self.nc, self.stack)
        self.debug_outputs = set(debug_outputs)
        self.dram = {}
        self.sb = None
        self.ps = {}
        self.cols = {}
        self.ncols = 0

    def dram_in(self, name, shape, dtype=F32):
        h = self.nc.dram_tensor(name, list(shape), dtype, kind="ExternalInput")
        self.dram[name] = h.ap()

    def dram_tmp(self, name, shape, dtype=F32):
        kind = "ExternalOutput" if name in self.debug_outputs else "Internal"
        h = self.nc.dram_tensor(name, list(shape), dtype, kind=kind)
        self.dram[name] = h.ap()

    def dram_out(self, name, shape, dtype=F32):
        h = self.nc.dram_tensor(name, list(shape), dtype, kind="ExternalOutput")
        self.dram[name] = h.ap()

    def declare(self):
        c = self.c
        T = c["T"]
        self.dram_in("x", [c["D"], T])
        for n, s in [("wdq", [c["D"], c["DQ"]]), ("wuq", [c["DQ"], c["D"]]),
                     ("wqr", [c["DQ"], c["D"]]), ("wdkv", [c["D"], c["DKV"]]),
                     ("wuk", [c["DKV"], c["D"]]), ("wuv", [c["DKV"], c["D"]]),
                     ("wkr", [c["DKV"], c["D"]]), ("wo0", [c["D"], c["D"]]),
                     ("wup", [c["D"], c["E0"]]), ("wgt", [c["D"], c["E0"]]),
                     ("wdn", [c["E0"], c["D"]]),
                     ("wq1", [c["D"], c["D"]]), ("wk1", [c["D"], c["KVD"]]),
                     ("wv1", [c["D"], c["KVD"]]), ("wo1", [c["D"], c["D"]]),
                     ("ww1", [c["D"], c["E1"]]), ("ww2", [c["D"], c["E1"]]),
                     ("ww3", [c["E1"], c["D"]]),
                     ("identf", [128, 128]),
                     ("bvrow", [1, c["D"]])]:
            self.dram_in(n, s)
        nbias = c["DQc"] + 6 * c["Dc"] + c["DKVc"]
        self.dram_in("bias", [128, nbias])
        self.dram_in("masks", [128, c["NOFF"] * c["TQ"]], BF16)
        self.dram_in("cexp", [c["D"], T], BF16)
        self.dram_in("sexp", [c["D"], T], BF16)
        for n in ("identb", "ptrope"):
            self.dram_in(n, [128, 128], BF16)
        self.dram_tmp("res", [c["D"], T])
        self.dram_tmp("k0", [c["D"], T], BF16)
        self.dram_tmp("v0", [T, c["D"]], BF16)
        self.dram_tmp("ab", [c["D"], T], BF16)
        self.dram_tmp("hd0", [c["E0"], T], BF16)
        self.dram_tmp("xh8", [c["D"], T], BF16)
        self.dram_tmp("tdn", [c["E0"], c["D"]], BF16)
        self.dram_tmp("tw3", [c["E1"], c["D"]], BF16)
        self.dram_out("y", [c["D"], T])

        self.sb = Arena(self.nc, self.stack, 204 * KB)
        pa = self.stack.enter_context(self.nc.psum_tensor("PA", [128, 1024], F32))
        pb = self.stack.enter_context(self.nc.psum_tensor("PB", [128, 1024], F32))
        self.ps["A"] = pa[:]
        self.ps["B"] = pb[:]
        self.ps[0] = pa[:][:, 0:512]
        self.ps[1] = pa[:][:, 512:1024]
        self.ps[2] = pb[:][:, 0:512]
        self.ps[3] = pb[:][:, 512:1024]
        for i in range(4, 7):
            self.ps[i] = self.stack.enter_context(
                self.nc.psum_tensor(f"P{i}", [128, 512], F32))[:]
        self.ps["bf"] = self.stack.enter_context(
            self.nc.psum_tensor("PBF", [128, 1024], BF16))[:]

    def t_perm(self, name):
        c = self.c
        lay = {
            "identb": (0, [128, 128], BF16),
            "ptrope": (256, [128, 128], BF16),
            "identf": (512, [128, 128], F32),
            "onesb":  (1024, [128, 128], BF16),
            "onesf":  (1280, [128, 128], F32),
            "cols":   (1792, [128, 64], F32),
            "biast":  (2048, [128, 128], F32),
            "statr":  (2560, [128, 1024], F32),
            "masks":  (6656, [128, c["NOFF"] * c["TQ"]], BF16),
            "bvb":    (10752, [128, c["D"]], F32),
            "kcast":  (18944, [128, 128], BF16),
        }
        off, shape, dt = lay[name]
        return self.sb.at(name, off, shape, dt)[:]

    def col(self, name):
        if name not in self.cols:
            self.cols[name] = self.ncols
            self.ncols += 1
            assert self.ncols <= 64
        i = self.cols[name]
        return self.t_perm("cols")[:, i:i + 1]

    def t(self, name, off_kb, free_elems, dtype):
        return self.sb.at(name, int(off_kb * KB), [128, free_elems], dtype)[:]

    def memset_col(self, name, val):
        cl = self.col(name)
        self.p.op("vector", lambda e, a=cl, v=float(val): e.memset(a, v),
                  inc="v", tag=f"memset:{name}")

    def bias_col(self, idx):
        return self.t_perm("biast")[:, idx:idx + 1]

    def b_dq(self):  return 0
    def b_uq(self):  return self.c["DQc"]
    def b_qr(self):  return self.c["DQc"] + self.c["Dc"]
    def b_dkv(self): return self.c["DQc"] + 2 * self.c["Dc"]
    def b_uk(self):  return self.c["DQc"] + 2 * self.c["Dc"] + self.c["DKVc"]
    def b_kr(self):  return self.c["DQc"] + 3 * self.c["Dc"] + self.c["DKVc"]
    def b_o0(self):  return self.c["DQc"] + 4 * self.c["Dc"] + self.c["DKVc"]
    def b_o1(self):  return self.c["DQc"] + 5 * self.c["Dc"] + self.c["DKVc"]

    def prelude(self):
        p, c = self.p, self.c
        ad = p.chan("ad0")
        for n in ("identb", "ptrope", "identf", "masks"):
            p.dma(self.t_perm(n), self.dram[n], ad, tag=f"ld:{n}")
        nbias = c["DQc"] + 6 * c["Dc"] + c["DKVc"]
        p.dma(self.t_perm("biast")[:, 0:nbias], self.dram["bias"], ad,
              tag="ld:bias")
        p.op("vector", lambda e: e.memset(self.t_perm("onesb"), 1.0), inc="v")
        p.op("vector", lambda e: e.memset(self.t_perm("onesf"), 1.0), inc="v")
        for nm, v in [("one", 1.0), ("neg1", -1.0), ("invD", 1.0 / c["D"]),
                      ("eps", 1e-5), ("neghalf", -0.5),
                      ("sm", 1.0 / np.sqrt(c["HD"])),
                      ("half", 0.5),
                      ("negsm", -1.0 / np.sqrt(c["HD"])),
                      ("invE0", 1.0 / (c["D"] * c["E0"])),
                      ("invE1", 1.0 / (c["D"] * c["E1"])),
                      ("eps8", 1e-8)]:
            self.memset_col(nm, v)
        for e in ("tensor", "vector", "scalar"):
            p.wait(e, ad, tag="prelude:consts")
            p.wait(e, "v", tag="prelude:memsets")

    def ln_in(self, src, tt, xhat, xhat_stride, sta, tag):
        p, c = self.p, self.c
        Dc, TQ = c["Dc"], c["TQ"]
        ad = p.chan("ad0")
        for chn in ("resw", "ody"):
            if chn in p.chans:
                p.wait("sync", chn, tag=f"{tag}:res-landed")
        p.wait("sync", "v", tag=f"{tag}:sta-free")
        src3 = src.rearrange("(cc p) t -> p cc t", p=128)[
            :, :, tt * TQ:(tt + 1) * TQ]
        dst3 = sta.rearrange("p (cc t) -> p cc t", cc=Dc)
        p.dma(dst3, src3, ad, tag=f"{tag}:ld")
        p.wait("vector", ad, tag=f"{tag}:v-sta")
        p.wait("scalar", ad, tag=f"{tag}:s-sta")
        rawc = [self.t("rawc0", 196, 512, BF16), self.t("rawc1", 197, 512, BF16)]
        sqc = [self.t("sqc0", 198, 512, BF16), self.t("sqc1", 199, 512, BF16)]
        statr = self.t_perm("statr")
        mrow = statr[0:1, 0:TQ]
        vrow = statr[32:33, 0:TQ]
        rrow = statr[0:1, 512:512 + TQ]
        xrow = statr[32:33, 512:512 + TQ]
        PM, PQ, PMB, PRB = self.ps[0], self.ps[1], self.ps[2], self.ps[3]
        p.wait("tensor", "s", tag=f"{tag}:banks-quiet")
        mm_marks = {}
        for cc in range(Dc):
            b2 = cc % 2
            sl = sta[:, cc * TQ:(cc + 1) * TQ]
            if cc >= 2:
                p.wait("vector", "t", mm_marks[cc - 2], tag=f"{tag}:rawfree{cc}")
                p.wait("scalar", "t", mm_marks[cc - 2], tag=f"{tag}:sqfree{cc}")
            vv = p.op("vector", lambda e, s=sl, d=rawc[b2]: e.tensor_copy(d, s),
                      inc="v", tag=f"{tag}:cast{cc}")
            ss = p.op("scalar", lambda e, s=sl, d=sqc[b2]:
                      e.activation(d, s, AF.Square), inc="s",
                      tag=f"{tag}:sq{cc}")
            p.wait("tensor", "v", vv, tag=f"{tag}:mm-m{cc}")
            p.op("tensor", lambda e, r=rawc[b2], s=(cc == 0), st=(cc == Dc - 1):
                 e.matmul(PM[0:1, 0:TQ], self.t_perm("onesb")[:, 0:1], r,
                          start=s, stop=st),
                 inc="t", tag=f"{tag}:mmm{cc}")
            p.wait("tensor", "s", ss, tag=f"{tag}:mm-q{cc}")
            mm_marks[cc] = p.op(
                "tensor", lambda e, r=sqc[b2], s=(cc == 0), st=(cc == Dc - 1):
                e.matmul(PQ[0:1, 0:TQ], self.t_perm("onesb")[:, 0:1], r,
                         start=s, stop=st),
                inc="t", tag=f"{tag}:mmq{cc}")
        tmm = p.chans["t"].n
        p.wait("vector", "t", tmm, tag=f"{tag}:stats")
        p.op("vector", lambda e: e.tensor_scalar(
            mrow, PM[0:1, 0:TQ], self.col("invD")[0:1, :], None, ALU.mult),
            inc="v", tag=f"{tag}:m")
        p.op("vector", lambda e: e.tensor_scalar(
            vrow, PQ[0:1, 0:TQ], self.col("invD")[0:1, :],
            self.col("eps")[0:1, :], ALU.mult, ALU.add),
            inc="v", tag=f"{tag}:q")
        p.op("vector", lambda e: e.tensor_tensor(xrow, mrow, mrow, ALU.mult),
             inc="v", tag=f"{tag}:m2")
        vfin = p.op("vector", lambda e: e.tensor_tensor(
            vrow, vrow, xrow, ALU.subtract), inc="v", tag=f"{tag}:var")
        p.wait("scalar", "v", vfin, tag=f"{tag}:rstd")
        p.op("scalar", lambda e: e.activation(xrow, vrow, AF.Ln), inc="s",
             tag=f"{tag}:ln")
        sfin = p.op("scalar", lambda e: e.activation(
            rrow, xrow, AF.Exp, scale=self.col("neghalf")[0:1, :]),
            inc="s", tag=f"{tag}:exp")
        p.wait("tensor", "v", vfin, tag=f"{tag}:bc-m")
        p.wait("tensor", "s", sfin, tag=f"{tag}:bc-r")
        p.op("tensor", lambda e: e.matmul(
            PMB[:, 0:TQ], self.t_perm("onesf")[0:1, 0:128], mrow,
            start=True, stop=True), inc="t", tag=f"{tag}:bcm")
        tbc = p.op("tensor", lambda e: e.matmul(
            PRB[:, 0:TQ], self.t_perm("onesf")[0:1, 0:128], rrow,
            start=True, stop=True), inc="t", tag=f"{tag}:bcr")
        p.wait("vector", "t", tbc, tag=f"{tag}:norm")
        tmpf = self.t("tmpf", 200, 512, F32)
        for cc in range(Dc):
            sl = sta[:, cc * TQ:(cc + 1) * TQ]
            if xhat_stride == TQ:
                dst = xhat[:, cc * TQ:(cc + 1) * TQ]
            else:
                dst = xhat[:, cc * xhat_stride + tt * TQ:
                           cc * xhat_stride + (tt + 1) * TQ]
            p.op("vector", lambda e, s=sl:
                 e.tensor_tensor(tmpf, s, PMB[:, 0:TQ], ALU.subtract),
                 inc="v", tag=f"{tag}:sub{cc}")
            p.op("vector", lambda e, d=dst:
                 e.tensor_tensor(d, tmpf, PRB[:, 0:TQ], ALU.mult),
                 inc="v", tag=f"{tag}:mul{cc}")
        vn = p.chans["v"].n
        p.wait("tensor", "v", vn, tag=f"{tag}:bank-free")
        return vn

    def ln_post_add(self, src_slab, chunk_stride, tt, res_src, res_dst,
                    out_ch, tag, sta_ready=None, make_xhat=None):
        p, c = self.p, self.c
        Dc, TQ = c["Dc"], c["TQ"]
        sta = sta_ready if sta_ready is not None else \
            self.t("staD", 84, Dc * TQ, F32)
        if sta_ready is None:
            ad = p.chan("ad1")
            for chn in ("resw", "ody"):
                if chn in p.chans:
                    p.wait("sync", chn, tag=f"{tag}:res-landed")
            p.wait("sync", "v", tag=f"{tag}:sta-free")
            src3 = res_src.rearrange("(cc p) t -> p cc t", p=128)[
                :, :, tt * TQ:(tt + 1) * TQ]
            p.dma(sta.rearrange("p (cc t) -> p cc t", cc=Dc), src3, ad,
                  tag=f"{tag}:ldres")
        toff = (tt * TQ) % chunk_stride if chunk_stride > TQ else 0

        def srcsl(cc):
            if chunk_stride > TQ:
                return src_slab[:, cc * chunk_stride + toff:
                                cc * chunk_stride + toff + TQ]
            return src_slab[:, cc * TQ:(cc + 1) * TQ]

        statr = self.t_perm("statr")
        mrow = statr[0:1, 0:TQ]
        vrow = statr[32:33, 0:TQ]
        rrow = statr[0:1, 512:512 + TQ]
        xrow = statr[32:33, 512:512 + TQ]
        PM, PQ, PMB, PRB = self.ps[0], self.ps[1], self.ps[2], self.ps[3]
        p.wait("tensor", "s", tag=f"{tag}:banks-quiet")
        sqc = [self.t("sqc0", 198, 512, BF16), self.t("sqc1", 199, 512, BF16)]
        mm_marks = {}
        for cc in range(Dc):
            b2 = cc % 2
            if cc >= 2:
                p.wait("vector", "t", mm_marks[cc - 2], tag=f"{tag}:sqfree{cc}")
            vv = p.op("vector", lambda e, s=srcsl(cc), d=sqc[b2]:
                      e.tensor_tensor(d, s, s, ALU.mult), inc="v",
                      tag=f"{tag}:sq{cc}")
            p.op("tensor", lambda e, r=srcsl(cc), s=(cc == 0),
                 st=(cc == Dc - 1):
                 e.matmul(PM[0:1, 0:TQ], self.t_perm("onesb")[:, 0:1], r,
                          start=s, stop=st),
                 inc="t", tag=f"{tag}:mmm{cc}")
            p.wait("tensor", "v", vv, tag=f"{tag}:wsq{cc}")
            mm_marks[cc] = p.op(
                "tensor", lambda e, r=sqc[b2], s=(cc == 0), st=(cc == Dc - 1):
                e.matmul(PQ[0:1, 0:TQ], self.t_perm("onesb")[:, 0:1], r,
                         start=s, stop=st),
                inc="t", tag=f"{tag}:mmq{cc}")
        tmm = p.chans["t"].n
        p.wait("vector", "t", tmm, tag=f"{tag}:stats")
        p.op("vector", lambda e: e.tensor_scalar(
            mrow, PM[0:1, 0:TQ], self.col("invD")[0:1, :], None, ALU.mult),
            inc="v", tag=f"{tag}:m")
        p.op("vector", lambda e: e.tensor_scalar(
            vrow, PQ[0:1, 0:TQ], self.col("invD")[0:1, :],
            self.col("eps")[0:1, :], ALU.mult, ALU.add), inc="v",
            tag=f"{tag}:q")
        p.op("vector", lambda e: e.tensor_tensor(xrow, mrow, mrow, ALU.mult),
             inc="v", tag=f"{tag}:m2")
        vfin = p.op("vector", lambda e: e.tensor_tensor(
            vrow, vrow, xrow, ALU.subtract), inc="v", tag=f"{tag}:var")
        p.wait("scalar", "v", vfin, tag=f"{tag}:rstd")
        p.op("scalar", lambda e: e.activation(xrow, vrow, AF.Ln), inc="s",
             tag=f"{tag}:ln")
        sfin = p.op("scalar", lambda e: e.activation(
            rrow, xrow, AF.Exp, scale=self.col("neghalf")[0:1, :]), inc="s",
            tag=f"{tag}:exp")
        p.wait("tensor", "v", vfin, tag=f"{tag}:bc")
        p.wait("tensor", "s", sfin, tag=f"{tag}:bc2")
        p.op("tensor", lambda e: e.matmul(
            PMB[:, 0:TQ], self.t_perm("onesf")[0:1, 0:128], mrow,
            start=True, stop=True), inc="t", tag=f"{tag}:bcm")
        tbc = p.op("tensor", lambda e: e.matmul(
            PRB[:, 0:TQ], self.t_perm("onesf")[0:1, 0:128], rrow,
            start=True, stop=True), inc="t", tag=f"{tag}:bcr")
        p.wait("vector", "t", tbc, tag=f"{tag}:norm")
        if sta_ready is None:
            p.wait("vector", "ad1", tag=f"{tag}:res-tile")
        tmpf = self.t("tmpf", 200, 512, F32)
        tmpf2 = self.t("tmpf2", 202, 512, F32)
        for cc in range(Dc):
            stasl = sta[:, cc * TQ:(cc + 1) * TQ]
            p.op("vector", lambda e, s=srcsl(cc):
                 e.tensor_tensor(tmpf, s, PMB[:, 0:TQ], ALU.subtract),
                 inc="v", tag=f"{tag}:sub{cc}")
            p.op("vector", lambda e:
                 e.tensor_tensor(tmpf2, tmpf, PRB[:, 0:TQ], ALU.mult),
                 inc="v", tag=f"{tag}:mul{cc}")
            p.op("vector", lambda e, d=stasl:
                 e.tensor_tensor(d, d, tmpf2, ALU.add),
                 inc="v", tag=f"{tag}:add{cc}")
        vn = p.chans["v"].n
        p.wait("tensor", "v", vn, tag=f"{tag}:bank-free")
        out3 = res_dst.rearrange("(cc p) t -> p cc t", p=128)[
            :, :, tt * TQ:(tt + 1) * TQ]
        p.wait("sync", "v", vn, tag=f"{tag}:st-wait")
        p.dma(out3, sta.rearrange("p (cc t) -> p cc t", cc=Dc), out_ch,
              tag=f"{tag}:st")
        if make_xhat is not None:
            self.ln_make_xhat(sta, make_xhat, tt, f"{tag}:mx")


    def ln_make_xhat(self, sta, xbuf, tt, tag):
        p, c = self.p, self.c
        Dc, TQ = c["Dc"], c["TQ"]
        statr = self.t_perm("statr")
        mrow = statr[0:1, 0:TQ]
        vrow = statr[32:33, 0:TQ]
        rrow = statr[0:1, 512:512 + TQ]
        xrow = statr[32:33, 512:512 + TQ]
        PM, PQ, PMB, PRB = self.ps[0], self.ps[1], self.ps[2], self.ps[3]
        p.wait("tensor", "s", tag=f"{tag}:banks-quiet")
        p.wait("scalar", "v", tag=f"{tag}:sta-ready")
        rawc = [self.t("rawc0", 196, 512, BF16), self.t("rawc1", 197, 512, BF16)]
        sqc = [self.t("sqc0", 198, 512, BF16), self.t("sqc1", 199, 512, BF16)]
        mm_marks = {}
        for cc in range(Dc):
            b2 = cc % 2
            sl = sta[:, cc * TQ:(cc + 1) * TQ]
            if cc >= 2:
                p.wait("vector", "t", mm_marks[cc - 2], tag=f"{tag}:rawfree{cc}")
                p.wait("scalar", "t", mm_marks[cc - 2], tag=f"{tag}:sqfree{cc}")
            vv = p.op("vector", lambda e, s=sl, d=rawc[b2]: e.tensor_copy(d, s),
                      inc="v", tag=f"{tag}:cast{cc}")
            ss = p.op("scalar", lambda e, s=sl, d=sqc[b2]:
                      e.activation(d, s, AF.Square), inc="s",
                      tag=f"{tag}:sq{cc}")
            p.wait("tensor", "v", vv, tag=f"{tag}:mm-m{cc}")
            p.op("tensor", lambda e, r=rawc[b2], s=(cc == 0), st=(cc == Dc - 1):
                 e.matmul(PM[0:1, 0:TQ], self.t_perm("onesb")[:, 0:1], r,
                          start=s, stop=st),
                 inc="t", tag=f"{tag}:mmm{cc}")
            p.wait("tensor", "s", ss, tag=f"{tag}:mm-q{cc}")
            mm_marks[cc] = p.op(
                "tensor", lambda e, r=sqc[b2], s=(cc == 0), st=(cc == Dc - 1):
                e.matmul(PQ[0:1, 0:TQ], self.t_perm("onesb")[:, 0:1], r,
                         start=s, stop=st),
                inc="t", tag=f"{tag}:mmq{cc}")
        tmm = p.chans["t"].n
        p.wait("vector", "t", tmm, tag=f"{tag}:stats")
        p.op("vector", lambda e: e.tensor_scalar(
            mrow, PM[0:1, 0:TQ], self.col("invD")[0:1, :], None, ALU.mult),
            inc="v", tag=f"{tag}:m")
        p.op("vector", lambda e: e.tensor_scalar(
            vrow, PQ[0:1, 0:TQ], self.col("invD")[0:1, :],
            self.col("eps")[0:1, :], ALU.mult, ALU.add),
            inc="v", tag=f"{tag}:q")
        p.op("vector", lambda e: e.tensor_tensor(xrow, mrow, mrow, ALU.mult),
             inc="v", tag=f"{tag}:m2")
        vfin = p.op("vector", lambda e: e.tensor_tensor(
            vrow, vrow, xrow, ALU.subtract), inc="v", tag=f"{tag}:var")
        p.wait("scalar", "v", vfin, tag=f"{tag}:rstd")
        p.op("scalar", lambda e: e.activation(xrow, vrow, AF.Ln), inc="s",
             tag=f"{tag}:ln")
        sfin = p.op("scalar", lambda e: e.activation(
            rrow, xrow, AF.Exp, scale=self.col("neghalf")[0:1, :]),
            inc="s", tag=f"{tag}:exp")
        p.wait("tensor", "v", vfin, tag=f"{tag}:bc-m")
        p.wait("tensor", "s", sfin, tag=f"{tag}:bc-r")
        p.op("tensor", lambda e: e.matmul(
            PMB[:, 0:TQ], self.t_perm("onesf")[0:1, 0:128], mrow,
            start=True, stop=True), inc="t", tag=f"{tag}:bcm")
        tbc = p.op("tensor", lambda e: e.matmul(
            PRB[:, 0:TQ], self.t_perm("onesf")[0:1, 0:128], rrow,
            start=True, stop=True), inc="t", tag=f"{tag}:bcr")
        p.wait("vector", "t", tbc, tag=f"{tag}:norm")
        p.wait("vector", "xhw", tag=f"{tag}:xbuf-war")
        tmpf = self.t("tmpf", 200, 512, F32)
        for cc in range(Dc):
            sl = sta[:, cc * TQ:(cc + 1) * TQ]
            p.op("vector", lambda e, s=sl:
                 e.tensor_tensor(tmpf, s, PMB[:, 0:TQ], ALU.subtract),
                 inc="v", tag=f"{tag}:sub{cc}")
            p.op("vector", lambda e, d=xbuf[:, cc * TQ:(cc + 1) * TQ]:
                 e.tensor_tensor(d, tmpf, PRB[:, 0:TQ], ALU.mult),
                 inc="v", tag=f"{tag}:mul{cc}")
        vn = p.chans["v"].n
        p.wait("tensor", "v", vn, tag=f"{tag}:bank-free")
        p.wait("sync", "v", vn, tag=f"{tag}:st-wait")
        out3 = self.dram["xh8"].rearrange("(cc p) t -> p cc t", p=128)[
            :, :, tt * c["TQ"]:(tt + 1) * c["TQ"]]
        p.dma(out3, xbuf.rearrange("p (cc t) -> p cc t", cc=Dc), "xhw",
              tag=f"{tag}:st")

    def proj(self, name, wsrc, n_oc, n_kc, NTn, rhs_fn, evac_fn, woff_kb,
             oc0=0, drain=False):
        p, c = self.p, self.c
        TQ = c["TQ"]
        wst = [self.t(f"{name}wst{i}", woff_kb + i * (n_kc * 128 * 4) / KB,
                      n_kc * 128, F32) for i in range(2)]
        bfo = woff_kb + 2 * (n_kc * 128 * 4) / KB
        wbf = [self.t(f"{name}wbf{i}", bfo + i * (n_kc * 128 * 2) / KB,
                      n_kc * 128, BF16) for i in range(2)]
        wd = [p.chan("wd0"), p.chan("wd1")]
        free_marks = {}
        cast_marks = {}
        p.wait("tensor", "s", tag=f"{name}:banks-quiet")
        p.wait("tensor", "v", tag=f"{name}:banks-quiet-v")
        for oc in range(n_oc):
            sl = oc % 2
            src3 = wsrc.rearrange("(kc p) o -> p kc o", p=128)[
                :, :, (oc0 + oc) * 128:(oc0 + oc + 1) * 128]
            dst3 = wst[sl].rearrange("p (kc j) -> p kc j", kc=n_kc)
            if oc < 2:
                p.wait("sync", "t", tag=f"{name}:fresh{oc}")
                p.wait("sync", "v", tag=f"{name}:fresh-v{oc}")
            else:
                p.wait("sync", "v", cast_marks[oc - 2],
                       tag=f"{name}:wstfree{oc}")
            p.dma(dst3, src3, wd[sl], tag=f"{name}:wld{oc}")
            p.wait("vector", wd[sl], tag=f"{name}:wcast-wait{oc}")
            if oc >= 2:
                chn, v = free_marks[(oc - 2, NTn - 1)]
                p.wait("vector", chn, v, tag=f"{name}:wbffree{oc}")
            cast_marks[oc] = p.op(
                "vector", lambda e, d=wbf[sl], s=wst[sl]: e.tensor_copy(d, s),
                inc="v", tag=f"{name}:wcast{oc}")
            if drain:
                self.bg_step(3)
            p.wait("tensor", "v", cast_marks[oc], tag=f"{name}:mm-wait{oc}")
            for n in range(NTn):
                g = oc * NTn + n
                bank = self.ps[g % 2]
                if g >= 2:
                    pk = ((g - 2) // NTn, (g - 2) % NTn)
                    chn, v = free_marks[pk]
                    p.wait("tensor", chn, v, tag=f"{name}:bankfree{oc},{n}")
                tmk = None
                for kc in range(n_kc):
                    tmk = p.op("tensor", lambda e, b=bank, w=wbf[sl], k=kc,
                               nn=n, s=(kc == 0), st=(kc == n_kc - 1):
                               e.matmul(b[:, 0:TQ], w[:, k * 128:(k + 1) * 128],
                                        rhs_fn(k, nn), start=s, stop=st),
                               inc="t", tag=f"{name}:mm{oc},{n},{kc}")
                ev = evac_fn(oc, n, bank[:, 0:TQ], tmk)
                free_marks[(oc, n)] = ev
        return free_marks

    def phase_A(self):
        p, c = self.p, self.c
        TQ, NT, Dc, T = c["TQ"], c["NT"], c["Dc"], c["T"]
        xhat = self.t("xhatA", 84, Dc * T, BF16)
        sta = self.t("staA", 148, Dc * TQ, F32)
        for tt in range(NT):
            self.ln_in(self.dram["x"], tt, xhat, T, sta, f"A:ln{tt}")

        ctq = self.t("ctq", 20, c["DQc"] * T, BF16)
        ctkv = self.t("ctkv", 68, c["DKVc"] * T, BF16)

        def rhs(kc, n):
            return xhat[:, kc * T + n * TQ: kc * T + (n + 1) * TQ]

        def mk_evac(dst_slab, bias0):
            def evac(oc, n, ps, tmk):
                p.wait("scalar", "t", tmk, tag=f"ct:ev-wait{oc},{n}")
                sv = p.op("scalar", lambda e,
                          d=dst_slab[:, oc * T + n * TQ:oc * T + (n + 1) * TQ],
                          s=ps, b=self.bias_col(bias0 + oc):
                          e.activation(d, s, AF.Identity, bias=b),
                          inc="s", tag=f"ct:ev{oc},{n}")
                return ("s", sv)
            return evac

        self.proj("ctq", self.dram["wdq"], c["DQc"], Dc, NT, rhs,
                  mk_evac(ctq, self.b_dq()), woff_kb=148)
        self.proj("ctkv", self.dram["wdkv"], c["DKVc"], Dc, NT, rhs,
                  mk_evac(ctkv, self.b_dkv()), woff_kb=148)

    def phase_B(self):
        p, c = self.p, self.c
        TQ, NT, Dc, KVc, T = c["TQ"], c["NT"], c["Dc"], c["DKVc"], c["T"]
        ctkv = self.t("ctkv", 68, KVc * T, BF16)
        wst = self.t("Bwst", 84, KVc * c["D"], F32)
        wukb = self.t("Bwuk", 116, KVc * c["D"], BF16)
        wkrb = self.t("Bwkr", 132, KVc * c["D"], BF16)
        wd = p.chan("wd0")
        p.wait("sync", "t", tag="B:fresh")
        p.wait("sync", "v", tag="B:fresh-v")
        p.dma(wst.rearrange("p (kc o) -> p kc o", kc=KVc),
              self.dram["wuk"].rearrange("(kc p) o -> p kc o", p=128),
              wd, tag="B:lduk")
        p.wait("vector", wd, tag="B:castuk")
        vuk = p.op("vector", lambda e: e.tensor_copy(wukb, wst), inc="v",
                   tag="B:castuk")
        p.wait("sync", "v", vuk, tag="B:wst-free1")
        p.dma(wst.rearrange("p (kc o) -> p kc o", kc=KVc),
              self.dram["wkr"].rearrange("(kc p) o -> p kc o", p=128),
              wd, tag="B:ldkr")
        p.wait("vector", wd, tag="B:castkr")
        vkr = p.op("vector", lambda e: e.tensor_copy(wkrb, wst), inc="v",
                   tag="B:castkr")
        csl = [self.t(f"Bcs{i}", 148 + i * (2 * T * 2) / KB, 2 * T, BF16)
               for i in range(2)]
        csd = [p.chan("csd0"), p.chan("csd1")]
        ssd = [p.chan("ssd0"), p.chan("ssd1")]
        krS = self.t("BkrS", 180, T, BF16)
        kslab = self.t("Bkslab", 184, T, BF16)
        vslab = self.t("Bvslab", 188, c["D"], BF16)
        tmpf = self.t("tmpf", 200, 512, F32)
        tmpf2 = self.t("tmpf2", 202, 512, F32)
        p.wait("tensor", "v", vkr, tag="B:wait-casts")
        kw = p.chan("kvw")
        PK, PR, PP = self.ps[0], self.ps[1], self.ps[2]
        vdone = {}
        for oc in range(Dc):
            sl = oc % 2
            if oc >= 1:
                p.wait("vector", kw, tag=f"B:kslab-war{oc}")
            if oc >= 2:
                p.wait("sync", "v", vdone[oc - 2], tag=f"B:csfree{oc}")
            p.dma(csl[sl][:, 0:T],
                  self.dram["cexp"][oc * 128:(oc + 1) * 128, :], csd[sl],
                  tag=f"B:ldc{oc}")
            p.dma(csl[sl][:, T:2 * T],
                  self.dram["sexp"][oc * 128:(oc + 1) * 128, :], ssd[sl],
                  tag=f"B:lds{oc}")
            for n in range(NT):
                tk = None
                for kc in range(KVc):
                    tk = p.op("tensor", lambda e, k=kc, nn=n, o=oc,
                              s=(kc == 0), st=(kc == KVc - 1):
                              e.matmul(PK[:, 0:TQ],
                                       wukb[:, k * c["D"] + o * 128:
                                            k * c["D"] + (o + 1) * 128],
                                       ctkv[:, k * T + nn * TQ:
                                            k * T + (nn + 1) * TQ],
                                       start=s, stop=st),
                              inc="t", tag=f"B:mmk{oc},{n},{kc}")
                tr = None
                for kc in range(KVc):
                    tr = p.op("tensor", lambda e, k=kc, nn=n, o=oc,
                              s=(kc == 0), st=(kc == KVc - 1):
                              e.matmul(PR[:, 0:TQ],
                                       wkrb[:, k * c["D"] + o * 128:
                                            k * c["D"] + (o + 1) * 128],
                                       ctkv[:, k * T + nn * TQ:
                                            k * T + (nn + 1) * TQ],
                                       start=s, stop=st),
                              inc="t", tag=f"B:mmr{oc},{n},{kc}")
                p.wait("scalar", "t", tr, tag=f"B:evr{oc},{n}")
                sv = p.op("scalar", lambda e, nn=n,
                          bb_=self.bias_col(self.b_kr() + oc):
                          e.activation(krS[:, nn * TQ:(nn + 1) * TQ],
                                       PR[:, 0:TQ], AF.Identity, bias=bb_),
                          inc="s", tag=f"B:evr{oc},{n}")
                p.wait("tensor", "s", sv, tag=f"B:pm{oc},{n}")
                tp = p.op("tensor", lambda e, nn=n:
                          e.matmul(PP[:, 0:TQ], self.t_perm("ptrope"),
                                   krS[:, nn * TQ:(nn + 1) * TQ],
                                   start=True, stop=True),
                          inc="t", tag=f"B:pmm{oc},{n}")
                p.wait("vector", "t", tp, tag=f"B:rope{oc},{n}")
                p.wait("vector", csd[sl], tag=f"B:cs{oc},{n}")
                p.wait("vector", ssd[sl], tag=f"B:ss{oc},{n}")
                ch_ = csl[sl][:, n * TQ:(n + 1) * TQ]
                sh_ = csl[sl][:, T + n * TQ:T + (n + 1) * TQ]
                p.op("vector", lambda e, nn=n, cc=ch_:
                     e.tensor_tensor(tmpf, krS[:, nn * TQ:(nn + 1) * TQ], cc,
                                     ALU.mult), inc="v", tag=f"B:r1{oc},{n}")
                p.op("vector", lambda e, ss=sh_:
                     e.tensor_tensor(tmpf2, PP[:, 0:TQ], ss, ALU.mult),
                     inc="v", tag=f"B:r2{oc},{n}")
                p.op("vector", lambda e:
                     e.tensor_tensor(tmpf, tmpf, tmpf2, ALU.add),
                     inc="v", tag=f"B:r2b{oc},{n}")
                vv = p.op("vector", lambda e, nn=n,
                          bb_=self.bias_col(self.b_uk() + oc):
                          e.scalar_tensor_tensor(
                              kslab[:, nn * TQ:(nn + 1) * TQ], PK[:, 0:TQ],
                              bb_, tmpf, ALU.add, ALU.add),
                          inc="v", tag=f"B:r3{oc},{n}")
                p.wait("tensor", "v", vv, tag=f"B:bank{oc},{n}")
            vdone[oc] = vv
            p.wait("sync", "v", vv, tag=f"B:stk{oc}")
            p.dma(self.dram["k0"][oc * 128:(oc + 1) * 128, :], kslab[:, 0:T],
                  kw, tag=f"B:stk{oc}")
        wuvb = self.t("Bwuv", 116, KVc * c["D"], BF16)
        p.wait("sync", "t", tag="B:uk-done")
        p.dma(wst.rearrange("p (kc o) -> p kc o", kc=KVc),
              self.dram["wuv"].rearrange("(kc p) o -> p kc o", p=128),
              wd, tag="B:lduv")
        p.wait("vector", wd, tag="B:castuv")
        p.wait("vector", "t", tag="B:castuv-t")
        vuv = p.op("vector", lambda e: e.tensor_copy(wuvb, wst), inc="v",
                   tag="B:castuv")
        bvb = self.t_perm("bvb")
        brow = self.t("Bbrow", 132, c["D"], F32)
        ad = p.chan("ad0")
        p.dma(brow[0:1, 0:c["D"]], self.dram["bvrow"], ad, tag="B:ldbv")
        p.wait("tensor", ad, tag="B:bvb")
        p.wait("tensor", "v", vuv, tag="B:v-ready")
        nD = max(1, c["D"] // TQ)
        for n in range(nD):
            if n >= 1:
                p.wait("tensor", "s", tag=f"B:bvbank{n}")
            tb = p.op("tensor", lambda e, nn=n:
                      e.matmul(self.ps[4][:, 0:TQ],
                               self.t_perm("onesf")[0:1, 0:128],
                               brow[0:1, nn * TQ:(nn + 1) * TQ],
                               start=True, stop=True),
                      inc="t", tag=f"B:bvmm{n}")
            p.wait("scalar", "t", tb, tag=f"B:bvev{n}")
            p.op("scalar", lambda e, nn=n:
                 e.activation(bvb[:, nn * TQ:(nn + 1) * TQ],
                              self.ps[4][:, 0:TQ], AF.Copy),
                 inc="s", tag=f"B:bvev{n}")
        sbv = p.chans["s"].n
        p.wait("tensor", "s", sbv, tag="B:bv-done")
        for ttk in range(c["NKT"]):
            if ttk >= 1:
                p.wait("vector", kw, tag=f"B:vslab-war{ttk}")
            for n in range(nD):
                b = self.ps[n % 4]
                tmk = None
                for kc in range(KVc):
                    tmk = p.op("tensor", lambda e, k=kc, nn=n, t2=ttk, P=b,
                               s=(kc == 0), st=(kc == KVc - 1):
                               e.matmul(P[:, 0:TQ],
                                        ctkv[:, k * T + t2 * 128:
                                             k * T + (t2 + 1) * 128],
                                        wuvb[:, k * c["D"] + nn * TQ:
                                             k * c["D"] + (nn + 1) * TQ],
                                        start=s, stop=st),
                               inc="t", tag=f"B:vmm{ttk},{n},{kc}")
                p.wait("vector", "t", tmk, tag=f"B:vev{ttk},{n}")
                vv = p.op("vector", lambda e, nn=n, P=b:
                          e.tensor_tensor(vslab[:, nn * TQ:(nn + 1) * TQ],
                                          P[:, 0:TQ],
                                          bvb[:, nn * TQ:(nn + 1) * TQ],
                                          ALU.add),
                          inc="v", tag=f"B:vev{ttk},{n}")
            p.wait("sync", "v", vv, tag=f"B:stv{ttk}")
            p.dma(self.dram["v0"][ttk * 128:(ttk + 1) * 128, :],
                  vslab[:, 0:c["D"]], kw, tag=f"B:stv{ttk}")
            p.wait("tensor", "v", vv, tag=f"B:vbank{ttk}")

    def head_bound(self, qh, kh_slice, mh_col, tag):
        p, c = self.p, self.c
        TQ, NT = c["TQ"], c["NT"]
        statr = self.t_perm("statr")
        mhs = statr[0:1, 1002:1003]
        kcast = self.t_perm("kcast")
        tmpf = self.t("tmpf", 200, 512, F32)
        PN = self.ps[6]
        p.wait("tensor", "s", tag=f"{tag}:quiet")
        p.wait("tensor", "v", tag=f"{tag}:quiet-v")
        tmm = p.op("tensor", lambda e: e.matmul(
            PN[:, 0:TQ], kh_slice, qh[:, (NT - 1) * TQ:NT * TQ],
            start=True, stop=True), inc="t", tag=f"{tag}:smm")
        p.wait("vector", "t", tmm, tag=f"{tag}:red")
        p.op("vector", lambda e: e.reduce_max(
            tmpf[:, 0:1], PN[:, 0:TQ], AX.X), inc="v", tag=f"{tag}:red")
        vv = p.op("vector", lambda e: e.tensor_copy(kcast[:, 0:1],
                                                    tmpf[:, 0:1]),
                  inc="v", tag=f"{tag}:cast")
        p.wait("tensor", "v", vv, tag=f"{tag}:tr")
        tt_ = p.op("tensor", lambda e: e.transpose(
            self.ps["bf"][0:1, 0:128], kcast[:, 0:1], self.t_perm("identb")),
            inc="t", tag=f"{tag}:tr")
        p.wait("vector", "t", tt_, tag=f"{tag}:red2")
        p.op("vector", lambda e: e.reduce_max(
            mhs, self.ps["bf"][0:1, 0:128], AX.X), inc="v", tag=f"{tag}:red2")
        vmh = p.op("vector", lambda e: e.tensor_scalar(
            mhs, mhs, self.col("negsm")[0:1, :], None, ALU.mult), inc="v",
            tag=f"{tag}:neg")
        p.wait("tensor", "v", vmh, tag=f"{tag}:bc")
        tmh = p.op("tensor", lambda e: e.matmul(
            PN[:, 0:1], self.t_perm("onesf")[0:1, 0:128], mhs,
            start=True, stop=True), inc="t", tag=f"{tag}:bc")
        p.wait("vector", "t", tmh, tag=f"{tag}:cp")
        vmh = p.op("vector", lambda e: e.tensor_copy(
            mh_col, PN[:, 0:1]), inc="v", tag=f"{tag}:cp")
        p.wait("scalar", "v", vmh, tag=f"{tag}:ready")
        p.wait("tensor", "v", vmh, tag=f"{tag}:bankfree")

    def phase_C(self):
        p, c = self.p, self.c
        TQ, NT, T, Dc = c["TQ"], c["NT"], c["T"], c["Dc"]
        DQc, NH, NKT = c["DQc"], c["NH"], c["NKT"]
        ctq = self.t("ctq", 20, DQc * T, BF16)
        kh = [self.t(f"Ckh{i}", 84 + i * (T * 2) / KB, T, BF16)
              for i in range(2)]
        vh = [self.t(f"Cvh{i}", 92 + i * (T * 2) / KB, T, BF16)
              for i in range(2)]
        qh = self.t("Cqh", 100, T, BF16)
        ctxh = self.t("Cctxh", 104, T, BF16)
        qrS = self.t("CqrS", 108, T, BF16)
        csl = [self.t(f"Ccs{i}", 112 + i * (2 * T * 2) / KB, 2 * T, BF16)
               for i in range(2)]
        pT = [self.t(f"CpT{i}", 164 + i * 2, 2 * TQ, BF16) for i in range(2)]
        rbf = self.t("Crbf", 168, TQ, F32)
        wstq = [self.t(f"Cwstq{i}", 128 + i * (DQc * 128 * 4) / KB,
                       DQc * 128, F32) for i in range(2)]
        wstr = [self.t(f"Cwstr{i}", 140 + i * (DQc * 128 * 4) / KB,
                       DQc * 128, F32) for i in range(2)]
        wqb = [self.t(f"Cwqb{i}", 152 + i * (DQc * 128 * 2) / KB,
                      DQc * 128, BF16) for i in range(2)]
        wrb = [self.t(f"Cwrb{i}", 158 + i * (DQc * 128 * 2) / KB,
                      DQc * 128, BF16) for i in range(2)]
        tmpf = self.t("tmpf", 200, 512, F32)
        tmpf2 = self.t("tmpf2", 202, 512, F32)
        statr = self.t_perm("statr")
        wd = [p.chan("wd0"), p.chan("wd1")]
        wdr = [p.chan("wdr0"), p.chan("wdr1")]
        khd = [p.chan("khd0"), p.chan("khd1")]
        vhd = [p.chan("vhd0"), p.chan("vhd1")]
        csd = [p.chan("csd0"), p.chan("csd1")]
        ssd = [p.chan("ssd0"), p.chan("ssd1")]
        abw = p.chan("abw")
        p.wait("sync", "kvw", tag="C:kv-landed")
        mh_col = self.col("mh0")
        self._smark = {}
        self._rbe_prev = None
        self._ctx_marks = {}
        cast_marks = {}
        head_done = {}
        rope_done = {}
        for h in range(NH):
            sl = h % 2
            if h >= 2:
                p.wait("sync", "s", cast_marks[h - 2], tag=f"C:wstfree{h}")
            src3 = self.dram["wuq"].rearrange("(kc p) o -> p kc o", p=128)[
                :, :, h * 128:(h + 1) * 128]
            p.dma(wstq[sl].rearrange("p (kc j) -> p kc j", kc=DQc), src3,
                  wd[sl], tag=f"C:wldq{h}")
            src3 = self.dram["wqr"].rearrange("(kc p) o -> p kc o", p=128)[
                :, :, h * 128:(h + 1) * 128]
            p.dma(wstr[sl].rearrange("p (kc j) -> p kc j", kc=DQc), src3,
                  wdr[sl], tag=f"C:wldr{h}")
            if h >= 2:
                p.wait("sync", "t", head_done[h - 2], tag=f"C:khfree{h}")
            p.dma(kh[sl], self.dram["k0"][h * 128:(h + 1) * 128, :],
                  khd[sl], tag=f"C:ldkh{h}")
            vsrc = bass.AP(self.dram["v0"].tensor, h * 128,
                           [[c["D"], 128], [128 * c["D"], NKT], [1, 128]])
            p.dma(vh[sl].rearrange("p (t j) -> p t j", t=NKT), vsrc,
                  vhd[sl], tag=f"C:ldvh{h}")
            if h >= 2:
                p.wait("sync", "v", rope_done[h - 2], tag=f"C:csfree{h}")
            p.dma(csl[sl][:, 0:T],
                  self.dram["cexp"][h * 128:(h + 1) * 128, :], csd[sl],
                  tag=f"C:ldc{h}")
            p.dma(csl[sl][:, T:2 * T],
                  self.dram["sexp"][h * 128:(h + 1) * 128, :], ssd[sl],
                  tag=f"C:lds{h}")
            self.bg_step(1, allow_collapse=True)
            p.wait("scalar", wd[sl], tag=f"C:wcast{h}")
            p.wait("scalar", wdr[sl], tag=f"C:wcastr-dma{h}")
            p.op("scalar", lambda e, s=sl: e.copy(wqb[s], wstq[s]),
                 inc="s", tag=f"C:wcastq{h}")
            cast_marks[h] = p.op(
                "scalar", lambda e, s=sl: e.copy(wrb[s], wstr[s]),
                inc="s", tag=f"C:wcastr{h}")
            p.wait("tensor", "s", cast_marks[h], tag=f"C:q-wait{h}")
            PQm, PRm, PPm = self.ps[0], self.ps[1], self.ps[2]
            for n in range(NT):
                for kc in range(DQc):
                    p.op("tensor", lambda e, k=kc, nn=n, s=sl,
                         st=(kc == DQc - 1), s0=(kc == 0):
                         e.matmul(PQm[:, 0:TQ],
                                  wqb[s][:, k * 128:(k + 1) * 128],
                                  ctq[:, k * T + nn * TQ:k * T + (nn + 1) * TQ],
                                  start=s0, stop=st),
                         inc="t", tag=f"C:qmm{h},{n},{kc}")
                tr = None
                for kc in range(DQc):
                    tr = p.op("tensor", lambda e, k=kc, nn=n, s=sl,
                              st=(kc == DQc - 1), s0=(kc == 0):
                              e.matmul(PRm[:, 0:TQ],
                                       wrb[s][:, k * 128:(k + 1) * 128],
                                       ctq[:, k * T + nn * TQ:
                                            k * T + (nn + 1) * TQ],
                                       start=s0, stop=st),
                              inc="t", tag=f"C:rmm{h},{n},{kc}")
                p.wait("scalar", "t", tr, tag=f"C:evr{h},{n}")
                sv = p.op("scalar", lambda e, nn=n,
                          bb_=self.bias_col(self.b_qr() + h):
                          e.activation(qrS[:, nn * TQ:(nn + 1) * TQ],
                                       PRm[:, 0:TQ], AF.Identity, bias=bb_),
                          inc="s", tag=f"C:evr{h},{n}")
                p.wait("tensor", "s", sv, tag=f"C:pm{h},{n}")
                tp = p.op("tensor", lambda e, nn=n:
                          e.matmul(PPm[:, 0:TQ], self.t_perm("ptrope"),
                                   qrS[:, nn * TQ:(nn + 1) * TQ],
                                   start=True, stop=True),
                          inc="t", tag=f"C:pmm{h},{n}")
                p.wait("vector", "t", tp, tag=f"C:rope{h},{n}")
                p.wait("vector", csd[sl], tag=f"C:cs{h},{n}")
                p.wait("vector", ssd[sl], tag=f"C:ss{h},{n}")
                ch_ = csl[sl][:, n * TQ:(n + 1) * TQ]
                sh_ = csl[sl][:, T + n * TQ:T + (n + 1) * TQ]
                p.op("vector", lambda e, nn=n, cc=ch_:
                     e.tensor_tensor(tmpf, qrS[:, nn * TQ:(nn + 1) * TQ], cc,
                                     ALU.mult), inc="v", tag=f"C:r1{h},{n}")
                p.op("vector", lambda e, ss=sh_:
                     e.tensor_tensor(tmpf2, PPm[:, 0:TQ], ss, ALU.mult),
                     inc="v", tag=f"C:r2{h},{n}")
                p.op("vector", lambda e:
                     e.tensor_tensor(tmpf, tmpf, tmpf2, ALU.add),
                     inc="v", tag=f"C:r2b{h},{n}")
                vq = p.op("vector", lambda e, nn=n,
                          bb_=self.bias_col(self.b_uq() + h):
                          e.scalar_tensor_tensor(
                              qh[:, nn * TQ:(nn + 1) * TQ], PQm[:, 0:TQ],
                              bb_, tmpf, ALU.add, ALU.add),
                          inc="v", tag=f"C:r3{h},{n}")
                p.wait("tensor", "v", vq, tag=f"C:qbank{h},{n}")
            rope_done[h] = vq
            p.wait("vector", "abw", tag=f"C:ctx-war{h}")
            p.wait("tensor", khd[sl], tag=f"C:kvh-ready{h}")
            p.wait("tensor", vhd[sl], tag=f"C:vh-ready{h}")
            self.head_bound(qh[:, 0:T], kh[sl][:, 0:128], mh_col, f"C:hb{h}")
            SP = [self.ps["A"], self.ps["B"]]
            for qc in range(NT):
                PC = self.ps[4] if qc % 2 == 0 else self.ps[5]
                PS6 = self.ps[6]
                exp_marks = {}
                npair = NKT // 2

                def smm(pi2, qv=qc):
                    if pi2 >= npair:
                        return
                    if pi2 >= 2:
                        chn, vv_ = exp_marks[pi2 - 2]
                        p.wait("tensor", chn, vv_, tag=f"C:Sfree{h},{qv},{pi2}")
                    mk = None
                    for half in range(2):
                        mk = p.op("tensor", lambda e, k=2 * pi2 + half, s=sl,
                                  q=qv, bb=pi2 % 2, hh=half:
                                  e.matmul(SP[bb][:, hh * TQ:(hh + 1) * TQ],
                                           kh[s][:, k * 128:(k + 1) * 128],
                                           qh[:, q * TQ:(q + 1) * TQ],
                                           start=True, stop=True),
                                  inc="t", tag=f"C:smm{h},{qv},{pi2},{half}")
                    self._smark[(h, qv, pi2)] = mk

                smm(0)
                for pi in range(npair):
                    b = pi % 2
                    smm(pi + 1)
                    p.wait("scalar", "t", self._smark[(h, qc, pi)],
                           tag=f"C:exp{h},{qc},{pi}")
                    sv = p.op("scalar", lambda e, bb=b, pp=pi:
                              e.activation(pT[pp % 2], SP[bb][:, 0:2 * TQ],
                                           AF.Exp, bias=mh_col,
                                           scale=self.col("sm")),
                              inc="s", tag=f"C:exp{h},{qc},{pi}")
                    exp_marks[pi] = ("s", sv)
                    p.wait("tensor", "s", sv, tag=f"C:pv{h},{qc},{pi}")
                    if pi == 0 and self._rbe_prev is not None:
                        p.wait("tensor", "s", self._rbe_prev,
                               tag=f"C:sumfree{h},{qc}")
                    if pi == 0 and self._ctx_marks.get(qc % 2) is not None:
                        p.wait("tensor", "v", self._ctx_marks[qc % 2],
                               tag=f"C:pcfree{h},{qc}")
                    for half in range(2):
                        kt = 2 * pi + half
                        p.op("tensor", lambda e, k=kt, s=sl, pp=pi, hh=half,
                             P=PC, s0=(kt == 0), st=(kt == NKT - 1):
                             e.matmul(P[:, 0:TQ],
                                      vh[s][:, k * 128:(k + 1) * 128],
                                      pT[pp % 2][:, hh * TQ:(hh + 1) * TQ],
                                      start=s0, stop=st),
                             inc="t", tag=f"C:pv{h},{qc},{kt}")
                        tsum = p.op("tensor", lambda e, pp=pi, hh=half,
                                    s0=(kt == 0), st=(kt == NKT - 1):
                                    e.matmul(PS6[0:1, 0:TQ],
                                             self.t_perm("onesb")[:, 0:1],
                                             pT[pp % 2][:, hh * TQ:(hh + 1) * TQ],
                                             start=s0, stop=st),
                                    inc="t", tag=f"C:sum{h},{qc},{kt}")
                p.wait("vector", "t", tsum, tag=f"C:recip{h},{qc}")
                srow = statr[0:1, 0:TQ]
                vv = p.op("vector", lambda e: e.reciprocal(
                    srow, PS6[0:1, 0:TQ]), inc="v", tag=f"C:recip{h},{qc}")
                p.wait("tensor", "v", vv, tag=f"C:rb{h},{qc}")
                tb = p.op("tensor", lambda e: e.matmul(
                    PS6[:, 0:TQ], self.t_perm("onesf")[0:1, 0:128], srow,
                    start=True, stop=True), inc="t", tag=f"C:rb{h},{qc}")
                p.wait("scalar", "t", tb, tag=f"C:rbe{h},{qc}")
                self._rbe_prev = p.op("scalar", lambda e: e.activation(
                    rbf, PS6[:, 0:TQ], AF.Copy), inc="s",
                    tag=f"C:rbe{h},{qc}")
                p.wait("vector", "s", self._rbe_prev, tag=f"C:ctx{h},{qc}")
                self._ctx_marks[qc % 2] = p.op(
                    "vector", lambda e, q=qc, P=PC: e.tensor_tensor(
                        ctxh[:, q * TQ:(q + 1) * TQ], P[:, 0:TQ], rbf,
                        ALU.mult),
                    inc="v", tag=f"C:ctx{h},{qc}")
            head_done[h] = p.chans["t"].n
            p.wait("sync", "v", tag=f"C:stab{h}")
            p.dma(self.dram["ab"][h * 128:(h + 1) * 128, :], ctxh[:, 0:T],
                  abw, tag=f"C:stab{h}")

    def phase_D(self, wo_name, bias0, res_src, res_dst, out_ch_name, tag):
        p, c = self.p, self.c
        TQ, Dc, T = c["TQ"], c["Dc"], c["T"]
        T2 = max(c["T2"], TQ)
        nhalves = T // T2
        p.wait("sync", "abw", tag=f"{tag}:ab-landed")
        asb = self.t("Dasb", 20, Dc * T2, BF16)
        aout = self.t("Daout", 52, Dc * T2, BF16)
        ad = p.chan("ad0")
        out_ch = p.chan(out_ch_name)
        NTh = T2 // TQ
        for half in range(nhalves):
            t0 = half * T2
            src3 = self.dram["ab"].rearrange("(cc p) t -> p cc t", p=128)[
                :, :, t0:t0 + T2]
            p.wait("sync", "t", tag=f"{tag}:asb-free{half}")
            p.wait("scalar", "v", tag=f"{tag}:aout-free{half}")
            p.dma(asb.rearrange("p (cc t) -> p cc t", cc=Dc), src3, ad,
                  tag=f"{tag}:lda{half}")
            p.wait("vector", ad, tag=f"{tag}:a-ready{half}")
            vr = p.op("vector", lambda e: e.engine_nop(), inc="v",
                      tag=f"{tag}:a-ready{half}")
            p.wait("tensor", "v", vr, tag=f"{tag}:a-ready-t{half}")
            self.bg_step(2, allow_collapse=True)

            def rhs(kc, n):
                return asb[:, kc * T2 + n * TQ: kc * T2 + (n + 1) * TQ]

            def evac(oc, n, ps, tmk):
                p.wait("scalar", "t", tmk, tag=f"{tag}:ev-wait{oc},{n}")
                sv = p.op("scalar", lambda e,
                          d=aout[:, oc * T2 + n * TQ:oc * T2 + (n + 1) * TQ],
                          s=ps, b=self.bias_col(bias0 + oc):
                          e.activation(d, s, AF.Identity, bias=b),
                          inc="s", tag=f"{tag}:ev{oc},{n}")
                return ("s", sv)

            self.proj(f"{tag}wo{half}", self.dram[wo_name], Dc, Dc, NTh,
                      rhs, evac, woff_kb=116, drain=True)
            sev = p.chans["s"].n
            p.wait("vector", "s", sev, tag=f"{tag}:ln-wait{half}")
            xbuf = self.t("Dxbuf", 148, Dc * TQ, BF16)
            for ttl in range(NTh):
                self.ln_post_add(aout, T2, half * NTh + ttl, res_src, res_dst,
                                 out_ch, f"{tag}:pa{half}.{ttl}",
                                 make_xhat=xbuf)
        return out_ch

    def bg_bufs(self, ring=2):
        b = getattr(self, "bg_region", 170)
        if ring == 4:
            gst = [self.t(f"bgr{i}_{b}", b + 4 * i, 1024, F32)
                   for i in range(4)]
            return gst, None, None, None
        gst = [self.t(f"bgst0_{b}", b, 1024, F32),
               self.t(f"bgst1_{b}", b + 4, 1024, F32)]
        pos = self.t(f"bgpos_{b}", b + 8, 1024, BF16)
        neg = self.t(f"bgneg_{b}", b + 10, 1024, BF16)
        bft = [self.t(f"bgbf0_{b}", b + 12, 1024, BF16),
               self.t(f"bgbf1_{b}", b + 14, 1024, BF16)]
        return gst, pos, neg, bft

    def gamma_tiles_gen(self, wname, gname, tag):
        p, c = self.p, self.c
        ap = self.dram[wname]
        K, O = ap.tensor.shape
        if O <= 1024:
            CW = O
        else:
            CW = next(w for w in (1024, 512, 256, 128) if O % w == 0)
        ncw = O // CW
        gd = [p.chan(f"g{i}") for i in range(4)]
        gacc = self.col(gname)
        gtmp = self.col(gname + "_t")
        p.op("vector", lambda e: e.memset(gacc, 0.0), inc="v", tag=f"{tag}:0")
        vprev = {}
        i = 0
        for rc in range(K // 128):
            for j in range(ncw):
                sl = i % 4
                if i == 0:
                    p.wait("gpsimd", "v", tag=f"{tag}:prev-user")
                    for chn in ("tzw0", "tzw1"):
                        if chn in p.chans:
                            p.wait("gpsimd", chn, tag=f"{tag}:tern-quiet")
                if i >= 4:
                    p.wait("gpsimd", "v", vprev[i - 4], tag=f"{tag}:free{i}")
                gst, _, _, _ = self.bg_bufs(ring=4)
                p.dma(gst[sl][:, 0:CW],
                      ap[rc * 128:(rc + 1) * 128, j * CW:(j + 1) * CW],
                      gd[sl], q="gpsimd", tag=f"{tag}:ld{i}")
                p.wait("vector", gd[sl], tag=f"{tag}:red{i}")
                p.op("vector", lambda e, g_=gst[sl], w=CW: e.tensor_reduce(
                    gtmp, g_[:, 0:w], AX.X, ALU.add,
                    apply_absolute_value=True), inc="v", tag=f"{tag}:red{i}")
                vprev[i] = p.op("vector", lambda e: e.tensor_tensor(
                    gacc, gacc, gtmp, ALU.add), inc="v", tag=f"{tag}:acc{i}")
                self._g_last_acc = vprev[i]
                i += 1
                yield

    def gamma_collapse_gen(self, gname, invN_name, tag):
        p = self.p
        gacc = self.col(gname)
        p.wait("tensor", "v", self._g_last_acc, tag=f"{tag}:tr")
        p.wait("tensor", "s", tag=f"{tag}:tr-s")
        tt_ = p.op("tensor", lambda e: e.transpose(
            self.ps[0][0:1, 0:128], gacc, self.t_perm("identf")),
            inc="t", tag=f"{tag}:tr")
        statr = self.t_perm("statr")
        g1 = statr[0:1, 1003:1004]
        g2 = statr[0:1, 1004:1005]
        p.wait("vector", "t", tt_, tag=f"{tag}:col")
        p.op("vector", lambda e: e.tensor_reduce(
            g1, self.ps[0][0:1, 0:128], AX.X, ALU.add), inc="v",
            tag=f"{tag}:col")
        p.op("vector", lambda e: e.tensor_scalar(
            g1, g1, self.col(invN_name)[0:1, :], self.col("eps8")[0:1, :],
            ALU.mult, ALU.add), inc="v", tag=f"{tag}:gam")
        vv = p.op("vector", lambda e: e.reciprocal(g2, g1), inc="v",
                  tag=f"{tag}:rec")
        p.wait("tensor", "v", vv, tag=f"{tag}:bc")
        p.op("tensor", lambda e: e.matmul(
            self.ps[0][:, 0:1], self.t_perm("onesf")[0:1, 0:128], g1,
            start=True, stop=True), inc="t", tag=f"{tag}:bc1")
        tb = p.op("tensor", lambda e: e.matmul(
            self.ps[1][:, 0:1], self.t_perm("onesf")[0:1, 0:128], g2,
            start=True, stop=True), inc="t", tag=f"{tag}:bc2")
        p.wait("vector", "t", tb, tag=f"{tag}:bcv")
        p.op("vector", lambda e: e.tensor_copy(
            self.col(gname + "_s"), self.ps[0][:, 0:1]), inc="v",
            tag=f"{tag}:bcv1")
        vv = p.op("vector", lambda e: e.tensor_copy(
            self.col(gname + "_r"), self.ps[1][:, 0:1]), inc="v",
            tag=f"{tag}:bcv2")
        p.wait("tensor", "v", vv, tag=f"{tag}:done")
        p.wait("scalar", "v", vv, tag=f"{tag}:done-s")
        yield

    def tern_tiles_gen(self, wname, cache, gname, tag):
        p, c = self.p, self.c
        ap = self.dram[wname]
        cap = self.dram[cache]
        K, O = ap.tensor.shape
        if O <= 1024:
            CW = O
        else:
            CW = next(w for w in (1024, 512, 256, 128) if O % w == 0)
        ncw = O // CW
        grecip_col = self.col(gname + "_r")
        gd = [p.chan("g0"), p.chan("g1")]
        tw = [p.chan("tzw0"), p.chan("tzw1")]
        sprev = {}
        vprev = {}
        i = 0
        for rc in range(K // 128):
            for j in range(ncw):
                sl = i % 2
                if i == 0:
                    p.wait("gpsimd", "v", tag=f"{tag}:prev-user")
                if i >= 2:
                    p.wait("gpsimd", "v", sprev[i - 2], tag=f"{tag}:free{i}")
                gst, pos, neg, bft = self.bg_bufs()
                p.dma(gst[sl][:, 0:CW],
                      ap[rc * 128:(rc + 1) * 128, j * CW:(j + 1) * CW],
                      gd[sl], q="gpsimd", tag=f"{tag}:ld{i}")
                p.wait("vector", gd[sl], tag=f"{tag}:cmp{i}")
                p.op("vector", lambda e, g_=gst[sl], po=pos, w=CW:
                     e.tensor_scalar(po[:, 0:w], g_[:, 0:w], grecip_col,
                                     self.col("half")[:, 0:1],
                                     ALU.mult, ALU.is_ge),
                     inc="v", tag=f"{tag}:pos{i}")
                sprev[i] = p.op("vector", lambda e, g_=gst[sl], ne=neg, w=CW:
                                e.tensor_scalar(ne[:, 0:w], g_[:, 0:w],
                                                grecip_col,
                                                self.col("neghalf")[:, 0:1],
                                                ALU.mult, ALU.is_le),
                                inc="v", tag=f"{tag}:neg{i}")
                p.wait("vector", tw[sl], tag=f"{tag}:bffree{i}")
                vprev[i] = p.op("vector", lambda e, b_=bft[sl], po=pos,
                                ne=neg, w=CW:
                                e.tensor_tensor(b_[:, 0:w], po[:, 0:w],
                                                ne[:, 0:w], ALU.subtract),
                                inc="v", tag=f"{tag}:sub{i}")
                p.wait("gpsimd", "v", vprev[i], tag=f"{tag}:st{i}")
                p.dma(cap[rc * 128:(rc + 1) * 128, j * CW:(j + 1) * CW],
                      bft[sl][:, 0:CW], tw[sl], q="gpsimd", tag=f"{tag}:st{i}")
                i += 1
                yield

    def bg_enqueue(self):
        q = []
        for wname, gname, invN in (("wup", "gup", "invE0"),
                                   ("wgt", "ggt", "invE0"),
                                   ("wdn", "gdn", "invE0")):
            q.append((self.gamma_tiles_gen(wname, gname, "G" + gname), False))
            q.append((self.gamma_collapse_gen(gname, invN, "Gc" + gname), True))
        q.append((self.tern_tiles_gen("wdn", "tdn", "gdn", "Tdn"), False))
        for wname, gname, invN in (("ww1", "gw1", "invE1"),
                                   ("ww2", "gw2", "invE1"),
                                   ("ww3", "gw3", "invE1")):
            q.append((self.gamma_tiles_gen(wname, gname, "G" + gname), False))
            q.append((self.gamma_collapse_gen(gname, invN, "Gc" + gname), True))
        q.append((self.tern_tiles_gen("ww3", "tw3", "gw3", "Tw3"), False))
        self.bg = q
        self.bg_total = len(q)

    def bg_step(self, n, allow_collapse=False):
        while n > 0 and self.bg:
            gen, is_col = self.bg[0]
            if is_col and not allow_collapse:
                return
            try:
                next(gen)
                n -= 1
            except StopIteration:
                self.bg.pop(0)

    def bg_finish(self, keep_last):
        while len(self.bg) > keep_last:
            gen, is_col = self.bg[0]
            try:
                next(gen)
            except StopIteration:
                self.bg.pop(0)

    def ffn(self, layer, res_src, res_dst, out_ch_name, tag):
        p, c = self.p, self.c
        TQ, Dc, T, NT = c["TQ"], c["Dc"], c["T"], c["NT"]
        Ec = c["E0c"] if layer == 0 else c["E1c"]
        wu_name, wg_name = (("wup", "wgt") if layer == 0 else ("ww1", "ww2"))
        dn_cache = "tdn" if layer == 0 else "tw3"
        gu = "gup" if layer == 0 else "gw1"
        gg = "ggt" if layer == 0 else "gw2"
        gd_ = "gdn" if layer == 0 else "gw3"
        out_ch = p.chan(out_ch_name)
        xh = self.t("Fxh", 20, Dc * T, BF16)
        sta = self.t("staD", 84, Dc * TQ, F32)
        wst = [[self.t(f"Fw{a}{i}", 116 + (a * 3 + i) * 8, Dc * 128, F32)
                for i in range(3)] for a in range(2)]
        pos = self.t("Fpos", 164, 2048, BF16)
        neg = self.t("Fneg", 168, 2048, BF16)
        wub = [self.t(f"Fwub{i}", 172 + i * 4, Dc * 128, BF16)
               for i in range(2)]
        wgb = [self.t(f"Fwgb{i}", 180 + i * 4, Dc * 128, BF16)
               for i in range(2)]
        hbuf = [self.t(f"Fhbuf{i}", 188 + i * 4, T, BF16) for i in range(2)]
        tsil = self.t("Ftsil", 202, 512, F32)
        wd = [p.chan(f"wd{i}") for i in range(2)]
        wdg = [p.chan(f"wdg{i}") for i in range(2)]
        wd3 = [p.chan("wd0"), p.chan("wd1"), p.chan("wd2")]
        wg3 = [p.chan("wdg0"), p.chan("wdg1"), p.chan("wdg2")]
        hw = [p.chan("hw0"), p.chan("hw1")]
        p.wait("sync", "xhw", tag=f"{tag}:xh-landed")
        xld = [p.chan("ad0"), p.chan("ad1")]
        for tt in range(NT):
            p.wait("sync", "t", tag=f"{tag}:xh-free{tt}")
            p.dma(xh.rearrange("p (cc t) -> p cc t", cc=Dc)[
                      :, :, tt * TQ:(tt + 1) * TQ],
                  self.dram["xh8"].rearrange("(cc p) t -> p cc t", p=128)[
                      :, :, tt * TQ:(tt + 1) * TQ],
                  xld[tt % 2], tag=f"{tag}:ldxh{tt}")
        hid = self.dram["hd0"]
        for chn in ("g0", "g1", "g2", "g3", "tzw0", "tzw1", "xhw"):
            if chn in p.chans:
                p.wait("vector", chn, tag=f"{tag}:bg-quiet-start")

        def w_dma(ec):
            if ec >= Ec:
                return
            s3 = ec % 3
            p.dma(wst[0][s3].rearrange("p (kc j) -> p kc j", kc=Dc),
                  self.dram[wu_name].rearrange("(kc p) o -> p kc o", p=128)[
                      :, :, ec * 128:(ec + 1) * 128],
                  wd3[s3], tag=f"{tag}:ldu{ec}")
            p.dma(wst[1][s3].rearrange("p (kc j) -> p kc j", kc=Dc),
                  self.dram[wg_name].rearrange("(kc p) o -> p kc o", p=128)[
                      :, :, ec * 128:(ec + 1) * 128],
                  wg3[s3], tag=f"{tag}:ldg{ec}")

        tern_marks = {}

        def w_tern(ec):
            if ec >= Ec:
                return
            s3, sl = ec % 3, ec % 2
            n_ = Dc * 128
            p.wait("vector", wd3[s3], tag=f"{tag}:tu{ec}")
            p.wait("vector", wg3[s3], tag=f"{tag}:tg{ec}")
            if ec >= 2:
                p.wait("vector", "t", mm_marks[ec - 2], tag=f"{tag}:wbf{ec}")
            vv = None
            for a, (wb, gcol) in enumerate(((wub, gu), (wgb, gg))):
                p.op("vector", lambda e, s=s3, aa=a, w=n_,
                     rc=self.col(gcol + "_r"):
                     e.tensor_scalar(pos[:, 0:w], wst[aa][s], rc,
                                     self.col("half")[:, 0:1],
                                     ALU.mult, ALU.is_ge),
                     inc="v", tag=f"{tag}:pos{ec},{a}")
                p.op("vector", lambda e, s=s3, aa=a, w=n_,
                     rc=self.col(gcol + "_r"):
                     e.tensor_scalar(neg[:, 0:w], wst[aa][s], rc,
                                     self.col("neghalf")[:, 0:1],
                                     ALU.mult, ALU.is_le),
                     inc="v", tag=f"{tag}:neg{ec},{a}")
                vv = p.op("vector", lambda e, s=sl, w=n_, wb_=wb:
                          e.tensor_tensor(wb_[s][:, 0:w], pos[:, 0:w],
                                          neg[:, 0:w], ALU.subtract),
                          inc="v", tag=f"{tag}:sub{ec},{a}")
            tern_marks[ec] = vv

        mm_marks = {}
        prev_stt = None
        w_dma(0)
        w_dma(1)
        w_tern(0)
        p.wait("tensor", "ad0", tag=f"{tag}:xh-rdy0")
        p.wait("tensor", "ad1", tag=f"{tag}:xh-rdy1")
        for ec in range(Ec):
            sl = ec % 2
            w_dma(ec + 2)
            w_tern(ec + 1)
            p.wait("tensor", "v", tern_marks[ec], tag=f"{tag}:mm{ec}")
            vv = None
            for tt in range(NT):
                g = ec * NT + tt
                PU = self.ps[0] if g % 2 == 0 else self.ps[2]
                PG = self.ps[1] if g % 2 == 0 else self.ps[3]
                if g >= 2:
                    key = ((g - 2) // NT, (g - 2) % NT)
                    if key in mm_marks:
                        chn, v_ = mm_marks[key]
                        p.wait("tensor", chn, v_, tag=f"{tag}:bank{ec},{tt}")
                tg = None
                for kc in range(Dc):
                    p.op("tensor", lambda e, k=kc, s=sl, P=PU, n2=tt,
                         s0=(kc == 0), st=(kc == Dc - 1):
                         e.matmul(P[:, 0:TQ], wub[s][:, k * 128:(k + 1) * 128],
                                  xh[:, k * T + n2 * TQ:k * T + (n2 + 1) * TQ],
                                  start=s0, stop=st),
                         inc="t", tag=f"{tag}:umm{ec},{tt},{kc}")
                for kc in range(Dc):
                    tg = p.op("tensor", lambda e, k=kc, s=sl, P=PG, n2=tt,
                              s0=(kc == 0), st=(kc == Dc - 1):
                              e.matmul(P[:, 0:TQ],
                                       wgb[s][:, k * 128:(k + 1) * 128],
                                       xh[:, k * T + n2 * TQ:
                                          k * T + (n2 + 1) * TQ],
                                       start=s0, stop=st),
                              inc="t", tag=f"{tag}:gmm{ec},{tt},{kc}")
                p.wait("scalar", "t", tg, tag=f"{tag}:act{ec},{tt}")
                if prev_stt is not None:
                    p.wait("scalar", "v", prev_stt,
                           tag=f"{tag}:tsil-war{ec},{tt}")
                if layer == 0:
                    sv = p.op("scalar", lambda e, P=PU:
                              e.activation(tsil, P[:, 0:TQ], AF.Silu,
                                           scale=self.col(gu + "_s")),
                              inc="s", tag=f"{tag}:silu{ec},{tt}")
                    gscol = self.col(gg + "_s")
                else:
                    sv = p.op("scalar", lambda e, P=PU:
                              e.activation(tsil, P[:, 0:TQ], AF.Sigmoid),
                              inc="s", tag=f"{tag}:sig{ec},{tt}")
                    gscol = self.col("one")
                p.wait("vector", "s", sv, tag=f"{tag}:hid{ec},{tt}")
                if tt == 0:
                    p.wait("vector", hw[sl], tag=f"{tag}:hbuf-war{ec}")
                vv = p.op("vector", lambda e, s=sl, n2=tt, P=PG, g_=gscol:
                          e.scalar_tensor_tensor(
                              hbuf[s][:, n2 * TQ:(n2 + 1) * TQ], P[:, 0:TQ],
                              g_, tsil, ALU.mult, ALU.mult),
                          inc="v", tag=f"{tag}:hid{ec},{tt}")
                prev_stt = vv
                mm_marks[(ec, tt)] = ("v", vv)
            mm_marks[ec] = p.chans["t"].n
            p.wait("sync", "v", vv, tag=f"{tag}:sth{ec}")
            p.dma(hid[ec * 128:(ec + 1) * 128, :], hbuf[sl][:, 0:T], hw[sl],
                  tag=f"{tag}:sth{ec}")
        wdn = [self.t(f"Fwd{layer}_{i}", 116 + i * 16, Ec * 128, BF16)
               for i in range(2)]
        fout = self.t("Ffout", 148, Dc * TQ, BF16)
        htile = self.t(f"Fht{layer}", 20, Ec * TQ, BF16)
        p.wait("sync", "tzw0", tag=f"{tag}:cache0")
        p.wait("sync", "tzw1", tag=f"{tag}:cache1")
        p.wait("sync", "hw0", tag=f"{tag}:hid-landed0")
        p.wait("sync", "hw1", tag=f"{tag}:hid-landed1")
        ad = [p.chan("ad0"), p.chan("ad1")]
        ev_marks = {}
        nhb = max(1, Ec // 8)
        for tt in range(NT):
            p.wait("sync", "t", tag=f"{tag}:ht-free{tt}")
            p.wait("sync", "v", tag=f"{tag}:ht-free-v{tt}")
            hmarks = {}
            for chn in ("resw", "ody"):
                if chn in p.chans:
                    p.wait("sync", chn, tag=f"{tag}:res-landed{tt}")
            p.wait("sync", "v", tag=f"{tag}:sta-free{tt}")
            rp = p.chan("rpf")
            src3r = res_src.rearrange("(cc p) t -> p cc t", p=128)[
                :, :, tt * TQ:(tt + 1) * TQ]
            p.dma(sta.rearrange("p (cc t) -> p cc t", cc=Dc), src3r, rp,
                  tag=f"{tag}:ldres{tt}")
            for hb in range(0, Ec, nhb):
                nb = min(nhb, Ec - hb)
                ch = ad[(hb // nhb) % 2]
                p.dma(htile.rearrange("p (ec t) -> p ec t", ec=Ec)[
                          :, hb:hb + nb, :],
                      hid.rearrange("(ec p) t -> p ec t", p=128)[
                          :, hb:hb + nb, tt * TQ:(tt + 1) * TQ],
                      ch, tag=f"{tag}:ldh{tt},{hb}")
                for k in range(hb, hb + nb):
                    hmarks[k] = (ch, ch.n)
            self.bg_region = 164
            p.wait("gpsimd", "t", tag=f"{tag}:bg-gate{tt}")
            for dc in range(Dc):
                sl = dc % 2
                PD = self.ps[4] if sl == 0 else self.ps[5]
                if dc >= 2:
                    chn, v_ = ev_marks[dc - 2]
                    p.wait("sync", chn, v_, tag=f"{tag}:wdfree{tt},{dc}")
                p.dma(wdn[sl].rearrange("p (kc j) -> p kc j", kc=Ec),
                      self.dram[dn_cache].rearrange(
                          "(kc p) o -> p kc o", p=128)[
                          :, :, dc * 128:(dc + 1) * 128],
                      wd[sl], tag=f"{tag}:ldd{tt},{dc}")
                p.wait("tensor", wd[sl], tag=f"{tag}:dmm{tt},{dc}")
                if dc >= 2:
                    chn, v_ = ev_marks[dc - 2]
                    p.wait("tensor", chn, v_, tag=f"{tag}:dbank{tt},{dc}")
                tmk = None
                for ec in range(Ec):
                    if dc == 0:
                        chh, vh2 = hmarks[ec]
                        p.wait("tensor", chh, vh2,
                               tag=f"{tag}:htw{tt},{ec}")
                    tmk = p.op("tensor", lambda e, k=ec, s=sl, P=PD,
                               s0=(ec == 0), st=(ec == Ec - 1):
                               e.matmul(P[:, 0:TQ],
                                        wdn[s][:, k * 128:(k + 1) * 128],
                                        htile[:, k * TQ:(k + 1) * TQ],
                                        start=s0, stop=st),
                               inc="t", tag=f"{tag}:dmm{tt},{dc},{ec}")
                p.wait("scalar", "t", tmk, tag=f"{tag}:dev{tt},{dc}")
                if layer == 0:
                    sv = p.op("scalar", lambda e, d=dc, P=PD:
                              e.activation(fout[:, d * TQ:(d + 1) * TQ],
                                           P[:, 0:TQ], AF.Identity,
                                           scale=self.col(gd_ + "_s")),
                              inc="s", tag=f"{tag}:dev{tt},{dc}")
                else:
                    sv = p.op("scalar", lambda e, d=dc, P=PD:
                              e.activation(fout[:, d * TQ:(d + 1) * TQ],
                                           P[:, 0:TQ], AF.Copy),
                              inc="s", tag=f"{tag}:dev{tt},{dc}")
                ev_marks[dc] = ("s", sv)
                self.bg_step(3, allow_collapse=True)
            self.bg_region = 170
            for chn in ("g0", "g1", "g2", "g3", "tzw0", "tzw1"):
                if chn in p.chans:
                    p.wait("vector", chn, tag=f"{tag}:bg-quiet{tt}")
                    p.wait("scalar", chn, tag=f"{tag}:bg-quiet-s{tt}")
            sev = p.chans["s"].n
            p.wait("vector", "s", sev, tag=f"{tag}:ln4{tt}")
            p.wait("vector", rp, tag=f"{tag}:sta-rdy{tt}")
            xb = self.t("Fxbuf", 180, Dc * TQ, BF16) if layer == 0 else None
            self.ln_post_add(fout, TQ, tt, res_src, res_dst, out_ch,
                             f"{tag}:pa{tt}", sta_ready=sta, make_xhat=xb)
        return out_ch

    def phase_L1A(self):
        p, c = self.p, self.c
        TQ, NT, Dc, T = c["TQ"], c["NT"], c["Dc"], c["T"]
        NKT, KVD = c["NKT"], c["KVD"]
        xhat = self.t("xh1", 20, Dc * T, BF16)
        p.wait("sync", "xhw", tag="1A:xh-landed")
        xld = [p.chan("ad0"), p.chan("ad1")]
        for tt in range(NT):
            p.wait("sync", "t", tag=f"1A:xh-free{tt}")
            p.dma(xhat.rearrange("p (cc t) -> p cc t", cc=Dc)[
                      :, :, tt * TQ:(tt + 1) * TQ],
                  self.dram["xh8"].rearrange("(cc p) t -> p cc t", p=128)[
                      :, :, tt * TQ:(tt + 1) * TQ],
                  xld[tt % 2], tag=f"1A:ldxh{tt}")
        p.wait("tensor", "ad0", tag="1A:xh-rdy0")
        p.wait("tensor", "ad1", tag="1A:xh-rdy1")
        wst = self.t("1Awst", 148, Dc * KVD, F32)
        wkb = self.t("1Awkb", 180, Dc * KVD, BF16)
        wd = p.chan("wd0")
        p.wait("sync", "t", tag="1A:fresh")
        p.wait("sync", "v", tag="1A:fresh-v")
        for chn in ("g0", "g1"):
            if chn in p.chans:
                p.wait("sync", chn, tag="1A:bg-quiet")
        p.dma(wst.rearrange("p (kc o) -> p kc o", kc=Dc),
              self.dram["wk1"].rearrange("(kc p) o -> p kc o", p=128),
              wd, tag="1A:ldk")
        p.wait("vector", wd, tag="1A:castk")
        vk = p.op("vector", lambda e: e.tensor_copy(wkb, wst), inc="v",
                  tag="1A:castk")
        p.wait("tensor", "v", vk, tag="1A:kmm")
        k1 = self.t("k1", 116, c["KVG"] * T, BF16)
        v1 = self.t("v1", 132, NKT * KVD, BF16)
        tmpf = self.t("tmpf", 200, 512, F32)
        for g in range(c["KVG"]):
            for n in range(NT):
                b = self.ps[n % 2]
                tmk = None
                for kc in range(Dc):
                    tmk = p.op("tensor", lambda e, k=kc, nn=n, gg=g, P=b,
                               s0=(kc == 0), st=(kc == Dc - 1):
                               e.matmul(P[:, 0:TQ],
                                        wkb[:, k * KVD + gg * 128:
                                            k * KVD + (gg + 1) * 128],
                                        xhat[:, k * T + nn * TQ:
                                             k * T + (nn + 1) * TQ],
                                        start=s0, stop=st),
                               inc="t", tag=f"1A:kmm{g},{n},{kc}")
                p.wait("vector", "t", tmk, tag=f"1A:kev{g},{n}")
                vv = p.op("vector", lambda e, gg=g, nn=n, P=b:
                          e.tensor_copy(k1[:, gg * T + nn * TQ:
                                           gg * T + (nn + 1) * TQ],
                                        P[:, 0:TQ]),
                          inc="v", tag=f"1A:kev{g},{n}")
                p.wait("tensor", "v", vv, tag=f"1A:kb{g},{n}")
        vlast = p.chans["v"].n
        p.wait("sync", "v", vlast, tag="1A:wst-free2")
        p.dma(wst.rearrange("p (kc o) -> p kc o", kc=Dc),
              self.dram["wv1"].rearrange("(kc p) o -> p kc o", p=128),
              wd, tag="1A:ldv")
        p.wait("vector", wd, tag="1A:castv")
        p.wait("vector", "t", tag="1A:castv-t")
        vk = p.op("vector", lambda e: e.tensor_copy(wkb, wst), inc="v",
                  tag="1A:castv")
        p.wait("tensor", "v", vk, tag="1A:vmm")
        VW = min(TQ, KVD)
        nV = KVD // VW
        for ttk in range(NKT):
            for n in range(nV):
                b = self.ps[n % 2]
                tmk = None
                for kc in range(Dc):
                    tmk = p.op("tensor", lambda e, k=kc, nn=n, t2=ttk, P=b,
                               s0=(kc == 0), st=(kc == Dc - 1):
                               e.matmul(P[:, 0:VW],
                                        xhat[:, k * T + t2 * 128:
                                             k * T + (t2 + 1) * 128],
                                        wkb[:, k * KVD + nn * VW:
                                            k * KVD + (nn + 1) * VW],
                                        start=s0, stop=st),
                               inc="t", tag=f"1A:vmm{ttk},{n},{kc}")
                p.wait("vector", "t", tmk, tag=f"1A:vev{ttk},{n}")
                vv = p.op("vector", lambda e, nn=n, t2=ttk, P=b, w=VW:
                          e.tensor_copy(v1[:, t2 * KVD + nn * w:
                                           t2 * KVD + (nn + 1) * w],
                                        P[:, 0:w]),
                          inc="v", tag=f"1A:vev{ttk},{n}")
                p.wait("tensor", "v", vv, tag=f"1A:vb{ttk},{n}")

    def phase_L1C(self):
        p, c = self.p, self.c
        TQ, NT, T, Dc = c["TQ"], c["NT"], c["T"], c["Dc"]
        NH, NKT, NOFF, KVD = c["NH"], c["NKT"], c["NOFF"], c["KVD"]
        xhat = self.t("xh1", 20, Dc * T, BF16)
        k1 = self.t("k1", 116, c["KVG"] * T, BF16)
        v1 = self.t("v1", 132, NKT * KVD, BF16)
        qh = self.t("1Cqh", 108, T, BF16)
        ctxh = self.t("1Cctxh", 112, T, BF16)
        pT = [self.t(f"1CpT{i}", 148 + i * 2, 2 * TQ, BF16) for i in range(2)]
        rbf = self.t("1Crbf", 152, TQ, F32)
        wstq = [self.t(f"1Cwstq{i}", 84 + i * (Dc * 128 * 4) / KB,
                       Dc * 128, F32) for i in range(2)]
        wqb = [self.t(f"1Cwqb{i}", 100 + i * (Dc * 128 * 2) / KB,
                      Dc * 128, BF16) for i in range(2)]
        statr = self.t_perm("statr")
        masks = self.t_perm("masks")
        wd = [p.chan("wd0"), p.chan("wd1")]
        abw = p.chan("abw")
        for chn in ("resw", "ody", "rpf"):
            if chn in p.chans:
                p.wait("sync", chn, tag="1C:res-quiet")
        mh_col = self.col("mh1")
        self._smark = {}
        self._rbe_prev = None
        self._ctx_marks = {}
        cast_marks = {}
        for h in range(NH):
            g = h // (NH // c["KVG"])
            sl = h % 2
            if h < 2:
                p.wait("sync", "t", tag=f"1C:fresh{h}")
                p.wait("sync", "v", tag=f"1C:fresh-v{h}")
                p.wait("sync", "s", tag=f"1C:fresh-s{h}")
            else:
                p.wait("sync", "s", cast_marks[h - 2], tag=f"1C:wstfree{h}")
            src3 = self.dram["wq1"].rearrange("(kc p) o -> p kc o", p=128)[
                :, :, h * 128:(h + 1) * 128]
            p.dma(wstq[sl].rearrange("p (kc j) -> p kc j", kc=Dc), src3,
                  wd[sl], tag=f"1C:wldq{h}")
            self.bg_step(1, allow_collapse=True)
            p.wait("scalar", wd[sl], tag=f"1C:wcast{h}")
            cast_marks[h] = p.op(
                "scalar", lambda e, s=sl: e.copy(wqb[s], wstq[s]),
                inc="s", tag=f"1C:wcast{h}")
            p.wait("tensor", "s", cast_marks[h], tag=f"1C:q-wait{h}")
            PQm = self.ps[0]
            for n in range(NT):
                tmk = None
                for kc in range(Dc):
                    tmk = p.op("tensor", lambda e, k=kc, nn=n, s=sl,
                               s0=(kc == 0), st=(kc == Dc - 1):
                               e.matmul(PQm[:, 0:TQ],
                                        wqb[s][:, k * 128:(k + 1) * 128],
                                        xhat[:, k * T + nn * TQ:
                                             k * T + (nn + 1) * TQ],
                                        start=s0, stop=st),
                               inc="t", tag=f"1C:qmm{h},{n},{kc}")
                p.wait("vector", "t", tmk, tag=f"1C:qev{h},{n}")
                vv = p.op("vector", lambda e, nn=n:
                          e.tensor_copy(qh[:, nn * TQ:(nn + 1) * TQ],
                                        PQm[:, 0:TQ]),
                          inc="v", tag=f"1C:qev{h},{n}")
                p.wait("tensor", "v", vv, tag=f"1C:qb{h},{n}")
            p.wait("vector", "abw", tag=f"1C:ctx-war{h}")
            self.head_bound(qh[:, 0:T], k1[:, g * T:g * T + 128], mh_col,
                            f"1C:hb{h}")
            SP = [self.ps["A"], self.ps["B"]]
            for qc in range(NT):
                PC = self.ps[4] if qc % 2 == 0 else self.ps[5]
                PS6 = self.ps[6]
                nkt = (qc + 1) * NOFF
                npair = nkt // 2
                exp_marks = {}

                def smm(pi2, qv=qc, nk=nkt, np_=npair):
                    if pi2 >= np_:
                        return
                    if pi2 >= 2:
                        chn, vv_ = exp_marks[pi2 - 2]
                        p.wait("tensor", chn, vv_,
                               tag=f"1C:Sfree{h},{qv},{pi2}")
                    vm = None
                    mk = None
                    for half in range(2):
                        kt = 2 * pi2 + half
                        mk = p.op("tensor", lambda e, k=kt, q=qv,
                                  bb=pi2 % 2, gg=g, hh=half:
                                  e.matmul(SP[bb][:, hh * TQ:(hh + 1) * TQ],
                                           k1[:, gg * T + k * 128:
                                              gg * T + (k + 1) * 128],
                                           qh[:, q * TQ:(q + 1) * TQ],
                                           start=True, stop=True),
                                  inc="t", tag=f"1C:smm{h},{qv},{kt}")
                        diag = kt - qv * NOFF
                        if diag >= 0:
                            p.wait("vector", "t", mk,
                                   tag=f"1C:mask{h},{qv},{kt}")
                            vm = p.op("vector", lambda e, bb=pi2 % 2,
                                      dd=diag, hh=half:
                                      e.tensor_tensor(
                                          SP[bb][:, hh * TQ:(hh + 1) * TQ],
                                          SP[bb][:, hh * TQ:(hh + 1) * TQ],
                                          masks[:, dd * TQ:(dd + 1) * TQ],
                                          ALU.add),
                                      inc="v", tag=f"1C:mask{h},{qv},{kt}")
                    self._smark[(1, h, qv, pi2)] = (mk, vm)

                smm(0)
                for pi in range(npair):
                    b = pi % 2
                    smm(pi + 1)
                    mk, vm = self._smark[(1, h, qc, pi)]
                    if vm is not None:
                        p.wait("scalar", "v", vm, tag=f"1C:expm{h},{qc},{pi}")
                    p.wait("scalar", "t", mk, tag=f"1C:exp{h},{qc},{pi}")
                    sv = p.op("scalar", lambda e, bb=b, pp=pi:
                              e.activation(pT[pp % 2], SP[bb][:, 0:2 * TQ],
                                           AF.Exp, bias=mh_col,
                                           scale=self.col("sm")),
                              inc="s", tag=f"1C:exp{h},{qc},{pi}")
                    exp_marks[pi] = ("s", sv)
                    p.wait("tensor", "s", sv, tag=f"1C:pv{h},{qc},{pi}")
                    if pi == 0 and self._rbe_prev is not None:
                        p.wait("tensor", "s", self._rbe_prev,
                               tag=f"1C:sumfree{h},{qc}")
                    if pi == 0 and self._ctx_marks.get(qc % 2) is not None:
                        p.wait("tensor", "v", self._ctx_marks[qc % 2],
                               tag=f"1C:pcfree{h},{qc}")
                    for half in range(2):
                        kt = 2 * pi + half
                        p.op("tensor", lambda e, k=kt, gg=g, pp=pi, hh=half,
                             P=PC, s0=(kt == 0), st=(kt == nkt - 1):
                             e.matmul(P[:, 0:TQ],
                                      v1[:, k * KVD + gg * 128:
                                         k * KVD + (gg + 1) * 128],
                                      pT[pp % 2][:, hh * TQ:(hh + 1) * TQ],
                                      start=s0, stop=st),
                             inc="t", tag=f"1C:pv{h},{qc},{kt}")
                        tsum = p.op("tensor", lambda e, pp=pi, hh=half,
                                    s0=(kt == 0), st=(kt == nkt - 1):
                                    e.matmul(PS6[0:1, 0:TQ],
                                             self.t_perm("onesb")[:, 0:1],
                                             pT[pp % 2][:, hh * TQ:(hh + 1) * TQ],
                                             start=s0, stop=st),
                                    inc="t", tag=f"1C:sum{h},{qc},{kt}")
                    if pi == 1 and qc >= 1:
                        self.bg_step(10)
                p.wait("vector", "t", tsum, tag=f"1C:recip{h},{qc}")
                srow = statr[0:1, 0:TQ]
                vv = p.op("vector", lambda e: e.reciprocal(
                    srow, PS6[0:1, 0:TQ]), inc="v", tag=f"1C:recip{h},{qc}")
                p.wait("tensor", "v", vv, tag=f"1C:rb{h},{qc}")
                tb = p.op("tensor", lambda e: e.matmul(
                    PS6[:, 0:TQ], self.t_perm("onesf")[0:1, 0:128], srow,
                    start=True, stop=True), inc="t", tag=f"1C:rb{h},{qc}")
                p.wait("scalar", "t", tb, tag=f"1C:rbe{h},{qc}")
                self._rbe_prev = p.op("scalar", lambda e: e.activation(
                    rbf, PS6[:, 0:TQ], AF.Copy), inc="s",
                    tag=f"1C:rbe{h},{qc}")
                p.wait("vector", "s", self._rbe_prev, tag=f"1C:ctx{h},{qc}")
                self._ctx_marks[qc % 2] = p.op(
                    "vector", lambda e, q=qc, P=PC: e.tensor_tensor(
                        ctxh[:, q * TQ:(q + 1) * TQ], P[:, 0:TQ], rbf,
                        ALU.mult),
                    inc="v", tag=f"1C:ctx{h},{qc}")
            p.wait("sync", "v", tag=f"1C:stab{h}")
            p.dma(self.dram["ab"][h * 128:(h + 1) * 128, :], ctxh[:, 0:T],
                  abw, tag=f"1C:stab{h}")

    def barrier(self, name):
        p = self.p
        ph = p.chan("ph")
        for e in ("tensor", "scalar", "sync", "gpsimd"):
            p.op(e, lambda eng: eng.nop(), inc=ph, tag=f"bar:{name}:{e}")
        p.op("vector", lambda eng: eng.engine_nop(), inc=ph,
             tag=f"bar:{name}:vector")
        tgt = ph.n
        for e in ("tensor", "vector", "scalar", "sync", "gpsimd"):
            p.wait(e, ph, tgt, tag=f"bar:{name}")

    def build(self, upto=None):
        self.declare()
        self.bg = []
        steps = [
            ("prelude", lambda: (self.prelude(), self.bg_enqueue())),
            ("A", self.phase_A),
            ("B", self.phase_B),
            ("C", self.phase_C),
            ("D0", lambda: self.phase_D("wo0", self.b_o0(), self.dram["x"],
                                        self.dram["res"], "resw", "D0")),
            ("T0fin", lambda: self.bg_finish(7)),
            ("F0", lambda: self.ffn(0, self.dram["res"], self.dram["res"],
                                    "resw", "F0")),
            ("L1A", self.phase_L1A),
            ("L1C", self.phase_L1C),
            ("D1", lambda: self.phase_D("wo1", self.b_o1(), self.dram["res"],
                                        self.dram["res"], "resw", "D1")),
            ("T1fin", lambda: self.bg_finish(0)),
            ("F1", lambda: self.ffn(1, self.dram["res"], self.dram["y"],
                                    "ody", "F1")),
        ]
        for name, fn in steps:
            fn()
            self.barrier(name)
            if upto is not None and name == upto:
                break
        for chn in ("resw", "kvw", "abw", "tzw0", "tzw1", "ody"):
            if chn in self.p.chans:
                self.p.wait("sync", chn, tag=f"final:{chn}")
        self.p.emit()
        return self.nc


def build_nc(cfg, upto=None, debug_outputs=()):
    mk = MK(cfg, debug_outputs)
    nc = mk.build(upto)
    mk.stack.close()
    return nc, mk



def kernel(**inp):
    cfg = FULL
    c = derive(cfg)
    nc, _ = build_nc(cfg)
    x = np.asarray(inp["L0_attn_x"], np.float32)
    common = host_inputs(c, inp)
    in_maps = []
    for b in range(8):
        m = dict(common)
        m["x"] = np.ascontiguousarray(x[b].T)
        in_maps.append(m)
    res = run_bass_kernel_spmd(nc, in_maps, core_ids=list(range(8)))
    out = np.stack([np.asarray(r["y"], np.float32).T for r in res.results], 0)
    return np.ascontiguousarray(out)


# version 13
# speedup vs baseline: 1.0353x; 1.0353x over previous
import numpy as np
import ml_dtypes
from contextlib import ExitStack

import concourse.bass as bass
import concourse.mybir as mybir
from concourse.bass_utils import run_bass_kernel_spmd

F32 = mybir.dt.float32
BF16 = mybir.dt.bfloat16
I8 = mybir.dt.int8
U8 = mybir.dt.uint8
AF = mybir.ActivationFunctionType
ALU = mybir.AluOpType
AX = mybir.AxisListType
BF = ml_dtypes.bfloat16

KB = 1024

FULL = dict(D=2048, T=2048, NH=16, HD=128, DQ=1536, DKV=512,
            E0=8192, E1=5632, KVG=4, TQ=512)
MINI = dict(D=512, T=512, NH=4, HD=128, DQ=384, DKV=128,
            E0=1024, E1=768, KVG=2, TQ=512)


def derive(cfg):
    c = dict(cfg)
    c["Dc"] = c["D"] // 128
    c["DQc"] = c["DQ"] // 128
    c["DKVc"] = c["DKV"] // 128
    c["E0c"] = c["E0"] // 128
    c["E1c"] = c["E1"] // 128
    c["NT"] = c["T"] // c["TQ"]
    c["NKT"] = c["T"] // 128
    c["KVD"] = c["KVG"] * c["HD"]
    c["KVDc"] = c["KVD"] // 128
    c["T2"] = c["T"] // 2
    c["NOFF"] = c["TQ"] // c["HD"]
    return c



def rope_tables(D, T):
    exps = np.arange(0, D, 2, dtype=np.float32)
    with np.errstate(over="ignore"):
        tn = np.float32(1.0) / (np.float32(10000.0) ** exps / np.float32(D))
    m = np.arange(T, dtype=np.float32)
    ang = np.outer(m, tn).astype(np.float32)
    cos = np.cos(ang).astype(np.float32)
    sin = np.sin(ang).astype(np.float32)
    d = np.arange(D)
    return (np.ascontiguousarray(cos[:, d // 2].T),
            np.ascontiguousarray(sin[:, d // 2].T))


def diag_masks(NOFF, TQ):
    k = np.arange(128)[:, None]
    q = np.arange(TQ)[None, :]
    out = np.zeros((128, NOFF * TQ), np.float32)
    for o in range(NOFF):
        out[:, o * TQ:(o + 1) * TQ] = np.where(o * 128 + k > q, -1.0e30, 0.0)
    return out.astype(BF)


def rope_perm():
    P = np.zeros((128, 128), np.float32)
    i = np.arange(64)
    P[2 * i, 2 * i + 1] = -1.0
    P[2 * i + 1, 2 * i] = 1.0
    return np.ascontiguousarray(P.T).astype(BF)


def pack_bias(b):
    b = np.asarray(b, np.float32)
    return np.ascontiguousarray(b.reshape(-1, 128).T)


def host_inputs(c, inp):
    t = lambda a: np.ascontiguousarray(np.asarray(a, np.float32).T)
    m = {}
    m["wdq"] = t(inp["L0_attn_w_down_q"])
    m["wuq"] = t(inp["L0_attn_w_up_q"])
    m["wqr"] = t(inp["L0_attn_wq_r"])
    m["wdkv"] = t(inp["L0_attn_w_down_kv"])
    m["wuk"] = t(inp["L0_attn_w_up_k"])
    m["wuv"] = t(inp["L0_attn_w_up_v"])
    m["wkr"] = t(inp["L0_attn_wk_r"])
    m["wo0"] = t(inp["L0_attn_wo"])
    m["bias"] = np.concatenate([
        pack_bias(inp["L0_attn_b_down_q"]),
        pack_bias(inp["L0_attn_b_up_q"]),
        pack_bias(inp["L0_attn_bq_r"]),
        pack_bias(inp["L0_attn_b_down_kv"]),
        pack_bias(inp["L0_attn_b_up_k"]),
        pack_bias(inp["L0_attn_bk_r"]),
        pack_bias(inp["L0_attn_bo"]),
        pack_bias(inp["L1_attn_bo"]),
    ], axis=1)
    m["bvrow"] = np.ascontiguousarray(
        np.asarray(inp["L0_attn_b_up_v"], np.float32)[None, :])
    m["wup"] = t(inp["L0_ffn_W_up"])
    m["wgt"] = t(inp["L0_ffn_W_gate"])
    m["wdn"] = t(inp["L0_ffn_W_down"])
    m["wq1"] = np.ascontiguousarray(np.asarray(inp["L1_attn_Wq"], np.float32))
    m["wk1"] = np.ascontiguousarray(np.asarray(inp["L1_attn_Wk"], np.float32))
    m["wv1"] = np.ascontiguousarray(np.asarray(inp["L1_attn_Wv"], np.float32))
    m["wo1"] = np.ascontiguousarray(np.asarray(inp["L1_attn_Wo"], np.float32))
    m["ww1"] = t(inp["L1_ffn_w1"])
    m["ww2"] = t(inp["L1_ffn_w2"])
    m["ww3"] = t(inp["L1_ffn_w3"])
    cexp, sexp = rope_tables(c["D"], c["T"])
    m["cexp"] = cexp.astype(BF)
    m["sexp"] = sexp.astype(BF)
    m["masks"] = diag_masks(c["NOFF"], c["TQ"])
    m["identb"] = np.eye(128, dtype=np.float32).astype(BF)
    m["identf"] = np.eye(128, dtype=np.float32)
    m["ptrope"] = rope_perm()
    return m



ENGINES = ("sync", "tensor", "vector", "scalar", "gpsimd")


class Chan:
    __slots__ = ("name", "sem", "n")

    def __init__(self, name, sem):
        self.name = name
        self.sem = sem
        self.n = 0


class Prog:
    def __init__(self, nc, stack):
        self.nc = nc
        self.stack = stack
        self.ops = {e: [] for e in ENGINES}
        self.chans = {}

    def chan(self, name):
        ch = self.chans.get(name)
        if ch is None:
            sem = self.stack.enter_context(self.nc.semaphore(name))
            ch = Chan(name, sem)
            self.chans[name] = ch
        return ch

    def wait(self, engine, ch, val=None, tag=""):
        if isinstance(ch, str):
            ch = self.chan(ch)
        v = ch.n if val is None else val
        if v <= 0:
            return 0
        self.ops[engine].append(("wait", ch, v, tag))
        return v

    def op(self, engine, fn, inc=None, inc_n=1, tag=""):
        if isinstance(inc, str):
            inc = self.chan(inc)
        self.ops[engine].append(("op", fn, inc, inc_n, tag))
        if inc is not None:
            inc.n += inc_n
        return inc.n if inc is not None else None

    def dma(self, out, in_, ch, q="sync", tag=""):
        if isinstance(ch, str):
            ch = self.chan(ch)
        if ch.n >= 16:
            self.wait(q, ch, ch.n, tag=tag + ":pace")
        return self.op(q, lambda eng, o=out, i=in_: eng.dma_start(o, i),
                       inc=ch, inc_n=16, tag=tag)

    def check(self):
        vals = {name: 0 for name in self.chans}
        ptr = {e: 0 for e in ENGINES}
        total = sum(len(v) for v in self.ops.values())
        done = 0
        progress = True
        while progress:
            progress = False
            for e in ENGINES:
                while ptr[e] < len(self.ops[e]):
                    item = self.ops[e][ptr[e]]
                    if item[0] == "wait":
                        _, ch, v, tag = item
                        if vals[ch.name] < v:
                            break
                    else:
                        _, fn, inc, inc_n, tag = item
                        if inc is not None:
                            vals[inc.name] += inc_n
                    ptr[e] += 1
                    done += 1
                    progress = True
        if done != total:
            lines = []
            for e in ENGINES:
                if ptr[e] < len(self.ops[e]):
                    item = self.ops[e][ptr[e]]
                    if item[0] == "wait":
                        _, ch, v, tag = item
                        lines.append(f"{e}[{ptr[e]}] waits {ch.name}>={v} "
                                     f"(have {vals[ch.name]}) tag={tag}")
            raise RuntimeError("deadlock in program build:\n" + "\n".join(lines))

    def emit(self):
        self.check()
        nc = self.nc
        own = {"vector": "v", "scalar": "s", "gpsimd": "g"}
        with nc.Block() as blk:
            def runner(name):
                def _run(eng):
                    och = self.chans.get(own.get(name, ""), None)
                    cnt = 0
                    for item in self.ops[name]:
                        if item[0] == "wait":
                            _, ch, v, tag = item
                            eng.wait_ge(ch.sem, v)
                        else:
                            _, fn, inc, inc_n, tag = item
                            if och is not None and cnt > 0:
                                eng.wait_ge(och.sem, cnt)
                            ins = fn(eng)
                            if inc is not None:
                                ins.then_inc(inc.sem, inc_n)
                            if inc is och:
                                cnt += inc_n
                    return
                return _run
            for name, deco in (("sync", blk.sync), ("tensor", blk.tensor),
                               ("vector", blk.vector), ("scalar", blk.scalar),
                               ("gpsimd", blk.gpsimd)):
                if self.ops[name]:
                    deco(runner(name))


class Arena:
    def __init__(self, nc, stack, size_bytes):
        self.nc = nc
        self.size = size_bytes
        self.back = stack.enter_context(
            nc.sbuf_tensor("arena", [128, size_bytes], U8))
        self.base = nc.lookup_mloc(self.back).addr
        self.names = {}

    def at(self, name, off_bytes, shape, dtype):
        key = (name, off_bytes)
        if key in self.names:
            return self.names[key]
        per_part = int(np.prod(shape[1:])) * mybir.dt.size(dtype)
        assert off_bytes + per_part <= self.size, \
            f"{name}: {off_bytes}+{per_part} > {self.size}"
        name = "sb_" + name
        hnd = self.nc.virtual_sbuf_tensor(name, shape, dtype)
        self.nc._tensor(hnd.name, shape, dtype, type="SB")
        mloc = self.nc.lookup_mloc(hnd)
        mloc.allocated = True
        mloc.addr = self.base + off_bytes
        self.names[key] = hnd
        return hnd



class MK:
    def __init__(self, cfg, debug_outputs=()):
        self.c = derive(cfg)
        self.stack = ExitStack()
        self.nc = bass.Bass("TRN2", target_bir_lowering=False)
        self.p = Prog(self.nc, self.stack)
        self.debug_outputs = set(debug_outputs)
        self.dram = {}
        self.sb = None
        self.ps = {}
        self.cols = {}
        self.ncols = 0

    def dram_in(self, name, shape, dtype=F32):
        h = self.nc.dram_tensor(name, list(shape), dtype, kind="ExternalInput")
        self.dram[name] = h.ap()

    def dram_tmp(self, name, shape, dtype=F32):
        kind = "ExternalOutput" if name in self.debug_outputs else "Internal"
        h = self.nc.dram_tensor(name, list(shape), dtype, kind=kind)
        self.dram[name] = h.ap()

    def dram_out(self, name, shape, dtype=F32):
        h = self.nc.dram_tensor(name, list(shape), dtype, kind="ExternalOutput")
        self.dram[name] = h.ap()

    def declare(self):
        c = self.c
        T = c["T"]
        self.dram_in("x", [c["D"], T])
        for n, s in [("wdq", [c["D"], c["DQ"]]), ("wuq", [c["DQ"], c["D"]]),
                     ("wqr", [c["DQ"], c["D"]]), ("wdkv", [c["D"], c["DKV"]]),
                     ("wuk", [c["DKV"], c["D"]]), ("wuv", [c["DKV"], c["D"]]),
                     ("wkr", [c["DKV"], c["D"]]), ("wo0", [c["D"], c["D"]]),
                     ("wup", [c["D"], c["E0"]]), ("wgt", [c["D"], c["E0"]]),
                     ("wdn", [c["E0"], c["D"]]),
                     ("wq1", [c["D"], c["D"]]), ("wk1", [c["D"], c["KVD"]]),
                     ("wv1", [c["D"], c["KVD"]]), ("wo1", [c["D"], c["D"]]),
                     ("ww1", [c["D"], c["E1"]]), ("ww2", [c["D"], c["E1"]]),
                     ("ww3", [c["E1"], c["D"]]),
                     ("identf", [128, 128]),
                     ("bvrow", [1, c["D"]])]:
            self.dram_in(n, s)
        nbias = c["DQc"] + 6 * c["Dc"] + c["DKVc"]
        self.dram_in("bias", [128, nbias])
        self.dram_in("masks", [128, c["NOFF"] * c["TQ"]], BF16)
        self.dram_in("cexp", [c["D"], T], BF16)
        self.dram_in("sexp", [c["D"], T], BF16)
        for n in ("identb", "ptrope"):
            self.dram_in(n, [128, 128], BF16)
        self.dram_tmp("res", [c["D"], T])
        self.dram_tmp("k0", [c["D"], T], BF16)
        self.dram_tmp("v0", [T, c["D"]], BF16)
        self.dram_tmp("ab", [c["D"], T], BF16)
        self.dram_tmp("hd0", [c["E0"], T], BF16)
        self.dram_tmp("xh8", [c["D"], T], BF16)
        self.dram_tmp("tdn", [c["E0"], c["D"]], BF16)
        self.dram_tmp("tw3", [c["E1"], c["D"]], BF16)
        self.dram_out("y", [c["D"], T])

        self.sb = Arena(self.nc, self.stack, 204 * KB)
        pa = self.stack.enter_context(self.nc.psum_tensor("PA", [128, 1024], F32))
        pb = self.stack.enter_context(self.nc.psum_tensor("PB", [128, 1024], F32))
        self.ps["A"] = pa[:]
        self.ps["B"] = pb[:]
        self.ps[0] = pa[:][:, 0:512]
        self.ps[1] = pa[:][:, 512:1024]
        self.ps[2] = pb[:][:, 0:512]
        self.ps[3] = pb[:][:, 512:1024]
        for i in range(4, 7):
            self.ps[i] = self.stack.enter_context(
                self.nc.psum_tensor(f"P{i}", [128, 512], F32))[:]
        self.ps["bf"] = self.stack.enter_context(
            self.nc.psum_tensor("PBF", [128, 1024], BF16))[:]

    def t_perm(self, name):
        c = self.c
        lay = {
            "identb": (0, [128, 128], BF16),
            "ptrope": (256, [128, 128], BF16),
            "identf": (512, [128, 128], F32),
            "onesb":  (1024, [128, 128], BF16),
            "onesf":  (1280, [128, 128], F32),
            "cols":   (1792, [128, 64], F32),
            "biast":  (2048, [128, 128], F32),
            "statr":  (2560, [128, 1024], F32),
            "masks":  (6656, [128, c["NOFF"] * c["TQ"]], BF16),
            "bvb":    (10752, [128, c["D"]], F32),
            "kcast":  (18944, [128, 128], BF16),
        }
        off, shape, dt = lay[name]
        return self.sb.at(name, off, shape, dt)[:]

    def col(self, name):
        if name not in self.cols:
            self.cols[name] = self.ncols
            self.ncols += 1
            assert self.ncols <= 64
        i = self.cols[name]
        return self.t_perm("cols")[:, i:i + 1]

    def t(self, name, off_kb, free_elems, dtype):
        return self.sb.at(name, int(off_kb * KB), [128, free_elems], dtype)[:]

    def memset_col(self, name, val):
        cl = self.col(name)
        self.p.op("vector", lambda e, a=cl, v=float(val): e.memset(a, v),
                  inc="v", tag=f"memset:{name}")

    def bias_col(self, idx):
        return self.t_perm("biast")[:, idx:idx + 1]

    def b_dq(self):  return 0
    def b_uq(self):  return self.c["DQc"]
    def b_qr(self):  return self.c["DQc"] + self.c["Dc"]
    def b_dkv(self): return self.c["DQc"] + 2 * self.c["Dc"]
    def b_uk(self):  return self.c["DQc"] + 2 * self.c["Dc"] + self.c["DKVc"]
    def b_kr(self):  return self.c["DQc"] + 3 * self.c["Dc"] + self.c["DKVc"]
    def b_o0(self):  return self.c["DQc"] + 4 * self.c["Dc"] + self.c["DKVc"]
    def b_o1(self):  return self.c["DQc"] + 5 * self.c["Dc"] + self.c["DKVc"]

    def prelude(self):
        p, c = self.p, self.c
        ad = p.chan("ad0")
        for n in ("identb", "ptrope", "identf", "masks"):
            p.dma(self.t_perm(n), self.dram[n], ad, tag=f"ld:{n}")
        nbias = c["DQc"] + 6 * c["Dc"] + c["DKVc"]
        p.dma(self.t_perm("biast")[:, 0:nbias], self.dram["bias"], ad,
              tag="ld:bias")
        p.op("vector", lambda e: e.memset(self.t_perm("onesb"), 1.0), inc="v")
        p.op("vector", lambda e: e.memset(self.t_perm("onesf"), 1.0), inc="v")
        for nm, v in [("one", 1.0), ("neg1", -1.0), ("invD", 1.0 / c["D"]),
                      ("eps", 1e-5), ("neghalf", -0.5),
                      ("sm", 1.0 / np.sqrt(c["HD"])),
                      ("half", 0.5),
                      ("negsm", -1.0 / np.sqrt(c["HD"])),
                      ("invE0", 1.0 / (c["D"] * c["E0"])),
                      ("invE1", 1.0 / (c["D"] * c["E1"])),
                      ("eps8", 1e-8)]:
            self.memset_col(nm, v)
        for e in ("tensor", "vector", "scalar"):
            p.wait(e, ad, tag="prelude:consts")
            p.wait(e, "v", tag="prelude:memsets")

    def ln_in(self, src, tt, xhat, xhat_stride, sta, tag):
        p, c = self.p, self.c
        Dc, TQ = c["Dc"], c["TQ"]
        ad = p.chan("ad0")
        for chn in ("resw", "ody"):
            if chn in p.chans:
                p.wait("sync", chn, tag=f"{tag}:res-landed")
        p.wait("sync", "v", tag=f"{tag}:sta-free")
        src3 = src.rearrange("(cc p) t -> p cc t", p=128)[
            :, :, tt * TQ:(tt + 1) * TQ]
        dst3 = sta.rearrange("p (cc t) -> p cc t", cc=Dc)
        p.dma(dst3, src3, ad, tag=f"{tag}:ld")
        p.wait("vector", ad, tag=f"{tag}:v-sta")
        p.wait("scalar", ad, tag=f"{tag}:s-sta")
        rawc = [self.t("rawc0", 196, 512, BF16), self.t("rawc1", 197, 512, BF16)]
        sqc = [self.t("sqc0", 198, 512, BF16), self.t("sqc1", 199, 512, BF16)]
        statr = self.t_perm("statr")
        mrow = statr[0:1, 0:TQ]
        vrow = statr[32:33, 0:TQ]
        rrow = statr[0:1, 512:512 + TQ]
        xrow = statr[32:33, 512:512 + TQ]
        PM, PQ, PMB, PRB = self.ps[0], self.ps[1], self.ps[2], self.ps[3]
        p.wait("tensor", "s", tag=f"{tag}:banks-quiet")
        mm_marks = {}
        for cc in range(Dc):
            b2 = cc % 2
            sl = sta[:, cc * TQ:(cc + 1) * TQ]
            if cc >= 2:
                p.wait("vector", "t", mm_marks[cc - 2], tag=f"{tag}:rawfree{cc}")
                p.wait("scalar", "t", mm_marks[cc - 2], tag=f"{tag}:sqfree{cc}")
            vv = p.op("vector", lambda e, s=sl, d=rawc[b2]: e.tensor_copy(d, s),
                      inc="v", tag=f"{tag}:cast{cc}")
            ss = p.op("scalar", lambda e, s=sl, d=sqc[b2]:
                      e.activation(d, s, AF.Square), inc="s",
                      tag=f"{tag}:sq{cc}")
            p.wait("tensor", "v", vv, tag=f"{tag}:mm-m{cc}")
            p.op("tensor", lambda e, r=rawc[b2], s=(cc == 0), st=(cc == Dc - 1):
                 e.matmul(PM[0:1, 0:TQ], self.t_perm("onesb")[:, 0:1], r,
                          start=s, stop=st),
                 inc="t", tag=f"{tag}:mmm{cc}")
            p.wait("tensor", "s", ss, tag=f"{tag}:mm-q{cc}")
            mm_marks[cc] = p.op(
                "tensor", lambda e, r=sqc[b2], s=(cc == 0), st=(cc == Dc - 1):
                e.matmul(PQ[0:1, 0:TQ], self.t_perm("onesb")[:, 0:1], r,
                         start=s, stop=st),
                inc="t", tag=f"{tag}:mmq{cc}")
        tmm = p.chans["t"].n
        p.wait("vector", "t", tmm, tag=f"{tag}:stats")
        p.op("vector", lambda e: e.tensor_scalar(
            mrow, PM[0:1, 0:TQ], self.col("invD")[0:1, :], None, ALU.mult),
            inc="v", tag=f"{tag}:m")
        p.op("vector", lambda e: e.tensor_scalar(
            vrow, PQ[0:1, 0:TQ], self.col("invD")[0:1, :],
            self.col("eps")[0:1, :], ALU.mult, ALU.add),
            inc="v", tag=f"{tag}:q")
        p.op("vector", lambda e: e.tensor_tensor(xrow, mrow, mrow, ALU.mult),
             inc="v", tag=f"{tag}:m2")
        vfin = p.op("vector", lambda e: e.tensor_tensor(
            vrow, vrow, xrow, ALU.subtract), inc="v", tag=f"{tag}:var")
        p.wait("scalar", "v", vfin, tag=f"{tag}:rstd")
        p.op("scalar", lambda e: e.activation(xrow, vrow, AF.Ln), inc="s",
             tag=f"{tag}:ln")
        sfin = p.op("scalar", lambda e: e.activation(
            rrow, xrow, AF.Exp, scale=self.col("neghalf")[0:1, :]),
            inc="s", tag=f"{tag}:exp")
        p.wait("tensor", "v", vfin, tag=f"{tag}:bc-m")
        p.wait("tensor", "s", sfin, tag=f"{tag}:bc-r")
        p.op("tensor", lambda e: e.matmul(
            PMB[:, 0:TQ], self.t_perm("onesf")[0:1, 0:128], mrow,
            start=True, stop=True), inc="t", tag=f"{tag}:bcm")
        tbc = p.op("tensor", lambda e: e.matmul(
            PRB[:, 0:TQ], self.t_perm("onesf")[0:1, 0:128], rrow,
            start=True, stop=True), inc="t", tag=f"{tag}:bcr")
        p.wait("vector", "t", tbc, tag=f"{tag}:norm")
        tmpf = self.t("tmpf", 200, 512, F32)
        for cc in range(Dc):
            sl = sta[:, cc * TQ:(cc + 1) * TQ]
            if xhat_stride == TQ:
                dst = xhat[:, cc * TQ:(cc + 1) * TQ]
            else:
                dst = xhat[:, cc * xhat_stride + tt * TQ:
                           cc * xhat_stride + (tt + 1) * TQ]
            p.op("vector", lambda e, s=sl:
                 e.tensor_tensor(tmpf, s, PMB[:, 0:TQ], ALU.subtract),
                 inc="v", tag=f"{tag}:sub{cc}")
            p.op("vector", lambda e, d=dst:
                 e.tensor_tensor(d, tmpf, PRB[:, 0:TQ], ALU.mult),
                 inc="v", tag=f"{tag}:mul{cc}")
        vn = p.chans["v"].n
        p.wait("tensor", "v", vn, tag=f"{tag}:bank-free")
        return vn

    def ln_post_add(self, src_slab, chunk_stride, tt, res_src, res_dst,
                    out_ch, tag, sta_ready=None, make_xhat=None):
        p, c = self.p, self.c
        Dc, TQ = c["Dc"], c["TQ"]
        sta = sta_ready if sta_ready is not None else \
            self.t("staD", 84, Dc * TQ, F32)
        if sta_ready is None:
            ad = p.chan("ad1")
            for chn in ("resw", "ody"):
                if chn in p.chans:
                    p.wait("sync", chn, tag=f"{tag}:res-landed")
            p.wait("sync", "v", tag=f"{tag}:sta-free")
            src3 = res_src.rearrange("(cc p) t -> p cc t", p=128)[
                :, :, tt * TQ:(tt + 1) * TQ]
            p.dma(sta.rearrange("p (cc t) -> p cc t", cc=Dc), src3, ad,
                  tag=f"{tag}:ldres")
        toff = (tt * TQ) % chunk_stride if chunk_stride > TQ else 0

        def srcsl(cc):
            if chunk_stride > TQ:
                return src_slab[:, cc * chunk_stride + toff:
                                cc * chunk_stride + toff + TQ]
            return src_slab[:, cc * TQ:(cc + 1) * TQ]

        statr = self.t_perm("statr")
        mrow = statr[0:1, 0:TQ]
        vrow = statr[32:33, 0:TQ]
        rrow = statr[0:1, 512:512 + TQ]
        xrow = statr[32:33, 512:512 + TQ]
        PM, PQ, PMB, PRB = self.ps[0], self.ps[1], self.ps[2], self.ps[3]
        p.wait("tensor", "s", tag=f"{tag}:banks-quiet")
        sqc = [self.t("sqc0", 198, 512, BF16), self.t("sqc1", 199, 512, BF16)]
        mm_marks = {}
        for cc in range(Dc):
            b2 = cc % 2
            if cc >= 2:
                p.wait("vector", "t", mm_marks[cc - 2], tag=f"{tag}:sqfree{cc}")
            vv = p.op("vector", lambda e, s=srcsl(cc), d=sqc[b2]:
                      e.tensor_tensor(d, s, s, ALU.mult), inc="v",
                      tag=f"{tag}:sq{cc}")
            p.op("tensor", lambda e, r=srcsl(cc), s=(cc == 0),
                 st=(cc == Dc - 1):
                 e.matmul(PM[0:1, 0:TQ], self.t_perm("onesb")[:, 0:1], r,
                          start=s, stop=st),
                 inc="t", tag=f"{tag}:mmm{cc}")
            p.wait("tensor", "v", vv, tag=f"{tag}:wsq{cc}")
            mm_marks[cc] = p.op(
                "tensor", lambda e, r=sqc[b2], s=(cc == 0), st=(cc == Dc - 1):
                e.matmul(PQ[0:1, 0:TQ], self.t_perm("onesb")[:, 0:1], r,
                         start=s, stop=st),
                inc="t", tag=f"{tag}:mmq{cc}")
        tmm = p.chans["t"].n
        p.wait("vector", "t", tmm, tag=f"{tag}:stats")
        p.op("vector", lambda e: e.tensor_scalar(
            mrow, PM[0:1, 0:TQ], self.col("invD")[0:1, :], None, ALU.mult),
            inc="v", tag=f"{tag}:m")
        p.op("vector", lambda e: e.tensor_scalar(
            vrow, PQ[0:1, 0:TQ], self.col("invD")[0:1, :],
            self.col("eps")[0:1, :], ALU.mult, ALU.add), inc="v",
            tag=f"{tag}:q")
        p.op("vector", lambda e: e.tensor_tensor(xrow, mrow, mrow, ALU.mult),
             inc="v", tag=f"{tag}:m2")
        vfin = p.op("vector", lambda e: e.tensor_tensor(
            vrow, vrow, xrow, ALU.subtract), inc="v", tag=f"{tag}:var")
        p.wait("scalar", "v", vfin, tag=f"{tag}:rstd")
        p.op("scalar", lambda e: e.activation(xrow, vrow, AF.Ln), inc="s",
             tag=f"{tag}:ln")
        sfin = p.op("scalar", lambda e: e.activation(
            rrow, xrow, AF.Exp, scale=self.col("neghalf")[0:1, :]), inc="s",
            tag=f"{tag}:exp")
        p.wait("tensor", "v", vfin, tag=f"{tag}:bc")
        p.wait("tensor", "s", sfin, tag=f"{tag}:bc2")
        p.op("tensor", lambda e: e.matmul(
            PMB[:, 0:TQ], self.t_perm("onesf")[0:1, 0:128], mrow,
            start=True, stop=True), inc="t", tag=f"{tag}:bcm")
        tbc = p.op("tensor", lambda e: e.matmul(
            PRB[:, 0:TQ], self.t_perm("onesf")[0:1, 0:128], rrow,
            start=True, stop=True), inc="t", tag=f"{tag}:bcr")
        p.wait("vector", "t", tbc, tag=f"{tag}:norm")
        if sta_ready is None:
            p.wait("vector", "ad1", tag=f"{tag}:res-tile")
        tmpf = self.t("tmpf", 200, 512, F32)
        tmpf2 = self.t("tmpf2", 202, 512, F32)
        for cc in range(Dc):
            stasl = sta[:, cc * TQ:(cc + 1) * TQ]
            p.op("vector", lambda e, s=srcsl(cc):
                 e.tensor_tensor(tmpf, s, PMB[:, 0:TQ], ALU.subtract),
                 inc="v", tag=f"{tag}:sub{cc}")
            p.op("vector", lambda e:
                 e.tensor_tensor(tmpf2, tmpf, PRB[:, 0:TQ], ALU.mult),
                 inc="v", tag=f"{tag}:mul{cc}")
            p.op("vector", lambda e, d=stasl:
                 e.tensor_tensor(d, d, tmpf2, ALU.add),
                 inc="v", tag=f"{tag}:add{cc}")
        vn = p.chans["v"].n
        p.wait("tensor", "v", vn, tag=f"{tag}:bank-free")
        out3 = res_dst.rearrange("(cc p) t -> p cc t", p=128)[
            :, :, tt * TQ:(tt + 1) * TQ]
        p.wait("sync", "v", vn, tag=f"{tag}:st-wait")
        p.dma(out3, sta.rearrange("p (cc t) -> p cc t", cc=Dc), out_ch,
              tag=f"{tag}:st")
        if make_xhat is not None:
            self.ln_make_xhat(sta, make_xhat, tt, f"{tag}:mx")


    def ln_make_xhat(self, sta, xbuf, tt, tag):
        p, c = self.p, self.c
        Dc, TQ = c["Dc"], c["TQ"]
        statr = self.t_perm("statr")
        mrow = statr[0:1, 0:TQ]
        vrow = statr[32:33, 0:TQ]
        rrow = statr[0:1, 512:512 + TQ]
        xrow = statr[32:33, 512:512 + TQ]
        PM, PQ, PMB, PRB = self.ps[0], self.ps[1], self.ps[2], self.ps[3]
        p.wait("tensor", "s", tag=f"{tag}:banks-quiet")
        p.wait("scalar", "v", tag=f"{tag}:sta-ready")
        rawc = [self.t("rawc0", 196, 512, BF16), self.t("rawc1", 197, 512, BF16)]
        sqc = [self.t("sqc0", 198, 512, BF16), self.t("sqc1", 199, 512, BF16)]
        mm_marks = {}
        for cc in range(Dc):
            b2 = cc % 2
            sl = sta[:, cc * TQ:(cc + 1) * TQ]
            if cc >= 2:
                p.wait("vector", "t", mm_marks[cc - 2], tag=f"{tag}:rawfree{cc}")
                p.wait("scalar", "t", mm_marks[cc - 2], tag=f"{tag}:sqfree{cc}")
            vv = p.op("vector", lambda e, s=sl, d=rawc[b2]: e.tensor_copy(d, s),
                      inc="v", tag=f"{tag}:cast{cc}")
            ss = p.op("scalar", lambda e, s=sl, d=sqc[b2]:
                      e.activation(d, s, AF.Square), inc="s",
                      tag=f"{tag}:sq{cc}")
            p.wait("tensor", "v", vv, tag=f"{tag}:mm-m{cc}")
            p.op("tensor", lambda e, r=rawc[b2], s=(cc == 0), st=(cc == Dc - 1):
                 e.matmul(PM[0:1, 0:TQ], self.t_perm("onesb")[:, 0:1], r,
                          start=s, stop=st),
                 inc="t", tag=f"{tag}:mmm{cc}")
            p.wait("tensor", "s", ss, tag=f"{tag}:mm-q{cc}")
            mm_marks[cc] = p.op(
                "tensor", lambda e, r=sqc[b2], s=(cc == 0), st=(cc == Dc - 1):
                e.matmul(PQ[0:1, 0:TQ], self.t_perm("onesb")[:, 0:1], r,
                         start=s, stop=st),
                inc="t", tag=f"{tag}:mmq{cc}")
        tmm = p.chans["t"].n
        p.wait("vector", "t", tmm, tag=f"{tag}:stats")
        p.op("vector", lambda e: e.tensor_scalar(
            mrow, PM[0:1, 0:TQ], self.col("invD")[0:1, :], None, ALU.mult),
            inc="v", tag=f"{tag}:m")
        p.op("vector", lambda e: e.tensor_scalar(
            vrow, PQ[0:1, 0:TQ], self.col("invD")[0:1, :],
            self.col("eps")[0:1, :], ALU.mult, ALU.add),
            inc="v", tag=f"{tag}:q")
        p.op("vector", lambda e: e.tensor_tensor(xrow, mrow, mrow, ALU.mult),
             inc="v", tag=f"{tag}:m2")
        vfin = p.op("vector", lambda e: e.tensor_tensor(
            vrow, vrow, xrow, ALU.subtract), inc="v", tag=f"{tag}:var")
        p.wait("scalar", "v", vfin, tag=f"{tag}:rstd")
        p.op("scalar", lambda e: e.activation(xrow, vrow, AF.Ln), inc="s",
             tag=f"{tag}:ln")
        sfin = p.op("scalar", lambda e: e.activation(
            rrow, xrow, AF.Exp, scale=self.col("neghalf")[0:1, :]),
            inc="s", tag=f"{tag}:exp")
        p.wait("tensor", "v", vfin, tag=f"{tag}:bc-m")
        p.wait("tensor", "s", sfin, tag=f"{tag}:bc-r")
        p.op("tensor", lambda e: e.matmul(
            PMB[:, 0:TQ], self.t_perm("onesf")[0:1, 0:128], mrow,
            start=True, stop=True), inc="t", tag=f"{tag}:bcm")
        tbc = p.op("tensor", lambda e: e.matmul(
            PRB[:, 0:TQ], self.t_perm("onesf")[0:1, 0:128], rrow,
            start=True, stop=True), inc="t", tag=f"{tag}:bcr")
        p.wait("vector", "t", tbc, tag=f"{tag}:norm")
        p.wait("vector", "xhw", tag=f"{tag}:xbuf-war")
        tmpf = self.t("tmpf", 200, 512, F32)
        for cc in range(Dc):
            sl = sta[:, cc * TQ:(cc + 1) * TQ]
            p.op("vector", lambda e, s=sl:
                 e.tensor_tensor(tmpf, s, PMB[:, 0:TQ], ALU.subtract),
                 inc="v", tag=f"{tag}:sub{cc}")
            p.op("vector", lambda e, d=xbuf[:, cc * TQ:(cc + 1) * TQ]:
                 e.tensor_tensor(d, tmpf, PRB[:, 0:TQ], ALU.mult),
                 inc="v", tag=f"{tag}:mul{cc}")
        vn = p.chans["v"].n
        p.wait("tensor", "v", vn, tag=f"{tag}:bank-free")
        p.wait("sync", "v", vn, tag=f"{tag}:st-wait")
        out3 = self.dram["xh8"].rearrange("(cc p) t -> p cc t", p=128)[
            :, :, tt * c["TQ"]:(tt + 1) * c["TQ"]]
        p.dma(out3, xbuf.rearrange("p (cc t) -> p cc t", cc=Dc), "xhw",
              tag=f"{tag}:st")

    def proj(self, name, wsrc, n_oc, n_kc, NTn, rhs_fn, evac_fn, woff_kb,
             oc0=0, drain=False):
        p, c = self.p, self.c
        TQ = c["TQ"]
        wst = [self.t(f"{name}wst{i}", woff_kb + i * (n_kc * 128 * 4) / KB,
                      n_kc * 128, F32) for i in range(2)]
        bfo = woff_kb + 2 * (n_kc * 128 * 4) / KB
        wbf = [self.t(f"{name}wbf{i}", bfo + i * (n_kc * 128 * 2) / KB,
                      n_kc * 128, BF16) for i in range(2)]
        wd = [p.chan("wd0"), p.chan("wd1")]
        free_marks = {}
        cast_marks = {}
        p.wait("tensor", "s", tag=f"{name}:banks-quiet")
        p.wait("tensor", "v", tag=f"{name}:banks-quiet-v")
        for oc in range(n_oc):
            sl = oc % 2
            src3 = wsrc.rearrange("(kc p) o -> p kc o", p=128)[
                :, :, (oc0 + oc) * 128:(oc0 + oc + 1) * 128]
            dst3 = wst[sl].rearrange("p (kc j) -> p kc j", kc=n_kc)
            if oc < 2:
                p.wait("sync", "t", tag=f"{name}:fresh{oc}")
                p.wait("sync", "v", tag=f"{name}:fresh-v{oc}")
            else:
                p.wait("sync", "v", cast_marks[oc - 2],
                       tag=f"{name}:wstfree{oc}")
            p.dma(dst3, src3, wd[sl], tag=f"{name}:wld{oc}")
            p.wait("vector", wd[sl], tag=f"{name}:wcast-wait{oc}")
            if oc >= 2:
                chn, v = free_marks[(oc - 2, NTn - 1)]
                p.wait("vector", chn, v, tag=f"{name}:wbffree{oc}")
            cast_marks[oc] = p.op(
                "vector", lambda e, d=wbf[sl], s=wst[sl]: e.tensor_copy(d, s),
                inc="v", tag=f"{name}:wcast{oc}")
            if drain:
                self.bg_step(3)
            p.wait("tensor", "v", cast_marks[oc], tag=f"{name}:mm-wait{oc}")
            for n in range(NTn):
                g = oc * NTn + n
                bank = self.ps[g % 2]
                if g >= 2:
                    pk = ((g - 2) // NTn, (g - 2) % NTn)
                    chn, v = free_marks[pk]
                    p.wait("tensor", chn, v, tag=f"{name}:bankfree{oc},{n}")
                tmk = None
                for kc in range(n_kc):
                    tmk = p.op("tensor", lambda e, b=bank, w=wbf[sl], k=kc,
                               nn=n, s=(kc == 0), st=(kc == n_kc - 1):
                               e.matmul(b[:, 0:TQ], w[:, k * 128:(k + 1) * 128],
                                        rhs_fn(k, nn), start=s, stop=st),
                               inc="t", tag=f"{name}:mm{oc},{n},{kc}")
                ev = evac_fn(oc, n, bank[:, 0:TQ], tmk)
                free_marks[(oc, n)] = ev
        return free_marks

    def phase_A(self):
        p, c = self.p, self.c
        TQ, NT, Dc, T = c["TQ"], c["NT"], c["Dc"], c["T"]
        xhat = self.t("xhatA", 84, Dc * T, BF16)
        sta = self.t("staA", 148, Dc * TQ, F32)
        for tt in range(NT):
            self.ln_in(self.dram["x"], tt, xhat, T, sta, f"A:ln{tt}")

        ctq = self.t("ctq", 20, c["DQc"] * T, BF16)
        ctkv = self.t("ctkv", 68, c["DKVc"] * T, BF16)

        def rhs(kc, n):
            return xhat[:, kc * T + n * TQ: kc * T + (n + 1) * TQ]

        def mk_evac(dst_slab, bias0):
            def evac(oc, n, ps, tmk):
                p.wait("scalar", "t", tmk, tag=f"ct:ev-wait{oc},{n}")
                sv = p.op("scalar", lambda e,
                          d=dst_slab[:, oc * T + n * TQ:oc * T + (n + 1) * TQ],
                          s=ps, b=self.bias_col(bias0 + oc):
                          e.activation(d, s, AF.Identity, bias=b),
                          inc="s", tag=f"ct:ev{oc},{n}")
                return ("s", sv)
            return evac

        self.proj("ctq", self.dram["wdq"], c["DQc"], Dc, NT, rhs,
                  mk_evac(ctq, self.b_dq()), woff_kb=148)
        self.proj("ctkv", self.dram["wdkv"], c["DKVc"], Dc, NT, rhs,
                  mk_evac(ctkv, self.b_dkv()), woff_kb=148)

    def phase_B(self):
        p, c = self.p, self.c
        TQ, NT, Dc, KVc, T = c["TQ"], c["NT"], c["Dc"], c["DKVc"], c["T"]
        ctkv = self.t("ctkv", 68, KVc * T, BF16)
        wst = self.t("Bwst", 84, KVc * c["D"], F32)
        wukb = self.t("Bwuk", 116, KVc * c["D"], BF16)
        wkrb = self.t("Bwkr", 132, KVc * c["D"], BF16)
        wd = p.chan("wd0")
        p.wait("sync", "t", tag="B:fresh")
        p.wait("sync", "v", tag="B:fresh-v")
        p.dma(wst.rearrange("p (kc o) -> p kc o", kc=KVc),
              self.dram["wuk"].rearrange("(kc p) o -> p kc o", p=128),
              wd, tag="B:lduk")
        p.wait("vector", wd, tag="B:castuk")
        vuk = p.op("vector", lambda e: e.tensor_copy(wukb, wst), inc="v",
                   tag="B:castuk")
        p.wait("sync", "v", vuk, tag="B:wst-free1")
        p.dma(wst.rearrange("p (kc o) -> p kc o", kc=KVc),
              self.dram["wkr"].rearrange("(kc p) o -> p kc o", p=128),
              wd, tag="B:ldkr")
        p.wait("vector", wd, tag="B:castkr")
        vkr = p.op("vector", lambda e: e.tensor_copy(wkrb, wst), inc="v",
                   tag="B:castkr")
        csl = [self.t(f"Bcs{i}", 148 + i * (2 * T * 2) / KB, 2 * T, BF16)
               for i in range(2)]
        csd = [p.chan("csd0"), p.chan("csd1")]
        ssd = [p.chan("ssd0"), p.chan("ssd1")]
        krS = self.t("BkrS", 180, T, BF16)
        kslab = self.t("Bkslab", 184, T, BF16)
        vslab = self.t("Bvslab", 188, c["D"], BF16)
        tmpf = self.t("tmpf", 200, 512, F32)
        tmpf2 = self.t("tmpf2", 202, 512, F32)
        p.wait("tensor", "v", vkr, tag="B:wait-casts")
        kw = p.chan("kvw")
        PK, PR, PP = self.ps[0], self.ps[1], self.ps[2]
        vdone = {}
        for oc in range(Dc):
            sl = oc % 2
            if oc >= 1:
                p.wait("vector", kw, tag=f"B:kslab-war{oc}")
            if oc >= 2:
                p.wait("sync", "v", vdone[oc - 2], tag=f"B:csfree{oc}")
            p.dma(csl[sl][:, 0:T],
                  self.dram["cexp"][oc * 128:(oc + 1) * 128, :], csd[sl],
                  tag=f"B:ldc{oc}")
            p.dma(csl[sl][:, T:2 * T],
                  self.dram["sexp"][oc * 128:(oc + 1) * 128, :], ssd[sl],
                  tag=f"B:lds{oc}")
            for n in range(NT):
                tk = None
                for kc in range(KVc):
                    tk = p.op("tensor", lambda e, k=kc, nn=n, o=oc,
                              s=(kc == 0), st=(kc == KVc - 1):
                              e.matmul(PK[:, 0:TQ],
                                       wukb[:, k * c["D"] + o * 128:
                                            k * c["D"] + (o + 1) * 128],
                                       ctkv[:, k * T + nn * TQ:
                                            k * T + (nn + 1) * TQ],
                                       start=s, stop=st),
                              inc="t", tag=f"B:mmk{oc},{n},{kc}")
                tr = None
                for kc in range(KVc):
                    tr = p.op("tensor", lambda e, k=kc, nn=n, o=oc,
                              s=(kc == 0), st=(kc == KVc - 1):
                              e.matmul(PR[:, 0:TQ],
                                       wkrb[:, k * c["D"] + o * 128:
                                            k * c["D"] + (o + 1) * 128],
                                       ctkv[:, k * T + nn * TQ:
                                            k * T + (nn + 1) * TQ],
                                       start=s, stop=st),
                              inc="t", tag=f"B:mmr{oc},{n},{kc}")
                p.wait("scalar", "t", tr, tag=f"B:evr{oc},{n}")
                sv = p.op("scalar", lambda e, nn=n,
                          bb_=self.bias_col(self.b_kr() + oc):
                          e.activation(krS[:, nn * TQ:(nn + 1) * TQ],
                                       PR[:, 0:TQ], AF.Identity, bias=bb_),
                          inc="s", tag=f"B:evr{oc},{n}")
                p.wait("tensor", "s", sv, tag=f"B:pm{oc},{n}")
                tp = p.op("tensor", lambda e, nn=n:
                          e.matmul(PP[:, 0:TQ], self.t_perm("ptrope"),
                                   krS[:, nn * TQ:(nn + 1) * TQ],
                                   start=True, stop=True),
                          inc="t", tag=f"B:pmm{oc},{n}")
                p.wait("vector", "t", tp, tag=f"B:rope{oc},{n}")
                p.wait("vector", csd[sl], tag=f"B:cs{oc},{n}")
                p.wait("vector", ssd[sl], tag=f"B:ss{oc},{n}")
                ch_ = csl[sl][:, n * TQ:(n + 1) * TQ]
                sh_ = csl[sl][:, T + n * TQ:T + (n + 1) * TQ]
                p.op("vector", lambda e, nn=n, cc=ch_:
                     e.tensor_tensor(tmpf, krS[:, nn * TQ:(nn + 1) * TQ], cc,
                                     ALU.mult), inc="v", tag=f"B:r1{oc},{n}")
                p.op("vector", lambda e, ss=sh_:
                     e.tensor_tensor(tmpf2, PP[:, 0:TQ], ss, ALU.mult),
                     inc="v", tag=f"B:r2{oc},{n}")
                p.op("vector", lambda e:
                     e.tensor_tensor(tmpf, tmpf, tmpf2, ALU.add),
                     inc="v", tag=f"B:r2b{oc},{n}")
                vv = p.op("vector", lambda e, nn=n,
                          bb_=self.bias_col(self.b_uk() + oc):
                          e.scalar_tensor_tensor(
                              kslab[:, nn * TQ:(nn + 1) * TQ], PK[:, 0:TQ],
                              bb_, tmpf, ALU.add, ALU.add),
                          inc="v", tag=f"B:r3{oc},{n}")
                p.wait("tensor", "v", vv, tag=f"B:bank{oc},{n}")
            vdone[oc] = vv
            p.wait("sync", "v", vv, tag=f"B:stk{oc}")
            p.dma(self.dram["k0"][oc * 128:(oc + 1) * 128, :], kslab[:, 0:T],
                  kw, tag=f"B:stk{oc}")
        wuvb = self.t("Bwuv", 116, KVc * c["D"], BF16)
        p.wait("sync", "t", tag="B:uk-done")
        p.dma(wst.rearrange("p (kc o) -> p kc o", kc=KVc),
              self.dram["wuv"].rearrange("(kc p) o -> p kc o", p=128),
              wd, tag="B:lduv")
        p.wait("vector", wd, tag="B:castuv")
        p.wait("vector", "t", tag="B:castuv-t")
        vuv = p.op("vector", lambda e: e.tensor_copy(wuvb, wst), inc="v",
                   tag="B:castuv")
        bvb = self.t_perm("bvb")
        brow = self.t("Bbrow", 132, c["D"], F32)
        ad = p.chan("ad0")
        p.dma(brow[0:1, 0:c["D"]], self.dram["bvrow"], ad, tag="B:ldbv")
        p.wait("tensor", ad, tag="B:bvb")
        p.wait("tensor", "v", vuv, tag="B:v-ready")
        nD = max(1, c["D"] // TQ)
        for n in range(nD):
            if n >= 1:
                p.wait("tensor", "s", tag=f"B:bvbank{n}")
            tb = p.op("tensor", lambda e, nn=n:
                      e.matmul(self.ps[4][:, 0:TQ],
                               self.t_perm("onesf")[0:1, 0:128],
                               brow[0:1, nn * TQ:(nn + 1) * TQ],
                               start=True, stop=True),
                      inc="t", tag=f"B:bvmm{n}")
            p.wait("scalar", "t", tb, tag=f"B:bvev{n}")
            p.op("scalar", lambda e, nn=n:
                 e.activation(bvb[:, nn * TQ:(nn + 1) * TQ],
                              self.ps[4][:, 0:TQ], AF.Copy),
                 inc="s", tag=f"B:bvev{n}")
        sbv = p.chans["s"].n
        p.wait("tensor", "s", sbv, tag="B:bv-done")
        for ttk in range(c["NKT"]):
            if ttk >= 1:
                p.wait("vector", kw, tag=f"B:vslab-war{ttk}")
            for n in range(nD):
                b = self.ps[n % 4]
                tmk = None
                for kc in range(KVc):
                    tmk = p.op("tensor", lambda e, k=kc, nn=n, t2=ttk, P=b,
                               s=(kc == 0), st=(kc == KVc - 1):
                               e.matmul(P[:, 0:TQ],
                                        ctkv[:, k * T + t2 * 128:
                                             k * T + (t2 + 1) * 128],
                                        wuvb[:, k * c["D"] + nn * TQ:
                                             k * c["D"] + (nn + 1) * TQ],
                                        start=s, stop=st),
                               inc="t", tag=f"B:vmm{ttk},{n},{kc}")
                p.wait("vector", "t", tmk, tag=f"B:vev{ttk},{n}")
                vv = p.op("vector", lambda e, nn=n, P=b:
                          e.tensor_tensor(vslab[:, nn * TQ:(nn + 1) * TQ],
                                          P[:, 0:TQ],
                                          bvb[:, nn * TQ:(nn + 1) * TQ],
                                          ALU.add),
                          inc="v", tag=f"B:vev{ttk},{n}")
            p.wait("sync", "v", vv, tag=f"B:stv{ttk}")
            p.dma(self.dram["v0"][ttk * 128:(ttk + 1) * 128, :],
                  vslab[:, 0:c["D"]], kw, tag=f"B:stv{ttk}")
            p.wait("tensor", "v", vv, tag=f"B:vbank{ttk}")

    def head_bound(self, qh, kh_slice, mh_col, tag):
        p, c = self.p, self.c
        TQ, NT = c["TQ"], c["NT"]
        statr = self.t_perm("statr")
        mhs = statr[0:1, 1002:1003]
        kcast = self.t_perm("kcast")
        tmpf = self.t("tmpf", 200, 512, F32)
        PN = self.ps[6]
        p.wait("tensor", "s", tag=f"{tag}:quiet")
        p.wait("tensor", "v", tag=f"{tag}:quiet-v")
        tmm = p.op("tensor", lambda e: e.matmul(
            PN[:, 0:TQ], kh_slice, qh[:, (NT - 1) * TQ:NT * TQ],
            start=True, stop=True), inc="t", tag=f"{tag}:smm")
        p.wait("vector", "t", tmm, tag=f"{tag}:red")
        p.op("vector", lambda e: e.reduce_max(
            tmpf[:, 0:1], PN[:, 0:TQ], AX.X), inc="v", tag=f"{tag}:red")
        vv = p.op("vector", lambda e: e.tensor_copy(kcast[:, 0:1],
                                                    tmpf[:, 0:1]),
                  inc="v", tag=f"{tag}:cast")
        p.wait("tensor", "v", vv, tag=f"{tag}:tr")
        tt_ = p.op("tensor", lambda e: e.transpose(
            self.ps["bf"][0:1, 0:128], kcast[:, 0:1], self.t_perm("identb")),
            inc="t", tag=f"{tag}:tr")
        p.wait("vector", "t", tt_, tag=f"{tag}:red2")
        p.op("vector", lambda e: e.reduce_max(
            mhs, self.ps["bf"][0:1, 0:128], AX.X), inc="v", tag=f"{tag}:red2")
        vmh = p.op("vector", lambda e: e.tensor_scalar(
            mhs, mhs, self.col("negsm")[0:1, :], None, ALU.mult), inc="v",
            tag=f"{tag}:neg")
        p.wait("tensor", "v", vmh, tag=f"{tag}:bc")
        tmh = p.op("tensor", lambda e: e.matmul(
            PN[:, 0:1], self.t_perm("onesf")[0:1, 0:128], mhs,
            start=True, stop=True), inc="t", tag=f"{tag}:bc")
        p.wait("vector", "t", tmh, tag=f"{tag}:cp")
        vmh = p.op("vector", lambda e: e.tensor_copy(
            mh_col, PN[:, 0:1]), inc="v", tag=f"{tag}:cp")
        p.wait("scalar", "v", vmh, tag=f"{tag}:ready")
        p.wait("tensor", "v", vmh, tag=f"{tag}:bankfree")

    def phase_C(self):
        p, c = self.p, self.c
        TQ, NT, T, Dc = c["TQ"], c["NT"], c["T"], c["Dc"]
        DQc, NH, NKT = c["DQc"], c["NH"], c["NKT"]
        ctq = self.t("ctq", 20, DQc * T, BF16)
        kh = [self.t(f"Ckh{i}", 84 + i * (T * 2) / KB, T, BF16)
              for i in range(2)]
        vh = [self.t(f"Cvh{i}", 92 + i * (T * 2) / KB, T, BF16)
              for i in range(2)]
        qh = self.t("Cqh", 100, T, BF16)
        ctxh = self.t("Cctxh", 104, T, BF16)
        qrS = self.t("CqrS", 108, T, BF16)
        csl = [self.t(f"Ccs{i}", 112 + i * (2 * T * 2) / KB, 2 * T, BF16)
               for i in range(2)]
        pT = [self.t(f"CpT{i}", 164 + i * 2, 2 * TQ, BF16) for i in range(2)]
        rbf = self.t("Crbf", 168, TQ, F32)
        wstq = [self.t(f"Cwstq{i}", 128 + i * (DQc * 128 * 4) / KB,
                       DQc * 128, F32) for i in range(2)]
        wstr = [self.t(f"Cwstr{i}", 140 + i * (DQc * 128 * 4) / KB,
                       DQc * 128, F32) for i in range(2)]
        wqb = [self.t(f"Cwqb{i}", 152 + i * (DQc * 128 * 2) / KB,
                      DQc * 128, BF16) for i in range(2)]
        wrb = [self.t(f"Cwrb{i}", 158 + i * (DQc * 128 * 2) / KB,
                      DQc * 128, BF16) for i in range(2)]
        tmpf = self.t("tmpf", 200, 512, F32)
        tmpf2 = self.t("tmpf2", 202, 512, F32)
        statr = self.t_perm("statr")
        wd = [p.chan("wd0"), p.chan("wd1")]
        wdr = [p.chan("wdr0"), p.chan("wdr1")]
        khd = [p.chan("khd0"), p.chan("khd1")]
        vhd = [p.chan("vhd0"), p.chan("vhd1")]
        csd = [p.chan("csd0"), p.chan("csd1")]
        ssd = [p.chan("ssd0"), p.chan("ssd1")]
        abw = p.chan("abw")
        p.wait("sync", "kvw", tag="C:kv-landed")
        mh_col = self.col("mh0")
        self._smark = {}
        self._rbe_prev = None
        self._ctx_marks = {}
        cast_marks = {}
        head_done = {}
        rope_done = {}
        for h in range(NH):
            sl = h % 2
            if h >= 2:
                p.wait("sync", "s", cast_marks[h - 2], tag=f"C:wstfree{h}")
            src3 = self.dram["wuq"].rearrange("(kc p) o -> p kc o", p=128)[
                :, :, h * 128:(h + 1) * 128]
            p.dma(wstq[sl].rearrange("p (kc j) -> p kc j", kc=DQc), src3,
                  wd[sl], tag=f"C:wldq{h}")
            src3 = self.dram["wqr"].rearrange("(kc p) o -> p kc o", p=128)[
                :, :, h * 128:(h + 1) * 128]
            p.dma(wstr[sl].rearrange("p (kc j) -> p kc j", kc=DQc), src3,
                  wdr[sl], tag=f"C:wldr{h}")
            if h >= 2:
                p.wait("sync", "t", head_done[h - 2], tag=f"C:khfree{h}")
            p.dma(kh[sl], self.dram["k0"][h * 128:(h + 1) * 128, :],
                  khd[sl], tag=f"C:ldkh{h}")
            vsrc = bass.AP(self.dram["v0"].tensor, h * 128,
                           [[c["D"], 128], [128 * c["D"], NKT], [1, 128]])
            p.dma(vh[sl].rearrange("p (t j) -> p t j", t=NKT), vsrc,
                  vhd[sl], tag=f"C:ldvh{h}")
            if h >= 2:
                p.wait("sync", "v", rope_done[h - 2], tag=f"C:csfree{h}")
            p.dma(csl[sl][:, 0:T],
                  self.dram["cexp"][h * 128:(h + 1) * 128, :], csd[sl],
                  tag=f"C:ldc{h}")
            p.dma(csl[sl][:, T:2 * T],
                  self.dram["sexp"][h * 128:(h + 1) * 128, :], ssd[sl],
                  tag=f"C:lds{h}")
            self.bg_step(1, allow_collapse=True)
            p.wait("scalar", wd[sl], tag=f"C:wcast{h}")
            p.wait("scalar", wdr[sl], tag=f"C:wcastr-dma{h}")
            p.op("scalar", lambda e, s=sl: e.copy(wqb[s], wstq[s]),
                 inc="s", tag=f"C:wcastq{h}")
            cast_marks[h] = p.op(
                "scalar", lambda e, s=sl: e.copy(wrb[s], wstr[s]),
                inc="s", tag=f"C:wcastr{h}")
            p.wait("tensor", "s", cast_marks[h], tag=f"C:q-wait{h}")
            PQm, PRm, PPm = self.ps[0], self.ps[1], self.ps[2]
            for n in range(NT):
                for kc in range(DQc):
                    p.op("tensor", lambda e, k=kc, nn=n, s=sl,
                         st=(kc == DQc - 1), s0=(kc == 0):
                         e.matmul(PQm[:, 0:TQ],
                                  wqb[s][:, k * 128:(k + 1) * 128],
                                  ctq[:, k * T + nn * TQ:k * T + (nn + 1) * TQ],
                                  start=s0, stop=st),
                         inc="t", tag=f"C:qmm{h},{n},{kc}")
                tr = None
                for kc in range(DQc):
                    tr = p.op("tensor", lambda e, k=kc, nn=n, s=sl,
                              st=(kc == DQc - 1), s0=(kc == 0):
                              e.matmul(PRm[:, 0:TQ],
                                       wrb[s][:, k * 128:(k + 1) * 128],
                                       ctq[:, k * T + nn * TQ:
                                            k * T + (nn + 1) * TQ],
                                       start=s0, stop=st),
                              inc="t", tag=f"C:rmm{h},{n},{kc}")
                p.wait("scalar", "t", tr, tag=f"C:evr{h},{n}")
                sv = p.op("scalar", lambda e, nn=n,
                          bb_=self.bias_col(self.b_qr() + h):
                          e.activation(qrS[:, nn * TQ:(nn + 1) * TQ],
                                       PRm[:, 0:TQ], AF.Identity, bias=bb_),
                          inc="s", tag=f"C:evr{h},{n}")
                p.wait("tensor", "s", sv, tag=f"C:pm{h},{n}")
                tp = p.op("tensor", lambda e, nn=n:
                          e.matmul(PPm[:, 0:TQ], self.t_perm("ptrope"),
                                   qrS[:, nn * TQ:(nn + 1) * TQ],
                                   start=True, stop=True),
                          inc="t", tag=f"C:pmm{h},{n}")
                p.wait("vector", "t", tp, tag=f"C:rope{h},{n}")
                p.wait("vector", csd[sl], tag=f"C:cs{h},{n}")
                p.wait("vector", ssd[sl], tag=f"C:ss{h},{n}")
                ch_ = csl[sl][:, n * TQ:(n + 1) * TQ]
                sh_ = csl[sl][:, T + n * TQ:T + (n + 1) * TQ]
                p.op("vector", lambda e, nn=n, cc=ch_:
                     e.tensor_tensor(tmpf, qrS[:, nn * TQ:(nn + 1) * TQ], cc,
                                     ALU.mult), inc="v", tag=f"C:r1{h},{n}")
                p.op("vector", lambda e, ss=sh_:
                     e.tensor_tensor(tmpf2, PPm[:, 0:TQ], ss, ALU.mult),
                     inc="v", tag=f"C:r2{h},{n}")
                p.op("vector", lambda e:
                     e.tensor_tensor(tmpf, tmpf, tmpf2, ALU.add),
                     inc="v", tag=f"C:r2b{h},{n}")
                vq = p.op("vector", lambda e, nn=n,
                          bb_=self.bias_col(self.b_uq() + h):
                          e.scalar_tensor_tensor(
                              qh[:, nn * TQ:(nn + 1) * TQ], PQm[:, 0:TQ],
                              bb_, tmpf, ALU.add, ALU.add),
                          inc="v", tag=f"C:r3{h},{n}")
                p.wait("tensor", "v", vq, tag=f"C:qbank{h},{n}")
            rope_done[h] = vq
            p.wait("vector", "abw", tag=f"C:ctx-war{h}")
            p.wait("tensor", khd[sl], tag=f"C:kvh-ready{h}")
            p.wait("tensor", vhd[sl], tag=f"C:vh-ready{h}")
            self.head_bound(qh[:, 0:T], kh[sl][:, 0:128], mh_col, f"C:hb{h}")
            SP = [self.ps["A"], self.ps["B"]]
            for qc in range(NT):
                PC = self.ps[4] if qc % 2 == 0 else self.ps[5]
                PS6 = self.ps[6]
                exp_marks = {}
                npair = NKT // 2

                def smm(pi2, qv=qc):
                    if pi2 >= npair:
                        return
                    if pi2 >= 2:
                        chn, vv_ = exp_marks[pi2 - 2]
                        p.wait("tensor", chn, vv_, tag=f"C:Sfree{h},{qv},{pi2}")
                    mk = None
                    for half in range(2):
                        mk = p.op("tensor", lambda e, k=2 * pi2 + half, s=sl,
                                  q=qv, bb=pi2 % 2, hh=half:
                                  e.matmul(SP[bb][:, hh * TQ:(hh + 1) * TQ],
                                           kh[s][:, k * 128:(k + 1) * 128],
                                           qh[:, q * TQ:(q + 1) * TQ],
                                           start=True, stop=True),
                                  inc="t", tag=f"C:smm{h},{qv},{pi2},{half}")
                    self._smark[(h, qv, pi2)] = mk

                smm(0)
                for pi in range(npair):
                    b = pi % 2
                    smm(pi + 1)
                    if pi == 1:
                        self.bg_step(12)
                    p.wait("scalar", "t", self._smark[(h, qc, pi)],
                           tag=f"C:exp{h},{qc},{pi}")
                    sv = p.op("scalar", lambda e, bb=b, pp=pi:
                              e.activation(pT[pp % 2], SP[bb][:, 0:2 * TQ],
                                           AF.Exp, bias=mh_col,
                                           scale=self.col("sm")),
                              inc="s", tag=f"C:exp{h},{qc},{pi}")
                    exp_marks[pi] = ("s", sv)
                    p.wait("tensor", "s", sv, tag=f"C:pv{h},{qc},{pi}")
                    if pi == 0 and self._rbe_prev is not None:
                        p.wait("tensor", "s", self._rbe_prev,
                               tag=f"C:sumfree{h},{qc}")
                    if pi == 0 and self._ctx_marks.get(qc % 2) is not None:
                        p.wait("tensor", "v", self._ctx_marks[qc % 2],
                               tag=f"C:pcfree{h},{qc}")
                    for half in range(2):
                        kt = 2 * pi + half
                        p.op("tensor", lambda e, k=kt, s=sl, pp=pi, hh=half,
                             P=PC, s0=(kt == 0), st=(kt == NKT - 1):
                             e.matmul(P[:, 0:TQ],
                                      vh[s][:, k * 128:(k + 1) * 128],
                                      pT[pp % 2][:, hh * TQ:(hh + 1) * TQ],
                                      start=s0, stop=st),
                             inc="t", tag=f"C:pv{h},{qc},{kt}")
                        tsum = p.op("tensor", lambda e, pp=pi, hh=half,
                                    s0=(kt == 0), st=(kt == NKT - 1):
                                    e.matmul(PS6[0:1, 0:TQ],
                                             self.t_perm("onesb")[:, 0:1],
                                             pT[pp % 2][:, hh * TQ:(hh + 1) * TQ],
                                             start=s0, stop=st),
                                    inc="t", tag=f"C:sum{h},{qc},{kt}")
                p.wait("vector", "t", tsum, tag=f"C:recip{h},{qc}")
                srow = statr[0:1, 0:TQ]
                vv = p.op("vector", lambda e: e.reciprocal(
                    srow, PS6[0:1, 0:TQ]), inc="v", tag=f"C:recip{h},{qc}")
                p.wait("tensor", "v", vv, tag=f"C:rb{h},{qc}")
                tb = p.op("tensor", lambda e: e.matmul(
                    PS6[:, 0:TQ], self.t_perm("onesf")[0:1, 0:128], srow,
                    start=True, stop=True), inc="t", tag=f"C:rb{h},{qc}")
                p.wait("scalar", "t", tb, tag=f"C:rbe{h},{qc}")
                self._rbe_prev = p.op("scalar", lambda e: e.activation(
                    rbf, PS6[:, 0:TQ], AF.Copy), inc="s",
                    tag=f"C:rbe{h},{qc}")
                p.wait("vector", "s", self._rbe_prev, tag=f"C:ctx{h},{qc}")
                self._ctx_marks[qc % 2] = p.op(
                    "vector", lambda e, q=qc, P=PC: e.tensor_tensor(
                        ctxh[:, q * TQ:(q + 1) * TQ], P[:, 0:TQ], rbf,
                        ALU.mult),
                    inc="v", tag=f"C:ctx{h},{qc}")
            head_done[h] = p.chans["t"].n
            p.wait("sync", "v", tag=f"C:stab{h}")
            p.dma(self.dram["ab"][h * 128:(h + 1) * 128, :], ctxh[:, 0:T],
                  abw, tag=f"C:stab{h}")

    def phase_D(self, wo_name, bias0, res_src, res_dst, out_ch_name, tag):
        p, c = self.p, self.c
        TQ, Dc, T = c["TQ"], c["Dc"], c["T"]
        T2 = max(c["T2"], TQ)
        nhalves = T // T2
        p.wait("sync", "abw", tag=f"{tag}:ab-landed")
        asb = self.t("Dasb", 20, Dc * T2, BF16)
        aout = self.t("Daout", 52, Dc * T2, BF16)
        ad = p.chan("ad0")
        out_ch = p.chan(out_ch_name)
        NTh = T2 // TQ
        for half in range(nhalves):
            t0 = half * T2
            src3 = self.dram["ab"].rearrange("(cc p) t -> p cc t", p=128)[
                :, :, t0:t0 + T2]
            p.wait("sync", "t", tag=f"{tag}:asb-free{half}")
            p.wait("scalar", "v", tag=f"{tag}:aout-free{half}")
            p.dma(asb.rearrange("p (cc t) -> p cc t", cc=Dc), src3, ad,
                  tag=f"{tag}:lda{half}")
            p.wait("vector", ad, tag=f"{tag}:a-ready{half}")
            vr = p.op("vector", lambda e: e.engine_nop(), inc="v",
                      tag=f"{tag}:a-ready{half}")
            p.wait("tensor", "v", vr, tag=f"{tag}:a-ready-t{half}")
            self.bg_step(2, allow_collapse=True)

            def rhs(kc, n):
                return asb[:, kc * T2 + n * TQ: kc * T2 + (n + 1) * TQ]

            def evac(oc, n, ps, tmk):
                p.wait("scalar", "t", tmk, tag=f"{tag}:ev-wait{oc},{n}")
                sv = p.op("scalar", lambda e,
                          d=aout[:, oc * T2 + n * TQ:oc * T2 + (n + 1) * TQ],
                          s=ps, b=self.bias_col(bias0 + oc):
                          e.activation(d, s, AF.Identity, bias=b),
                          inc="s", tag=f"{tag}:ev{oc},{n}")
                return ("s", sv)

            self.proj(f"{tag}wo{half}", self.dram[wo_name], Dc, Dc, NTh,
                      rhs, evac, woff_kb=116, drain=True)
            sev = p.chans["s"].n
            p.wait("vector", "s", sev, tag=f"{tag}:ln-wait{half}")
            xbuf = self.t("Dxbuf", 148, Dc * TQ, BF16)
            for ttl in range(NTh):
                self.ln_post_add(aout, T2, half * NTh + ttl, res_src, res_dst,
                                 out_ch, f"{tag}:pa{half}.{ttl}",
                                 make_xhat=xbuf)
        return out_ch

    def bg_bufs(self, ring=2):
        b = getattr(self, "bg_region", 170)
        if ring == 4:
            gst = [self.t(f"bgr{i}_{b}", b + 4 * i, 1024, F32)
                   for i in range(4)]
            return gst, None, None, None
        gst = [self.t(f"bgst0_{b}", b, 1024, F32),
               self.t(f"bgst1_{b}", b + 4, 1024, F32)]
        pos = self.t(f"bgpos_{b}", b + 8, 1024, BF16)
        neg = self.t(f"bgneg_{b}", b + 10, 1024, BF16)
        bft = [self.t(f"bgbf0_{b}", b + 12, 1024, BF16),
               self.t(f"bgbf1_{b}", b + 14, 1024, BF16)]
        return gst, pos, neg, bft

    def gamma_tiles_gen(self, wname, gname, tag):
        p, c = self.p, self.c
        ap = self.dram[wname]
        K, O = ap.tensor.shape
        if O <= 1024:
            CW = O
        else:
            CW = next(w for w in (1024, 512, 256, 128) if O % w == 0)
        ncw = O // CW
        gd = [p.chan(f"g{i}") for i in range(4)]
        gacc = self.col(gname)
        gtmp = self.col(gname + "_t")
        p.op("vector", lambda e: e.memset(gacc, 0.0), inc="v", tag=f"{tag}:0")
        vprev = {}
        i = 0
        for rc in range(K // 128):
            for j in range(ncw):
                sl = i % 4
                if i == 0:
                    p.wait("gpsimd", "v", tag=f"{tag}:prev-user")
                    for chn in ("tzw0", "tzw1"):
                        if chn in p.chans:
                            p.wait("gpsimd", chn, tag=f"{tag}:tern-quiet")
                if i >= 4:
                    p.wait("gpsimd", "v", vprev[i - 4], tag=f"{tag}:free{i}")
                gst, _, _, _ = self.bg_bufs(ring=4)
                p.dma(gst[sl][:, 0:CW],
                      ap[rc * 128:(rc + 1) * 128, j * CW:(j + 1) * CW],
                      gd[sl], q="gpsimd", tag=f"{tag}:ld{i}")
                p.wait("vector", gd[sl], tag=f"{tag}:red{i}")
                p.op("vector", lambda e, g_=gst[sl], w=CW: e.tensor_reduce(
                    gtmp, g_[:, 0:w], AX.X, ALU.add,
                    apply_absolute_value=True), inc="v", tag=f"{tag}:red{i}")
                vprev[i] = p.op("vector", lambda e: e.tensor_tensor(
                    gacc, gacc, gtmp, ALU.add), inc="v", tag=f"{tag}:acc{i}")
                self._g_last_acc = vprev[i]
                i += 1
                yield

    def gamma_collapse_gen(self, gname, invN_name, tag):
        p = self.p
        gacc = self.col(gname)
        p.wait("tensor", "v", self._g_last_acc, tag=f"{tag}:tr")
        p.wait("tensor", "s", tag=f"{tag}:tr-s")
        tt_ = p.op("tensor", lambda e: e.transpose(
            self.ps[0][0:1, 0:128], gacc, self.t_perm("identf")),
            inc="t", tag=f"{tag}:tr")
        statr = self.t_perm("statr")
        g1 = statr[0:1, 1003:1004]
        g2 = statr[0:1, 1004:1005]
        p.wait("vector", "t", tt_, tag=f"{tag}:col")
        p.op("vector", lambda e: e.tensor_reduce(
            g1, self.ps[0][0:1, 0:128], AX.X, ALU.add), inc="v",
            tag=f"{tag}:col")
        p.op("vector", lambda e: e.tensor_scalar(
            g1, g1, self.col(invN_name)[0:1, :], self.col("eps8")[0:1, :],
            ALU.mult, ALU.add), inc="v", tag=f"{tag}:gam")
        vv = p.op("vector", lambda e: e.reciprocal(g2, g1), inc="v",
                  tag=f"{tag}:rec")
        p.wait("tensor", "v", vv, tag=f"{tag}:bc")
        p.op("tensor", lambda e: e.matmul(
            self.ps[0][:, 0:1], self.t_perm("onesf")[0:1, 0:128], g1,
            start=True, stop=True), inc="t", tag=f"{tag}:bc1")
        tb = p.op("tensor", lambda e: e.matmul(
            self.ps[1][:, 0:1], self.t_perm("onesf")[0:1, 0:128], g2,
            start=True, stop=True), inc="t", tag=f"{tag}:bc2")
        p.wait("vector", "t", tb, tag=f"{tag}:bcv")
        p.op("vector", lambda e: e.tensor_copy(
            self.col(gname + "_s"), self.ps[0][:, 0:1]), inc="v",
            tag=f"{tag}:bcv1")
        vv = p.op("vector", lambda e: e.tensor_copy(
            self.col(gname + "_r"), self.ps[1][:, 0:1]), inc="v",
            tag=f"{tag}:bcv2")
        p.wait("tensor", "v", vv, tag=f"{tag}:done")
        p.wait("scalar", "v", vv, tag=f"{tag}:done-s")
        yield

    def tern_tiles_gen(self, wname, cache, gname, tag):
        p, c = self.p, self.c
        ap = self.dram[wname]
        cap = self.dram[cache]
        K, O = ap.tensor.shape
        if O <= 1024:
            CW = O
        else:
            CW = next(w for w in (1024, 512, 256, 128) if O % w == 0)
        ncw = O // CW
        grecip_col = self.col(gname + "_r")
        gd = [p.chan("g0"), p.chan("g1")]
        tw = [p.chan("tzw0"), p.chan("tzw1")]
        sprev = {}
        vprev = {}
        i = 0
        for rc in range(K // 128):
            for j in range(ncw):
                sl = i % 2
                if i == 0:
                    p.wait("gpsimd", "v", tag=f"{tag}:prev-user")
                if i >= 2:
                    p.wait("gpsimd", "v", sprev[i - 2], tag=f"{tag}:free{i}")
                gst, pos, neg, bft = self.bg_bufs()
                p.dma(gst[sl][:, 0:CW],
                      ap[rc * 128:(rc + 1) * 128, j * CW:(j + 1) * CW],
                      gd[sl], q="gpsimd", tag=f"{tag}:ld{i}")
                p.wait("vector", gd[sl], tag=f"{tag}:cmp{i}")
                p.op("vector", lambda e, g_=gst[sl], po=pos, w=CW:
                     e.tensor_scalar(po[:, 0:w], g_[:, 0:w], grecip_col,
                                     self.col("half")[:, 0:1],
                                     ALU.mult, ALU.is_ge),
                     inc="v", tag=f"{tag}:pos{i}")
                sprev[i] = p.op("vector", lambda e, g_=gst[sl], ne=neg, w=CW:
                                e.tensor_scalar(ne[:, 0:w], g_[:, 0:w],
                                                grecip_col,
                                                self.col("neghalf")[:, 0:1],
                                                ALU.mult, ALU.is_le),
                                inc="v", tag=f"{tag}:neg{i}")
                p.wait("vector", tw[sl], tag=f"{tag}:bffree{i}")
                vprev[i] = p.op("vector", lambda e, b_=bft[sl], po=pos,
                                ne=neg, w=CW:
                                e.tensor_tensor(b_[:, 0:w], po[:, 0:w],
                                                ne[:, 0:w], ALU.subtract),
                                inc="v", tag=f"{tag}:sub{i}")
                p.wait("gpsimd", "v", vprev[i], tag=f"{tag}:st{i}")
                p.dma(cap[rc * 128:(rc + 1) * 128, j * CW:(j + 1) * CW],
                      bft[sl][:, 0:CW], tw[sl], q="gpsimd", tag=f"{tag}:st{i}")
                i += 1
                yield

    def bg_enqueue(self):
        q = []
        for wname, gname, invN in (("wup", "gup", "invE0"),
                                   ("wgt", "ggt", "invE0"),
                                   ("wdn", "gdn", "invE0")):
            q.append((self.gamma_tiles_gen(wname, gname, "G" + gname), False))
            q.append((self.gamma_collapse_gen(gname, invN, "Gc" + gname), True))
        q.append((self.tern_tiles_gen("wdn", "tdn", "gdn", "Tdn"), False))
        for wname, gname, invN in (("ww1", "gw1", "invE1"),
                                   ("ww2", "gw2", "invE1"),
                                   ("ww3", "gw3", "invE1")):
            q.append((self.gamma_tiles_gen(wname, gname, "G" + gname), False))
            q.append((self.gamma_collapse_gen(gname, invN, "Gc" + gname), True))
        q.append((self.tern_tiles_gen("ww3", "tw3", "gw3", "Tw3"), False))
        self.bg = q
        self.bg_total = len(q)

    def bg_step(self, n, allow_collapse=False):
        while n > 0 and self.bg:
            gen, is_col = self.bg[0]
            if is_col and not allow_collapse:
                return
            try:
                next(gen)
                n -= 1
            except StopIteration:
                self.bg.pop(0)

    def bg_finish(self, keep_last):
        while len(self.bg) > keep_last:
            gen, is_col = self.bg[0]
            try:
                next(gen)
            except StopIteration:
                self.bg.pop(0)

    def ffn(self, layer, res_src, res_dst, out_ch_name, tag):
        p, c = self.p, self.c
        TQ, Dc, T, NT = c["TQ"], c["Dc"], c["T"], c["NT"]
        Ec = c["E0c"] if layer == 0 else c["E1c"]
        wu_name, wg_name = (("wup", "wgt") if layer == 0 else ("ww1", "ww2"))
        dn_cache = "tdn" if layer == 0 else "tw3"
        gu = "gup" if layer == 0 else "gw1"
        gg = "ggt" if layer == 0 else "gw2"
        gd_ = "gdn" if layer == 0 else "gw3"
        out_ch = p.chan(out_ch_name)
        xh = self.t("Fxh", 20, Dc * T, BF16)
        sta = self.t("staD", 84, Dc * TQ, F32)
        wst = [[self.t(f"Fw{a}{i}", 116 + (a * 3 + i) * 8, Dc * 128, F32)
                for i in range(3)] for a in range(2)]
        pos = self.t("Fpos", 164, 2048, BF16)
        neg = self.t("Fneg", 168, 2048, BF16)
        wub = [self.t(f"Fwub{i}", 172 + i * 4, Dc * 128, BF16)
               for i in range(2)]
        wgb = [self.t(f"Fwgb{i}", 180 + i * 4, Dc * 128, BF16)
               for i in range(2)]
        hbuf = [self.t(f"Fhbuf{i}", 188 + i * 4, T, BF16) for i in range(2)]
        tsil = self.t("Ftsil", 202, 512, F32)
        wd = [p.chan(f"wd{i}") for i in range(2)]
        wdg = [p.chan(f"wdg{i}") for i in range(2)]
        wd3 = [p.chan("wd0"), p.chan("wd1"), p.chan("wd2")]
        wg3 = [p.chan("wdg0"), p.chan("wdg1"), p.chan("wdg2")]
        hw = [p.chan("hw0"), p.chan("hw1")]
        p.wait("sync", "xhw", tag=f"{tag}:xh-landed")
        xld = [p.chan("ad0"), p.chan("ad1")]
        for tt in range(NT):
            p.wait("sync", "t", tag=f"{tag}:xh-free{tt}")
            p.dma(xh.rearrange("p (cc t) -> p cc t", cc=Dc)[
                      :, :, tt * TQ:(tt + 1) * TQ],
                  self.dram["xh8"].rearrange("(cc p) t -> p cc t", p=128)[
                      :, :, tt * TQ:(tt + 1) * TQ],
                  xld[tt % 2], tag=f"{tag}:ldxh{tt}")
        hid = self.dram["hd0"]
        for chn in ("g0", "g1", "g2", "g3", "tzw0", "tzw1", "xhw"):
            if chn in p.chans:
                p.wait("vector", chn, tag=f"{tag}:bg-quiet-start")

        def w_dma(ec):
            if ec >= Ec:
                return
            s3 = ec % 3
            p.dma(wst[0][s3].rearrange("p (kc j) -> p kc j", kc=Dc),
                  self.dram[wu_name].rearrange("(kc p) o -> p kc o", p=128)[
                      :, :, ec * 128:(ec + 1) * 128],
                  wd3[s3], tag=f"{tag}:ldu{ec}")
            p.dma(wst[1][s3].rearrange("p (kc j) -> p kc j", kc=Dc),
                  self.dram[wg_name].rearrange("(kc p) o -> p kc o", p=128)[
                      :, :, ec * 128:(ec + 1) * 128],
                  wg3[s3], tag=f"{tag}:ldg{ec}")

        tern_marks = {}

        def w_tern(ec):
            if ec >= Ec:
                return
            s3, sl = ec % 3, ec % 2
            n_ = Dc * 128
            p.wait("vector", wd3[s3], tag=f"{tag}:tu{ec}")
            p.wait("vector", wg3[s3], tag=f"{tag}:tg{ec}")
            if ec >= 2:
                p.wait("vector", "t", mm_marks[ec - 2], tag=f"{tag}:wbf{ec}")
            vv = None
            for a, (wb, gcol) in enumerate(((wub, gu), (wgb, gg))):
                p.op("vector", lambda e, s=s3, aa=a, w=n_,
                     rc=self.col(gcol + "_r"):
                     e.tensor_scalar(pos[:, 0:w], wst[aa][s], rc,
                                     self.col("half")[:, 0:1],
                                     ALU.mult, ALU.is_ge),
                     inc="v", tag=f"{tag}:pos{ec},{a}")
                p.op("vector", lambda e, s=s3, aa=a, w=n_,
                     rc=self.col(gcol + "_r"):
                     e.tensor_scalar(neg[:, 0:w], wst[aa][s], rc,
                                     self.col("neghalf")[:, 0:1],
                                     ALU.mult, ALU.is_le),
                     inc="v", tag=f"{tag}:neg{ec},{a}")
                vv = p.op("vector", lambda e, s=sl, w=n_, wb_=wb:
                          e.tensor_tensor(wb_[s][:, 0:w], pos[:, 0:w],
                                          neg[:, 0:w], ALU.subtract),
                          inc="v", tag=f"{tag}:sub{ec},{a}")
            tern_marks[ec] = vv

        mm_marks = {}
        prev_stt = None
        w_dma(0)
        w_dma(1)
        w_tern(0)
        p.wait("tensor", "ad0", tag=f"{tag}:xh-rdy0")
        p.wait("tensor", "ad1", tag=f"{tag}:xh-rdy1")
        for ec in range(Ec):
            sl = ec % 2
            w_dma(ec + 2)
            w_tern(ec + 1)
            p.wait("tensor", "v", tern_marks[ec], tag=f"{tag}:mm{ec}")
            vv = None
            for tt in range(NT):
                g = ec * NT + tt
                PU = self.ps[0] if g % 2 == 0 else self.ps[2]
                PG = self.ps[1] if g % 2 == 0 else self.ps[3]
                if g >= 2:
                    key = ((g - 2) // NT, (g - 2) % NT)
                    if key in mm_marks:
                        chn, v_ = mm_marks[key]
                        p.wait("tensor", chn, v_, tag=f"{tag}:bank{ec},{tt}")
                tg = None
                for kc in range(Dc):
                    p.op("tensor", lambda e, k=kc, s=sl, P=PU, n2=tt,
                         s0=(kc == 0), st=(kc == Dc - 1):
                         e.matmul(P[:, 0:TQ], wub[s][:, k * 128:(k + 1) * 128],
                                  xh[:, k * T + n2 * TQ:k * T + (n2 + 1) * TQ],
                                  start=s0, stop=st),
                         inc="t", tag=f"{tag}:umm{ec},{tt},{kc}")
                for kc in range(Dc):
                    tg = p.op("tensor", lambda e, k=kc, s=sl, P=PG, n2=tt,
                              s0=(kc == 0), st=(kc == Dc - 1):
                              e.matmul(P[:, 0:TQ],
                                       wgb[s][:, k * 128:(k + 1) * 128],
                                       xh[:, k * T + n2 * TQ:
                                          k * T + (n2 + 1) * TQ],
                                       start=s0, stop=st),
                              inc="t", tag=f"{tag}:gmm{ec},{tt},{kc}")
                p.wait("scalar", "t", tg, tag=f"{tag}:act{ec},{tt}")
                if prev_stt is not None:
                    p.wait("scalar", "v", prev_stt,
                           tag=f"{tag}:tsil-war{ec},{tt}")
                if layer == 0:
                    sv = p.op("scalar", lambda e, P=PU:
                              e.activation(tsil, P[:, 0:TQ], AF.Silu,
                                           scale=self.col(gu + "_s")),
                              inc="s", tag=f"{tag}:silu{ec},{tt}")
                    gscol = self.col(gg + "_s")
                else:
                    sv = p.op("scalar", lambda e, P=PU:
                              e.activation(tsil, P[:, 0:TQ], AF.Sigmoid),
                              inc="s", tag=f"{tag}:sig{ec},{tt}")
                    gscol = self.col("one")
                p.wait("vector", "s", sv, tag=f"{tag}:hid{ec},{tt}")
                if tt == 0:
                    p.wait("vector", hw[sl], tag=f"{tag}:hbuf-war{ec}")
                vv = p.op("vector", lambda e, s=sl, n2=tt, P=PG, g_=gscol:
                          e.scalar_tensor_tensor(
                              hbuf[s][:, n2 * TQ:(n2 + 1) * TQ], P[:, 0:TQ],
                              g_, tsil, ALU.mult, ALU.mult),
                          inc="v", tag=f"{tag}:hid{ec},{tt}")
                prev_stt = vv
                mm_marks[(ec, tt)] = ("v", vv)
            mm_marks[ec] = p.chans["t"].n
            p.wait("sync", "v", vv, tag=f"{tag}:sth{ec}")
            p.dma(hid[ec * 128:(ec + 1) * 128, :], hbuf[sl][:, 0:T], hw[sl],
                  tag=f"{tag}:sth{ec}")
        wdn = [self.t(f"Fwd{layer}_{i}", 116 + i * 16, Ec * 128, BF16)
               for i in range(2)]
        fout = self.t("Ffout", 148, Dc * TQ, BF16)
        htile = self.t(f"Fht{layer}", 20, Ec * TQ, BF16)
        p.wait("sync", "tzw0", tag=f"{tag}:cache0")
        p.wait("sync", "tzw1", tag=f"{tag}:cache1")
        p.wait("sync", "hw0", tag=f"{tag}:hid-landed0")
        p.wait("sync", "hw1", tag=f"{tag}:hid-landed1")
        ad = [p.chan("ad0"), p.chan("ad1")]
        ev_marks = {}
        nhb = max(1, Ec // 8)
        for tt in range(NT):
            p.wait("sync", "t", tag=f"{tag}:ht-free{tt}")
            p.wait("sync", "v", tag=f"{tag}:ht-free-v{tt}")
            hmarks = {}
            for chn in ("resw", "ody"):
                if chn in p.chans:
                    p.wait("sync", chn, tag=f"{tag}:res-landed{tt}")
            p.wait("sync", "v", tag=f"{tag}:sta-free{tt}")
            rp = p.chan("rpf")
            src3r = res_src.rearrange("(cc p) t -> p cc t", p=128)[
                :, :, tt * TQ:(tt + 1) * TQ]
            p.dma(sta.rearrange("p (cc t) -> p cc t", cc=Dc), src3r, rp,
                  tag=f"{tag}:ldres{tt}")
            for hb in range(0, Ec, nhb):
                nb = min(nhb, Ec - hb)
                ch = ad[(hb // nhb) % 2]
                p.dma(htile.rearrange("p (ec t) -> p ec t", ec=Ec)[
                          :, hb:hb + nb, :],
                      hid.rearrange("(ec p) t -> p ec t", p=128)[
                          :, hb:hb + nb, tt * TQ:(tt + 1) * TQ],
                      ch, tag=f"{tag}:ldh{tt},{hb}")
                for k in range(hb, hb + nb):
                    hmarks[k] = (ch, ch.n)
            self.bg_region = 164
            p.wait("gpsimd", "t", tag=f"{tag}:bg-gate{tt}")
            for dc in range(Dc):
                sl = dc % 2
                PD = self.ps[4] if sl == 0 else self.ps[5]
                if dc >= 2:
                    chn, v_ = ev_marks[dc - 2]
                    p.wait("sync", chn, v_, tag=f"{tag}:wdfree{tt},{dc}")
                p.dma(wdn[sl].rearrange("p (kc j) -> p kc j", kc=Ec),
                      self.dram[dn_cache].rearrange(
                          "(kc p) o -> p kc o", p=128)[
                          :, :, dc * 128:(dc + 1) * 128],
                      wd[sl], tag=f"{tag}:ldd{tt},{dc}")
                p.wait("tensor", wd[sl], tag=f"{tag}:dmm{tt},{dc}")
                if dc >= 2:
                    chn, v_ = ev_marks[dc - 2]
                    p.wait("tensor", chn, v_, tag=f"{tag}:dbank{tt},{dc}")
                tmk = None
                for ec in range(Ec):
                    if dc == 0:
                        chh, vh2 = hmarks[ec]
                        p.wait("tensor", chh, vh2,
                               tag=f"{tag}:htw{tt},{ec}")
                    tmk = p.op("tensor", lambda e, k=ec, s=sl, P=PD,
                               s0=(ec == 0), st=(ec == Ec - 1):
                               e.matmul(P[:, 0:TQ],
                                        wdn[s][:, k * 128:(k + 1) * 128],
                                        htile[:, k * TQ:(k + 1) * TQ],
                                        start=s0, stop=st),
                               inc="t", tag=f"{tag}:dmm{tt},{dc},{ec}")
                p.wait("scalar", "t", tmk, tag=f"{tag}:dev{tt},{dc}")
                if layer == 0:
                    sv = p.op("scalar", lambda e, d=dc, P=PD:
                              e.activation(fout[:, d * TQ:(d + 1) * TQ],
                                           P[:, 0:TQ], AF.Identity,
                                           scale=self.col(gd_ + "_s")),
                              inc="s", tag=f"{tag}:dev{tt},{dc}")
                else:
                    sv = p.op("scalar", lambda e, d=dc, P=PD:
                              e.activation(fout[:, d * TQ:(d + 1) * TQ],
                                           P[:, 0:TQ], AF.Copy),
                              inc="s", tag=f"{tag}:dev{tt},{dc}")
                ev_marks[dc] = ("s", sv)
                self.bg_step(3, allow_collapse=True)
            self.bg_region = 170
            for chn in ("g0", "g1", "g2", "g3", "tzw0", "tzw1"):
                if chn in p.chans:
                    p.wait("vector", chn, tag=f"{tag}:bg-quiet{tt}")
                    p.wait("scalar", chn, tag=f"{tag}:bg-quiet-s{tt}")
            sev = p.chans["s"].n
            p.wait("vector", "s", sev, tag=f"{tag}:ln4{tt}")
            p.wait("vector", rp, tag=f"{tag}:sta-rdy{tt}")
            xb = self.t("Fxbuf", 180, Dc * TQ, BF16) if layer == 0 else None
            self.ln_post_add(fout, TQ, tt, res_src, res_dst, out_ch,
                             f"{tag}:pa{tt}", sta_ready=sta, make_xhat=xb)
        return out_ch

    def phase_L1A(self):
        p, c = self.p, self.c
        TQ, NT, Dc, T = c["TQ"], c["NT"], c["Dc"], c["T"]
        NKT, KVD = c["NKT"], c["KVD"]
        xhat = self.t("xh1", 20, Dc * T, BF16)
        p.wait("sync", "xhw", tag="1A:xh-landed")
        xld = [p.chan("ad0"), p.chan("ad1")]
        for tt in range(NT):
            p.wait("sync", "t", tag=f"1A:xh-free{tt}")
            p.dma(xhat.rearrange("p (cc t) -> p cc t", cc=Dc)[
                      :, :, tt * TQ:(tt + 1) * TQ],
                  self.dram["xh8"].rearrange("(cc p) t -> p cc t", p=128)[
                      :, :, tt * TQ:(tt + 1) * TQ],
                  xld[tt % 2], tag=f"1A:ldxh{tt}")
        p.wait("tensor", "ad0", tag="1A:xh-rdy0")
        p.wait("tensor", "ad1", tag="1A:xh-rdy1")
        wst = self.t("1Awst", 148, Dc * KVD, F32)
        wkb = self.t("1Awkb", 180, Dc * KVD, BF16)
        wd = p.chan("wd0")
        p.wait("sync", "t", tag="1A:fresh")
        p.wait("sync", "v", tag="1A:fresh-v")
        for chn in ("g0", "g1"):
            if chn in p.chans:
                p.wait("sync", chn, tag="1A:bg-quiet")
        p.dma(wst.rearrange("p (kc o) -> p kc o", kc=Dc),
              self.dram["wk1"].rearrange("(kc p) o -> p kc o", p=128),
              wd, tag="1A:ldk")
        p.wait("vector", wd, tag="1A:castk")
        vk = p.op("vector", lambda e: e.tensor_copy(wkb, wst), inc="v",
                  tag="1A:castk")
        p.wait("tensor", "v", vk, tag="1A:kmm")
        k1 = self.t("k1", 116, c["KVG"] * T, BF16)
        v1 = self.t("v1", 132, NKT * KVD, BF16)
        tmpf = self.t("tmpf", 200, 512, F32)
        for g in range(c["KVG"]):
            for n in range(NT):
                b = self.ps[n % 2]
                tmk = None
                for kc in range(Dc):
                    tmk = p.op("tensor", lambda e, k=kc, nn=n, gg=g, P=b,
                               s0=(kc == 0), st=(kc == Dc - 1):
                               e.matmul(P[:, 0:TQ],
                                        wkb[:, k * KVD + gg * 128:
                                            k * KVD + (gg + 1) * 128],
                                        xhat[:, k * T + nn * TQ:
                                             k * T + (nn + 1) * TQ],
                                        start=s0, stop=st),
                               inc="t", tag=f"1A:kmm{g},{n},{kc}")
                p.wait("vector", "t", tmk, tag=f"1A:kev{g},{n}")
                vv = p.op("vector", lambda e, gg=g, nn=n, P=b:
                          e.tensor_copy(k1[:, gg * T + nn * TQ:
                                           gg * T + (nn + 1) * TQ],
                                        P[:, 0:TQ]),
                          inc="v", tag=f"1A:kev{g},{n}")
                p.wait("tensor", "v", vv, tag=f"1A:kb{g},{n}")
        vlast = p.chans["v"].n
        p.wait("sync", "v", vlast, tag="1A:wst-free2")
        p.dma(wst.rearrange("p (kc o) -> p kc o", kc=Dc),
              self.dram["wv1"].rearrange("(kc p) o -> p kc o", p=128),
              wd, tag="1A:ldv")
        p.wait("vector", wd, tag="1A:castv")
        p.wait("vector", "t", tag="1A:castv-t")
        vk = p.op("vector", lambda e: e.tensor_copy(wkb, wst), inc="v",
                  tag="1A:castv")
        p.wait("tensor", "v", vk, tag="1A:vmm")
        VW = min(TQ, KVD)
        nV = KVD // VW
        for ttk in range(NKT):
            for n in range(nV):
                b = self.ps[n % 2]
                tmk = None
                for kc in range(Dc):
                    tmk = p.op("tensor", lambda e, k=kc, nn=n, t2=ttk, P=b,
                               s0=(kc == 0), st=(kc == Dc - 1):
                               e.matmul(P[:, 0:VW],
                                        xhat[:, k * T + t2 * 128:
                                             k * T + (t2 + 1) * 128],
                                        wkb[:, k * KVD + nn * VW:
                                            k * KVD + (nn + 1) * VW],
                                        start=s0, stop=st),
                               inc="t", tag=f"1A:vmm{ttk},{n},{kc}")
                p.wait("vector", "t", tmk, tag=f"1A:vev{ttk},{n}")
                vv = p.op("vector", lambda e, nn=n, t2=ttk, P=b, w=VW:
                          e.tensor_copy(v1[:, t2 * KVD + nn * w:
                                           t2 * KVD + (nn + 1) * w],
                                        P[:, 0:w]),
                          inc="v", tag=f"1A:vev{ttk},{n}")
                p.wait("tensor", "v", vv, tag=f"1A:vb{ttk},{n}")

    def phase_L1C(self):
        p, c = self.p, self.c
        TQ, NT, T, Dc = c["TQ"], c["NT"], c["T"], c["Dc"]
        NH, NKT, NOFF, KVD = c["NH"], c["NKT"], c["NOFF"], c["KVD"]
        xhat = self.t("xh1", 20, Dc * T, BF16)
        k1 = self.t("k1", 116, c["KVG"] * T, BF16)
        v1 = self.t("v1", 132, NKT * KVD, BF16)
        qh = self.t("1Cqh", 108, T, BF16)
        ctxh = self.t("1Cctxh", 112, T, BF16)
        pT = [self.t(f"1CpT{i}", 148 + i * 2, 2 * TQ, BF16) for i in range(2)]
        rbf = self.t("1Crbf", 152, TQ, F32)
        wstq = [self.t(f"1Cwstq{i}", 84 + i * (Dc * 128 * 4) / KB,
                       Dc * 128, F32) for i in range(2)]
        wqb = [self.t(f"1Cwqb{i}", 100 + i * (Dc * 128 * 2) / KB,
                      Dc * 128, BF16) for i in range(2)]
        statr = self.t_perm("statr")
        masks = self.t_perm("masks")
        wd = [p.chan("wd0"), p.chan("wd1")]
        abw = p.chan("abw")
        for chn in ("resw", "ody", "rpf"):
            if chn in p.chans:
                p.wait("sync", chn, tag="1C:res-quiet")
        mh_col = self.col("mh1")
        self._smark = {}
        self._rbe_prev = None
        self._ctx_marks = {}
        cast_marks = {}
        for h in range(NH):
            g = h // (NH // c["KVG"])
            sl = h % 2
            if h < 2:
                p.wait("sync", "t", tag=f"1C:fresh{h}")
                p.wait("sync", "v", tag=f"1C:fresh-v{h}")
                p.wait("sync", "s", tag=f"1C:fresh-s{h}")
            else:
                p.wait("sync", "s", cast_marks[h - 2], tag=f"1C:wstfree{h}")
            src3 = self.dram["wq1"].rearrange("(kc p) o -> p kc o", p=128)[
                :, :, h * 128:(h + 1) * 128]
            p.dma(wstq[sl].rearrange("p (kc j) -> p kc j", kc=Dc), src3,
                  wd[sl], tag=f"1C:wldq{h}")
            self.bg_step(1, allow_collapse=True)
            p.wait("scalar", wd[sl], tag=f"1C:wcast{h}")
            cast_marks[h] = p.op(
                "scalar", lambda e, s=sl: e.copy(wqb[s], wstq[s]),
                inc="s", tag=f"1C:wcast{h}")
            p.wait("tensor", "s", cast_marks[h], tag=f"1C:q-wait{h}")
            PQm = self.ps[0]
            for n in range(NT):
                tmk = None
                for kc in range(Dc):
                    tmk = p.op("tensor", lambda e, k=kc, nn=n, s=sl,
                               s0=(kc == 0), st=(kc == Dc - 1):
                               e.matmul(PQm[:, 0:TQ],
                                        wqb[s][:, k * 128:(k + 1) * 128],
                                        xhat[:, k * T + nn * TQ:
                                             k * T + (nn + 1) * TQ],
                                        start=s0, stop=st),
                               inc="t", tag=f"1C:qmm{h},{n},{kc}")
                p.wait("vector", "t", tmk, tag=f"1C:qev{h},{n}")
                vv = p.op("vector", lambda e, nn=n:
                          e.tensor_copy(qh[:, nn * TQ:(nn + 1) * TQ],
                                        PQm[:, 0:TQ]),
                          inc="v", tag=f"1C:qev{h},{n}")
                p.wait("tensor", "v", vv, tag=f"1C:qb{h},{n}")
            p.wait("vector", "abw", tag=f"1C:ctx-war{h}")
            self.head_bound(qh[:, 0:T], k1[:, g * T:g * T + 128], mh_col,
                            f"1C:hb{h}")
            SP = [self.ps["A"], self.ps["B"]]
            for qc in range(NT):
                PC = self.ps[4] if qc % 2 == 0 else self.ps[5]
                PS6 = self.ps[6]
                nkt = (qc + 1) * NOFF
                npair = nkt // 2
                exp_marks = {}

                def smm(pi2, qv=qc, nk=nkt, np_=npair):
                    if pi2 >= np_:
                        return
                    if pi2 >= 2:
                        chn, vv_ = exp_marks[pi2 - 2]
                        p.wait("tensor", chn, vv_,
                               tag=f"1C:Sfree{h},{qv},{pi2}")
                    vm = None
                    mk = None
                    for half in range(2):
                        kt = 2 * pi2 + half
                        mk = p.op("tensor", lambda e, k=kt, q=qv,
                                  bb=pi2 % 2, gg=g, hh=half:
                                  e.matmul(SP[bb][:, hh * TQ:(hh + 1) * TQ],
                                           k1[:, gg * T + k * 128:
                                              gg * T + (k + 1) * 128],
                                           qh[:, q * TQ:(q + 1) * TQ],
                                           start=True, stop=True),
                                  inc="t", tag=f"1C:smm{h},{qv},{kt}")
                        diag = kt - qv * NOFF
                        if diag >= 0:
                            p.wait("vector", "t", mk,
                                   tag=f"1C:mask{h},{qv},{kt}")
                            vm = p.op("vector", lambda e, bb=pi2 % 2,
                                      dd=diag, hh=half:
                                      e.tensor_tensor(
                                          SP[bb][:, hh * TQ:(hh + 1) * TQ],
                                          SP[bb][:, hh * TQ:(hh + 1) * TQ],
                                          masks[:, dd * TQ:(dd + 1) * TQ],
                                          ALU.add),
                                      inc="v", tag=f"1C:mask{h},{qv},{kt}")
                    self._smark[(1, h, qv, pi2)] = (mk, vm)

                smm(0)
                for pi in range(npair):
                    b = pi % 2
                    smm(pi + 1)
                    mk, vm = self._smark[(1, h, qc, pi)]
                    if vm is not None:
                        p.wait("scalar", "v", vm, tag=f"1C:expm{h},{qc},{pi}")
                    p.wait("scalar", "t", mk, tag=f"1C:exp{h},{qc},{pi}")
                    sv = p.op("scalar", lambda e, bb=b, pp=pi:
                              e.activation(pT[pp % 2], SP[bb][:, 0:2 * TQ],
                                           AF.Exp, bias=mh_col,
                                           scale=self.col("sm")),
                              inc="s", tag=f"1C:exp{h},{qc},{pi}")
                    exp_marks[pi] = ("s", sv)
                    p.wait("tensor", "s", sv, tag=f"1C:pv{h},{qc},{pi}")
                    if pi == 0 and self._rbe_prev is not None:
                        p.wait("tensor", "s", self._rbe_prev,
                               tag=f"1C:sumfree{h},{qc}")
                    if pi == 0 and self._ctx_marks.get(qc % 2) is not None:
                        p.wait("tensor", "v", self._ctx_marks[qc % 2],
                               tag=f"1C:pcfree{h},{qc}")
                    for half in range(2):
                        kt = 2 * pi + half
                        p.op("tensor", lambda e, k=kt, gg=g, pp=pi, hh=half,
                             P=PC, s0=(kt == 0), st=(kt == nkt - 1):
                             e.matmul(P[:, 0:TQ],
                                      v1[:, k * KVD + gg * 128:
                                         k * KVD + (gg + 1) * 128],
                                      pT[pp % 2][:, hh * TQ:(hh + 1) * TQ],
                                      start=s0, stop=st),
                             inc="t", tag=f"1C:pv{h},{qc},{kt}")
                        tsum = p.op("tensor", lambda e, pp=pi, hh=half,
                                    s0=(kt == 0), st=(kt == nkt - 1):
                                    e.matmul(PS6[0:1, 0:TQ],
                                             self.t_perm("onesb")[:, 0:1],
                                             pT[pp % 2][:, hh * TQ:(hh + 1) * TQ],
                                             start=s0, stop=st),
                                    inc="t", tag=f"1C:sum{h},{qc},{kt}")
                    if pi == 1 and qc >= 1:
                        self.bg_step(12)
                p.wait("vector", "t", tsum, tag=f"1C:recip{h},{qc}")
                srow = statr[0:1, 0:TQ]
                vv = p.op("vector", lambda e: e.reciprocal(
                    srow, PS6[0:1, 0:TQ]), inc="v", tag=f"1C:recip{h},{qc}")
                p.wait("tensor", "v", vv, tag=f"1C:rb{h},{qc}")
                tb = p.op("tensor", lambda e: e.matmul(
                    PS6[:, 0:TQ], self.t_perm("onesf")[0:1, 0:128], srow,
                    start=True, stop=True), inc="t", tag=f"1C:rb{h},{qc}")
                p.wait("scalar", "t", tb, tag=f"1C:rbe{h},{qc}")
                self._rbe_prev = p.op("scalar", lambda e: e.activation(
                    rbf, PS6[:, 0:TQ], AF.Copy), inc="s",
                    tag=f"1C:rbe{h},{qc}")
                p.wait("vector", "s", self._rbe_prev, tag=f"1C:ctx{h},{qc}")
                self._ctx_marks[qc % 2] = p.op(
                    "vector", lambda e, q=qc, P=PC: e.tensor_tensor(
                        ctxh[:, q * TQ:(q + 1) * TQ], P[:, 0:TQ], rbf,
                        ALU.mult),
                    inc="v", tag=f"1C:ctx{h},{qc}")
            p.wait("sync", "v", tag=f"1C:stab{h}")
            p.dma(self.dram["ab"][h * 128:(h + 1) * 128, :], ctxh[:, 0:T],
                  abw, tag=f"1C:stab{h}")

    def barrier(self, name):
        p = self.p
        ph = p.chan("ph")
        for e in ("tensor", "scalar", "sync", "gpsimd"):
            p.op(e, lambda eng: eng.nop(), inc=ph, tag=f"bar:{name}:{e}")
        p.op("vector", lambda eng: eng.engine_nop(), inc=ph,
             tag=f"bar:{name}:vector")
        tgt = ph.n
        for e in ("tensor", "vector", "scalar", "sync", "gpsimd"):
            p.wait(e, ph, tgt, tag=f"bar:{name}")

    def build(self, upto=None):
        self.declare()
        self.bg = []
        steps = [
            ("prelude", lambda: (self.prelude(), self.bg_enqueue())),
            ("A", self.phase_A),
            ("B", self.phase_B),
            ("C", self.phase_C),
            ("D0", lambda: self.phase_D("wo0", self.b_o0(), self.dram["x"],
                                        self.dram["res"], "resw", "D0")),
            ("T0fin", lambda: self.bg_finish(7)),
            ("F0", lambda: self.ffn(0, self.dram["res"], self.dram["res"],
                                    "resw", "F0")),
            ("L1A", self.phase_L1A),
            ("L1C", self.phase_L1C),
            ("D1", lambda: self.phase_D("wo1", self.b_o1(), self.dram["res"],
                                        self.dram["res"], "resw", "D1")),
            ("T1fin", lambda: self.bg_finish(0)),
            ("F1", lambda: self.ffn(1, self.dram["res"], self.dram["y"],
                                    "ody", "F1")),
        ]
        for name, fn in steps:
            fn()
            self.barrier(name)
            if upto is not None and name == upto:
                break
        for chn in ("resw", "kvw", "abw", "tzw0", "tzw1", "ody"):
            if chn in self.p.chans:
                self.p.wait("sync", chn, tag=f"final:{chn}")
        self.p.emit()
        return self.nc


def build_nc(cfg, upto=None, debug_outputs=()):
    mk = MK(cfg, debug_outputs)
    nc = mk.build(upto)
    mk.stack.close()
    return nc, mk



def kernel(**inp):
    cfg = FULL
    c = derive(cfg)
    nc, _ = build_nc(cfg)
    x = np.asarray(inp["L0_attn_x"], np.float32)
    common = host_inputs(c, inp)
    in_maps = []
    for b in range(8):
        m = dict(common)
        m["x"] = np.ascontiguousarray(x[b].T)
        in_maps.append(m)
    res = run_bass_kernel_spmd(nc, in_maps, core_ids=list(range(8)))
    out = np.stack([np.asarray(r["y"], np.float32).T for r in res.results], 0)
    return np.ascontiguousarray(out)


# version 14 (final)
# speedup vs baseline: 1.0369x; 1.0015x over previous
import numpy as np
import ml_dtypes
from contextlib import ExitStack

import concourse.bass as bass
import concourse.mybir as mybir
from concourse.bass_utils import run_bass_kernel_spmd

F32 = mybir.dt.float32
BF16 = mybir.dt.bfloat16
I8 = mybir.dt.int8
U8 = mybir.dt.uint8
AF = mybir.ActivationFunctionType
ALU = mybir.AluOpType
AX = mybir.AxisListType
BF = ml_dtypes.bfloat16

KB = 1024

FULL = dict(D=2048, T=2048, NH=16, HD=128, DQ=1536, DKV=512,
            E0=8192, E1=5632, KVG=4, TQ=512)
MINI = dict(D=512, T=512, NH=4, HD=128, DQ=384, DKV=128,
            E0=1024, E1=768, KVG=2, TQ=512)


def derive(cfg):
    c = dict(cfg)
    c["Dc"] = c["D"] // 128
    c["DQc"] = c["DQ"] // 128
    c["DKVc"] = c["DKV"] // 128
    c["E0c"] = c["E0"] // 128
    c["E1c"] = c["E1"] // 128
    c["NT"] = c["T"] // c["TQ"]
    c["NKT"] = c["T"] // 128
    c["KVD"] = c["KVG"] * c["HD"]
    c["KVDc"] = c["KVD"] // 128
    c["T2"] = c["T"] // 2
    c["NOFF"] = c["TQ"] // c["HD"]
    return c



def rope_tables(D, T):
    exps = np.arange(0, D, 2, dtype=np.float32)
    with np.errstate(over="ignore"):
        tn = np.float32(1.0) / (np.float32(10000.0) ** exps / np.float32(D))
    m = np.arange(T, dtype=np.float32)
    ang = np.outer(m, tn).astype(np.float32)
    cos = np.cos(ang).astype(np.float32)
    sin = np.sin(ang).astype(np.float32)
    d = np.arange(D)
    return (np.ascontiguousarray(cos[:, d // 2].T),
            np.ascontiguousarray(sin[:, d // 2].T))


def diag_masks(NOFF, TQ):
    k = np.arange(128)[:, None]
    q = np.arange(TQ)[None, :]
    out = np.zeros((128, NOFF * TQ), np.float32)
    for o in range(NOFF):
        out[:, o * TQ:(o + 1) * TQ] = np.where(o * 128 + k > q, -1.0e30, 0.0)
    return out.astype(BF)


def rope_perm():
    P = np.zeros((128, 128), np.float32)
    i = np.arange(64)
    P[2 * i, 2 * i + 1] = -1.0
    P[2 * i + 1, 2 * i] = 1.0
    return np.ascontiguousarray(P.T).astype(BF)


def pack_bias(b):
    b = np.asarray(b, np.float32)
    return np.ascontiguousarray(b.reshape(-1, 128).T)


def host_inputs(c, inp):
    t = lambda a: np.ascontiguousarray(np.asarray(a, np.float32).T)
    m = {}
    m["wdq"] = t(inp["L0_attn_w_down_q"])
    m["wuq"] = t(inp["L0_attn_w_up_q"])
    m["wqr"] = t(inp["L0_attn_wq_r"])
    m["wdkv"] = t(inp["L0_attn_w_down_kv"])
    m["wuk"] = t(inp["L0_attn_w_up_k"])
    m["wuv"] = t(inp["L0_attn_w_up_v"])
    m["wkr"] = t(inp["L0_attn_wk_r"])
    m["wo0"] = t(inp["L0_attn_wo"])
    m["bias"] = np.concatenate([
        pack_bias(inp["L0_attn_b_down_q"]),
        pack_bias(inp["L0_attn_b_up_q"]),
        pack_bias(inp["L0_attn_bq_r"]),
        pack_bias(inp["L0_attn_b_down_kv"]),
        pack_bias(inp["L0_attn_b_up_k"]),
        pack_bias(inp["L0_attn_bk_r"]),
        pack_bias(inp["L0_attn_bo"]),
        pack_bias(inp["L1_attn_bo"]),
    ], axis=1)
    m["bvrow"] = np.ascontiguousarray(
        np.asarray(inp["L0_attn_b_up_v"], np.float32)[None, :])
    m["wup"] = t(inp["L0_ffn_W_up"])
    m["wgt"] = t(inp["L0_ffn_W_gate"])
    m["wdn"] = t(inp["L0_ffn_W_down"])
    m["wq1"] = np.ascontiguousarray(np.asarray(inp["L1_attn_Wq"], np.float32))
    m["wk1"] = np.ascontiguousarray(np.asarray(inp["L1_attn_Wk"], np.float32))
    m["wv1"] = np.ascontiguousarray(np.asarray(inp["L1_attn_Wv"], np.float32))
    m["wo1"] = np.ascontiguousarray(np.asarray(inp["L1_attn_Wo"], np.float32))
    m["ww1"] = t(inp["L1_ffn_w1"])
    m["ww2"] = t(inp["L1_ffn_w2"])
    m["ww3"] = t(inp["L1_ffn_w3"])
    cexp, sexp = rope_tables(c["D"], c["T"])
    m["cexp"] = cexp.astype(BF)
    m["sexp"] = sexp.astype(BF)
    m["masks"] = diag_masks(c["NOFF"], c["TQ"])
    m["identb"] = np.eye(128, dtype=np.float32).astype(BF)
    m["identf"] = np.eye(128, dtype=np.float32)
    m["ptrope"] = rope_perm()
    return m



ENGINES = ("sync", "tensor", "vector", "scalar", "gpsimd")


class Chan:
    __slots__ = ("name", "sem", "n")

    def __init__(self, name, sem):
        self.name = name
        self.sem = sem
        self.n = 0


class Prog:
    def __init__(self, nc, stack):
        self.nc = nc
        self.stack = stack
        self.ops = {e: [] for e in ENGINES}
        self.chans = {}

    def chan(self, name):
        ch = self.chans.get(name)
        if ch is None:
            sem = self.stack.enter_context(self.nc.semaphore(name))
            ch = Chan(name, sem)
            self.chans[name] = ch
        return ch

    def wait(self, engine, ch, val=None, tag=""):
        if isinstance(ch, str):
            ch = self.chan(ch)
        v = ch.n if val is None else val
        if v <= 0:
            return 0
        self.ops[engine].append(("wait", ch, v, tag))
        return v

    def op(self, engine, fn, inc=None, inc_n=1, tag=""):
        if isinstance(inc, str):
            inc = self.chan(inc)
        self.ops[engine].append(("op", fn, inc, inc_n, tag))
        if inc is not None:
            inc.n += inc_n
        return inc.n if inc is not None else None

    def dma(self, out, in_, ch, q="sync", tag=""):
        if isinstance(ch, str):
            ch = self.chan(ch)
        if ch.n >= 16:
            self.wait(q, ch, ch.n, tag=tag + ":pace")
        return self.op(q, lambda eng, o=out, i=in_: eng.dma_start(o, i),
                       inc=ch, inc_n=16, tag=tag)

    def check(self):
        vals = {name: 0 for name in self.chans}
        ptr = {e: 0 for e in ENGINES}
        total = sum(len(v) for v in self.ops.values())
        done = 0
        progress = True
        while progress:
            progress = False
            for e in ENGINES:
                while ptr[e] < len(self.ops[e]):
                    item = self.ops[e][ptr[e]]
                    if item[0] == "wait":
                        _, ch, v, tag = item
                        if vals[ch.name] < v:
                            break
                    else:
                        _, fn, inc, inc_n, tag = item
                        if inc is not None:
                            vals[inc.name] += inc_n
                    ptr[e] += 1
                    done += 1
                    progress = True
        if done != total:
            lines = []
            for e in ENGINES:
                if ptr[e] < len(self.ops[e]):
                    item = self.ops[e][ptr[e]]
                    if item[0] == "wait":
                        _, ch, v, tag = item
                        lines.append(f"{e}[{ptr[e]}] waits {ch.name}>={v} "
                                     f"(have {vals[ch.name]}) tag={tag}")
            raise RuntimeError("deadlock in program build:\n" + "\n".join(lines))

    def emit(self):
        self.check()
        nc = self.nc
        own = {"vector": "v", "scalar": "s", "gpsimd": "g"}
        with nc.Block() as blk:
            def runner(name):
                def _run(eng):
                    och = self.chans.get(own.get(name, ""), None)
                    cnt = 0
                    for item in self.ops[name]:
                        if item[0] == "wait":
                            _, ch, v, tag = item
                            eng.wait_ge(ch.sem, v)
                        else:
                            _, fn, inc, inc_n, tag = item
                            if och is not None and cnt > 0:
                                eng.wait_ge(och.sem, cnt)
                            ins = fn(eng)
                            if inc is not None:
                                ins.then_inc(inc.sem, inc_n)
                            if inc is och:
                                cnt += inc_n
                    return
                return _run
            for name, deco in (("sync", blk.sync), ("tensor", blk.tensor),
                               ("vector", blk.vector), ("scalar", blk.scalar),
                               ("gpsimd", blk.gpsimd)):
                if self.ops[name]:
                    deco(runner(name))


class Arena:
    def __init__(self, nc, stack, size_bytes):
        self.nc = nc
        self.size = size_bytes
        self.back = stack.enter_context(
            nc.sbuf_tensor("arena", [128, size_bytes], U8))
        self.base = nc.lookup_mloc(self.back).addr
        self.names = {}

    def at(self, name, off_bytes, shape, dtype):
        key = (name, off_bytes)
        if key in self.names:
            return self.names[key]
        per_part = int(np.prod(shape[1:])) * mybir.dt.size(dtype)
        assert off_bytes + per_part <= self.size, \
            f"{name}: {off_bytes}+{per_part} > {self.size}"
        name = "sb_" + name
        hnd = self.nc.virtual_sbuf_tensor(name, shape, dtype)
        self.nc._tensor(hnd.name, shape, dtype, type="SB")
        mloc = self.nc.lookup_mloc(hnd)
        mloc.allocated = True
        mloc.addr = self.base + off_bytes
        self.names[key] = hnd
        return hnd



class MK:
    def __init__(self, cfg, debug_outputs=()):
        self.c = derive(cfg)
        self.stack = ExitStack()
        self.nc = bass.Bass("TRN2", target_bir_lowering=False)
        self.p = Prog(self.nc, self.stack)
        self.debug_outputs = set(debug_outputs)
        self.dram = {}
        self.sb = None
        self.ps = {}
        self.cols = {}
        self.ncols = 0

    def dram_in(self, name, shape, dtype=F32):
        h = self.nc.dram_tensor(name, list(shape), dtype, kind="ExternalInput")
        self.dram[name] = h.ap()

    def dram_tmp(self, name, shape, dtype=F32):
        kind = "ExternalOutput" if name in self.debug_outputs else "Internal"
        h = self.nc.dram_tensor(name, list(shape), dtype, kind=kind)
        self.dram[name] = h.ap()

    def dram_out(self, name, shape, dtype=F32):
        h = self.nc.dram_tensor(name, list(shape), dtype, kind="ExternalOutput")
        self.dram[name] = h.ap()

    def declare(self):
        c = self.c
        T = c["T"]
        self.dram_in("x", [c["D"], T])
        for n, s in [("wdq", [c["D"], c["DQ"]]), ("wuq", [c["DQ"], c["D"]]),
                     ("wqr", [c["DQ"], c["D"]]), ("wdkv", [c["D"], c["DKV"]]),
                     ("wuk", [c["DKV"], c["D"]]), ("wuv", [c["DKV"], c["D"]]),
                     ("wkr", [c["DKV"], c["D"]]), ("wo0", [c["D"], c["D"]]),
                     ("wup", [c["D"], c["E0"]]), ("wgt", [c["D"], c["E0"]]),
                     ("wdn", [c["E0"], c["D"]]),
                     ("wq1", [c["D"], c["D"]]), ("wk1", [c["D"], c["KVD"]]),
                     ("wv1", [c["D"], c["KVD"]]), ("wo1", [c["D"], c["D"]]),
                     ("ww1", [c["D"], c["E1"]]), ("ww2", [c["D"], c["E1"]]),
                     ("ww3", [c["E1"], c["D"]]),
                     ("identf", [128, 128]),
                     ("bvrow", [1, c["D"]])]:
            self.dram_in(n, s)
        nbias = c["DQc"] + 6 * c["Dc"] + c["DKVc"]
        self.dram_in("bias", [128, nbias])
        self.dram_in("masks", [128, c["NOFF"] * c["TQ"]], BF16)
        self.dram_in("cexp", [c["D"], T], BF16)
        self.dram_in("sexp", [c["D"], T], BF16)
        for n in ("identb", "ptrope"):
            self.dram_in(n, [128, 128], BF16)
        self.dram_tmp("res", [c["D"], T])
        self.dram_tmp("k0", [c["D"], T], BF16)
        self.dram_tmp("v0", [T, c["D"]], BF16)
        self.dram_tmp("ab", [c["D"], T], BF16)
        self.dram_tmp("hd0", [c["E0"], T], BF16)
        self.dram_tmp("xh8", [c["D"], T], BF16)
        self.dram_tmp("tdn", [c["E0"], c["D"]], BF16)
        self.dram_tmp("tw3", [c["E1"], c["D"]], BF16)
        self.dram_out("y", [c["D"], T])

        self.sb = Arena(self.nc, self.stack, 204 * KB)
        pa = self.stack.enter_context(self.nc.psum_tensor("PA", [128, 1024], F32))
        pb = self.stack.enter_context(self.nc.psum_tensor("PB", [128, 1024], F32))
        self.ps["A"] = pa[:]
        self.ps["B"] = pb[:]
        self.ps[0] = pa[:][:, 0:512]
        self.ps[1] = pa[:][:, 512:1024]
        self.ps[2] = pb[:][:, 0:512]
        self.ps[3] = pb[:][:, 512:1024]
        for i in range(4, 7):
            self.ps[i] = self.stack.enter_context(
                self.nc.psum_tensor(f"P{i}", [128, 512], F32))[:]
        self.ps["bf"] = self.stack.enter_context(
            self.nc.psum_tensor("PBF", [128, 1024], BF16))[:]

    def t_perm(self, name):
        c = self.c
        lay = {
            "identb": (0, [128, 128], BF16),
            "ptrope": (256, [128, 128], BF16),
            "identf": (512, [128, 128], F32),
            "onesb":  (1024, [128, 128], BF16),
            "onesf":  (1280, [128, 128], F32),
            "cols":   (1792, [128, 64], F32),
            "biast":  (2048, [128, 128], F32),
            "statr":  (2560, [128, 1024], F32),
            "masks":  (6656, [128, c["NOFF"] * c["TQ"]], BF16),
            "bvb":    (10752, [128, c["D"]], F32),
            "kcast":  (18944, [128, 128], BF16),
        }
        off, shape, dt = lay[name]
        return self.sb.at(name, off, shape, dt)[:]

    def col(self, name):
        if name not in self.cols:
            self.cols[name] = self.ncols
            self.ncols += 1
            assert self.ncols <= 64
        i = self.cols[name]
        return self.t_perm("cols")[:, i:i + 1]

    def t(self, name, off_kb, free_elems, dtype):
        return self.sb.at(name, int(off_kb * KB), [128, free_elems], dtype)[:]

    def memset_col(self, name, val):
        cl = self.col(name)
        self.p.op("vector", lambda e, a=cl, v=float(val): e.memset(a, v),
                  inc="v", tag=f"memset:{name}")

    def bias_col(self, idx):
        return self.t_perm("biast")[:, idx:idx + 1]

    def b_dq(self):  return 0
    def b_uq(self):  return self.c["DQc"]
    def b_qr(self):  return self.c["DQc"] + self.c["Dc"]
    def b_dkv(self): return self.c["DQc"] + 2 * self.c["Dc"]
    def b_uk(self):  return self.c["DQc"] + 2 * self.c["Dc"] + self.c["DKVc"]
    def b_kr(self):  return self.c["DQc"] + 3 * self.c["Dc"] + self.c["DKVc"]
    def b_o0(self):  return self.c["DQc"] + 4 * self.c["Dc"] + self.c["DKVc"]
    def b_o1(self):  return self.c["DQc"] + 5 * self.c["Dc"] + self.c["DKVc"]

    def prelude(self):
        p, c = self.p, self.c
        ad = p.chan("ad0")
        for n in ("identb", "ptrope", "identf", "masks"):
            p.dma(self.t_perm(n), self.dram[n], ad, tag=f"ld:{n}")
        nbias = c["DQc"] + 6 * c["Dc"] + c["DKVc"]
        p.dma(self.t_perm("biast")[:, 0:nbias], self.dram["bias"], ad,
              tag="ld:bias")
        p.op("vector", lambda e: e.memset(self.t_perm("onesb"), 1.0), inc="v")
        p.op("vector", lambda e: e.memset(self.t_perm("onesf"), 1.0), inc="v")
        for nm, v in [("one", 1.0), ("neg1", -1.0), ("invD", 1.0 / c["D"]),
                      ("eps", 1e-5), ("neghalf", -0.5),
                      ("sm", 1.0 / np.sqrt(c["HD"])),
                      ("half", 0.5),
                      ("negsm", -1.0 / np.sqrt(c["HD"])),
                      ("invE0", 1.0 / (c["D"] * c["E0"])),
                      ("invE1", 1.0 / (c["D"] * c["E1"])),
                      ("eps8", 1e-8)]:
            self.memset_col(nm, v)
        for e in ("tensor", "vector", "scalar"):
            p.wait(e, ad, tag="prelude:consts")
            p.wait(e, "v", tag="prelude:memsets")

    def ln_in(self, src, tt, xhat, xhat_stride, sta, tag):
        p, c = self.p, self.c
        Dc, TQ = c["Dc"], c["TQ"]
        ad = p.chan("ad0")
        for chn in ("resw", "ody"):
            if chn in p.chans:
                p.wait("sync", chn, tag=f"{tag}:res-landed")
        p.wait("sync", "v", tag=f"{tag}:sta-free")
        src3 = src.rearrange("(cc p) t -> p cc t", p=128)[
            :, :, tt * TQ:(tt + 1) * TQ]
        dst3 = sta.rearrange("p (cc t) -> p cc t", cc=Dc)
        p.dma(dst3, src3, ad, tag=f"{tag}:ld")
        p.wait("vector", ad, tag=f"{tag}:v-sta")
        p.wait("scalar", ad, tag=f"{tag}:s-sta")
        rawc = [self.t("rawc0", 196, 512, BF16), self.t("rawc1", 197, 512, BF16)]
        sqc = [self.t("sqc0", 198, 512, BF16), self.t("sqc1", 199, 512, BF16)]
        statr = self.t_perm("statr")
        mrow = statr[0:1, 0:TQ]
        vrow = statr[32:33, 0:TQ]
        rrow = statr[0:1, 512:512 + TQ]
        xrow = statr[32:33, 512:512 + TQ]
        PM, PQ, PMB, PRB = self.ps[0], self.ps[1], self.ps[2], self.ps[3]
        p.wait("tensor", "s", tag=f"{tag}:banks-quiet")
        mm_marks = {}
        for cc in range(Dc):
            b2 = cc % 2
            sl = sta[:, cc * TQ:(cc + 1) * TQ]
            if cc >= 2:
                p.wait("vector", "t", mm_marks[cc - 2], tag=f"{tag}:rawfree{cc}")
                p.wait("scalar", "t", mm_marks[cc - 2], tag=f"{tag}:sqfree{cc}")
            vv = p.op("vector", lambda e, s=sl, d=rawc[b2]: e.tensor_copy(d, s),
                      inc="v", tag=f"{tag}:cast{cc}")
            ss = p.op("scalar", lambda e, s=sl, d=sqc[b2]:
                      e.activation(d, s, AF.Square), inc="s",
                      tag=f"{tag}:sq{cc}")
            p.wait("tensor", "v", vv, tag=f"{tag}:mm-m{cc}")
            p.op("tensor", lambda e, r=rawc[b2], s=(cc == 0), st=(cc == Dc - 1):
                 e.matmul(PM[0:1, 0:TQ], self.t_perm("onesb")[:, 0:1], r,
                          start=s, stop=st),
                 inc="t", tag=f"{tag}:mmm{cc}")
            p.wait("tensor", "s", ss, tag=f"{tag}:mm-q{cc}")
            mm_marks[cc] = p.op(
                "tensor", lambda e, r=sqc[b2], s=(cc == 0), st=(cc == Dc - 1):
                e.matmul(PQ[0:1, 0:TQ], self.t_perm("onesb")[:, 0:1], r,
                         start=s, stop=st),
                inc="t", tag=f"{tag}:mmq{cc}")
        tmm = p.chans["t"].n
        p.wait("vector", "t", tmm, tag=f"{tag}:stats")
        p.op("vector", lambda e: e.tensor_scalar(
            mrow, PM[0:1, 0:TQ], self.col("invD")[0:1, :], None, ALU.mult),
            inc="v", tag=f"{tag}:m")
        p.op("vector", lambda e: e.tensor_scalar(
            vrow, PQ[0:1, 0:TQ], self.col("invD")[0:1, :],
            self.col("eps")[0:1, :], ALU.mult, ALU.add),
            inc="v", tag=f"{tag}:q")
        p.op("vector", lambda e: e.tensor_tensor(xrow, mrow, mrow, ALU.mult),
             inc="v", tag=f"{tag}:m2")
        vfin = p.op("vector", lambda e: e.tensor_tensor(
            vrow, vrow, xrow, ALU.subtract), inc="v", tag=f"{tag}:var")
        p.wait("scalar", "v", vfin, tag=f"{tag}:rstd")
        p.op("scalar", lambda e: e.activation(xrow, vrow, AF.Ln), inc="s",
             tag=f"{tag}:ln")
        sfin = p.op("scalar", lambda e: e.activation(
            rrow, xrow, AF.Exp, scale=self.col("neghalf")[0:1, :]),
            inc="s", tag=f"{tag}:exp")
        p.wait("tensor", "v", vfin, tag=f"{tag}:bc-m")
        p.wait("tensor", "s", sfin, tag=f"{tag}:bc-r")
        p.op("tensor", lambda e: e.matmul(
            PMB[:, 0:TQ], self.t_perm("onesf")[0:1, 0:128], mrow,
            start=True, stop=True), inc="t", tag=f"{tag}:bcm")
        tbc = p.op("tensor", lambda e: e.matmul(
            PRB[:, 0:TQ], self.t_perm("onesf")[0:1, 0:128], rrow,
            start=True, stop=True), inc="t", tag=f"{tag}:bcr")
        p.wait("vector", "t", tbc, tag=f"{tag}:norm")
        tmpf = self.t("tmpf", 200, 512, F32)
        for cc in range(Dc):
            sl = sta[:, cc * TQ:(cc + 1) * TQ]
            if xhat_stride == TQ:
                dst = xhat[:, cc * TQ:(cc + 1) * TQ]
            else:
                dst = xhat[:, cc * xhat_stride + tt * TQ:
                           cc * xhat_stride + (tt + 1) * TQ]
            p.op("vector", lambda e, s=sl:
                 e.tensor_tensor(tmpf, s, PMB[:, 0:TQ], ALU.subtract),
                 inc="v", tag=f"{tag}:sub{cc}")
            p.op("vector", lambda e, d=dst:
                 e.tensor_tensor(d, tmpf, PRB[:, 0:TQ], ALU.mult),
                 inc="v", tag=f"{tag}:mul{cc}")
        vn = p.chans["v"].n
        p.wait("tensor", "v", vn, tag=f"{tag}:bank-free")
        return vn

    def ln_post_add(self, src_slab, chunk_stride, tt, res_src, res_dst,
                    out_ch, tag, sta_ready=None, make_xhat=None):
        p, c = self.p, self.c
        Dc, TQ = c["Dc"], c["TQ"]
        sta = sta_ready if sta_ready is not None else \
            self.t("staD", 84, Dc * TQ, F32)
        if sta_ready is None:
            ad = p.chan("ad1")
            for chn in ("resw", "ody"):
                if chn in p.chans:
                    p.wait("sync", chn, tag=f"{tag}:res-landed")
            p.wait("sync", "v", tag=f"{tag}:sta-free")
            src3 = res_src.rearrange("(cc p) t -> p cc t", p=128)[
                :, :, tt * TQ:(tt + 1) * TQ]
            p.dma(sta.rearrange("p (cc t) -> p cc t", cc=Dc), src3, ad,
                  tag=f"{tag}:ldres")
        toff = (tt * TQ) % chunk_stride if chunk_stride > TQ else 0

        def srcsl(cc):
            if chunk_stride > TQ:
                return src_slab[:, cc * chunk_stride + toff:
                                cc * chunk_stride + toff + TQ]
            return src_slab[:, cc * TQ:(cc + 1) * TQ]

        statr = self.t_perm("statr")
        mrow = statr[0:1, 0:TQ]
        vrow = statr[32:33, 0:TQ]
        rrow = statr[0:1, 512:512 + TQ]
        xrow = statr[32:33, 512:512 + TQ]
        PM, PQ, PMB, PRB = self.ps[0], self.ps[1], self.ps[2], self.ps[3]
        p.wait("tensor", "s", tag=f"{tag}:banks-quiet")
        sqc = [self.t("sqc0", 198, 512, BF16), self.t("sqc1", 199, 512, BF16)]
        mm_marks = {}
        for cc in range(Dc):
            b2 = cc % 2
            if cc >= 2:
                p.wait("vector", "t", mm_marks[cc - 2], tag=f"{tag}:sqfree{cc}")
            vv = p.op("vector", lambda e, s=srcsl(cc), d=sqc[b2]:
                      e.tensor_tensor(d, s, s, ALU.mult), inc="v",
                      tag=f"{tag}:sq{cc}")
            p.op("tensor", lambda e, r=srcsl(cc), s=(cc == 0),
                 st=(cc == Dc - 1):
                 e.matmul(PM[0:1, 0:TQ], self.t_perm("onesb")[:, 0:1], r,
                          start=s, stop=st),
                 inc="t", tag=f"{tag}:mmm{cc}")
            p.wait("tensor", "v", vv, tag=f"{tag}:wsq{cc}")
            mm_marks[cc] = p.op(
                "tensor", lambda e, r=sqc[b2], s=(cc == 0), st=(cc == Dc - 1):
                e.matmul(PQ[0:1, 0:TQ], self.t_perm("onesb")[:, 0:1], r,
                         start=s, stop=st),
                inc="t", tag=f"{tag}:mmq{cc}")
        tmm = p.chans["t"].n
        p.wait("vector", "t", tmm, tag=f"{tag}:stats")
        p.op("vector", lambda e: e.tensor_scalar(
            mrow, PM[0:1, 0:TQ], self.col("invD")[0:1, :], None, ALU.mult),
            inc="v", tag=f"{tag}:m")
        p.op("vector", lambda e: e.tensor_scalar(
            vrow, PQ[0:1, 0:TQ], self.col("invD")[0:1, :],
            self.col("eps")[0:1, :], ALU.mult, ALU.add), inc="v",
            tag=f"{tag}:q")
        p.op("vector", lambda e: e.tensor_tensor(xrow, mrow, mrow, ALU.mult),
             inc="v", tag=f"{tag}:m2")
        vfin = p.op("vector", lambda e: e.tensor_tensor(
            vrow, vrow, xrow, ALU.subtract), inc="v", tag=f"{tag}:var")
        p.wait("scalar", "v", vfin, tag=f"{tag}:rstd")
        p.op("scalar", lambda e: e.activation(xrow, vrow, AF.Ln), inc="s",
             tag=f"{tag}:ln")
        sfin = p.op("scalar", lambda e: e.activation(
            rrow, xrow, AF.Exp, scale=self.col("neghalf")[0:1, :]), inc="s",
            tag=f"{tag}:exp")
        p.wait("tensor", "v", vfin, tag=f"{tag}:bc")
        p.wait("tensor", "s", sfin, tag=f"{tag}:bc2")
        p.op("tensor", lambda e: e.matmul(
            PMB[:, 0:TQ], self.t_perm("onesf")[0:1, 0:128], mrow,
            start=True, stop=True), inc="t", tag=f"{tag}:bcm")
        tbc = p.op("tensor", lambda e: e.matmul(
            PRB[:, 0:TQ], self.t_perm("onesf")[0:1, 0:128], rrow,
            start=True, stop=True), inc="t", tag=f"{tag}:bcr")
        p.wait("vector", "t", tbc, tag=f"{tag}:norm")
        if sta_ready is None:
            p.wait("vector", "ad1", tag=f"{tag}:res-tile")
        tmpf = self.t("tmpf", 200, 512, F32)
        tmpf2 = self.t("tmpf2", 202, 512, F32)
        for cc in range(Dc):
            stasl = sta[:, cc * TQ:(cc + 1) * TQ]
            p.op("vector", lambda e, s=srcsl(cc):
                 e.tensor_tensor(tmpf, s, PMB[:, 0:TQ], ALU.subtract),
                 inc="v", tag=f"{tag}:sub{cc}")
            p.op("vector", lambda e:
                 e.tensor_tensor(tmpf2, tmpf, PRB[:, 0:TQ], ALU.mult),
                 inc="v", tag=f"{tag}:mul{cc}")
            p.op("vector", lambda e, d=stasl:
                 e.tensor_tensor(d, d, tmpf2, ALU.add),
                 inc="v", tag=f"{tag}:add{cc}")
        vn = p.chans["v"].n
        p.wait("tensor", "v", vn, tag=f"{tag}:bank-free")
        out3 = res_dst.rearrange("(cc p) t -> p cc t", p=128)[
            :, :, tt * TQ:(tt + 1) * TQ]
        p.wait("sync", "v", vn, tag=f"{tag}:st-wait")
        p.dma(out3, sta.rearrange("p (cc t) -> p cc t", cc=Dc), out_ch,
              tag=f"{tag}:st")
        if make_xhat is not None:
            self.ln_make_xhat(sta, make_xhat, tt, f"{tag}:mx")


    def ln_make_xhat(self, sta, xbuf, tt, tag):
        p, c = self.p, self.c
        Dc, TQ = c["Dc"], c["TQ"]
        statr = self.t_perm("statr")
        mrow = statr[0:1, 0:TQ]
        vrow = statr[32:33, 0:TQ]
        rrow = statr[0:1, 512:512 + TQ]
        xrow = statr[32:33, 512:512 + TQ]
        PM, PQ, PMB, PRB = self.ps[0], self.ps[1], self.ps[2], self.ps[3]
        p.wait("tensor", "s", tag=f"{tag}:banks-quiet")
        p.wait("scalar", "v", tag=f"{tag}:sta-ready")
        rawc = [self.t("rawc0", 196, 512, BF16), self.t("rawc1", 197, 512, BF16)]
        sqc = [self.t("sqc0", 198, 512, BF16), self.t("sqc1", 199, 512, BF16)]
        mm_marks = {}
        for cc in range(Dc):
            b2 = cc % 2
            sl = sta[:, cc * TQ:(cc + 1) * TQ]
            if cc >= 2:
                p.wait("vector", "t", mm_marks[cc - 2], tag=f"{tag}:rawfree{cc}")
                p.wait("scalar", "t", mm_marks[cc - 2], tag=f"{tag}:sqfree{cc}")
            vv = p.op("vector", lambda e, s=sl, d=rawc[b2]: e.tensor_copy(d, s),
                      inc="v", tag=f"{tag}:cast{cc}")
            ss = p.op("scalar", lambda e, s=sl, d=sqc[b2]:
                      e.activation(d, s, AF.Square), inc="s",
                      tag=f"{tag}:sq{cc}")
            p.wait("tensor", "v", vv, tag=f"{tag}:mm-m{cc}")
            p.op("tensor", lambda e, r=rawc[b2], s=(cc == 0), st=(cc == Dc - 1):
                 e.matmul(PM[0:1, 0:TQ], self.t_perm("onesb")[:, 0:1], r,
                          start=s, stop=st),
                 inc="t", tag=f"{tag}:mmm{cc}")
            p.wait("tensor", "s", ss, tag=f"{tag}:mm-q{cc}")
            mm_marks[cc] = p.op(
                "tensor", lambda e, r=sqc[b2], s=(cc == 0), st=(cc == Dc - 1):
                e.matmul(PQ[0:1, 0:TQ], self.t_perm("onesb")[:, 0:1], r,
                         start=s, stop=st),
                inc="t", tag=f"{tag}:mmq{cc}")
        tmm = p.chans["t"].n
        p.wait("vector", "t", tmm, tag=f"{tag}:stats")
        p.op("vector", lambda e: e.tensor_scalar(
            mrow, PM[0:1, 0:TQ], self.col("invD")[0:1, :], None, ALU.mult),
            inc="v", tag=f"{tag}:m")
        p.op("vector", lambda e: e.tensor_scalar(
            vrow, PQ[0:1, 0:TQ], self.col("invD")[0:1, :],
            self.col("eps")[0:1, :], ALU.mult, ALU.add),
            inc="v", tag=f"{tag}:q")
        p.op("vector", lambda e: e.tensor_tensor(xrow, mrow, mrow, ALU.mult),
             inc="v", tag=f"{tag}:m2")
        vfin = p.op("vector", lambda e: e.tensor_tensor(
            vrow, vrow, xrow, ALU.subtract), inc="v", tag=f"{tag}:var")
        p.wait("scalar", "v", vfin, tag=f"{tag}:rstd")
        p.op("scalar", lambda e: e.activation(xrow, vrow, AF.Ln), inc="s",
             tag=f"{tag}:ln")
        sfin = p.op("scalar", lambda e: e.activation(
            rrow, xrow, AF.Exp, scale=self.col("neghalf")[0:1, :]),
            inc="s", tag=f"{tag}:exp")
        p.wait("tensor", "v", vfin, tag=f"{tag}:bc-m")
        p.wait("tensor", "s", sfin, tag=f"{tag}:bc-r")
        p.op("tensor", lambda e: e.matmul(
            PMB[:, 0:TQ], self.t_perm("onesf")[0:1, 0:128], mrow,
            start=True, stop=True), inc="t", tag=f"{tag}:bcm")
        tbc = p.op("tensor", lambda e: e.matmul(
            PRB[:, 0:TQ], self.t_perm("onesf")[0:1, 0:128], rrow,
            start=True, stop=True), inc="t", tag=f"{tag}:bcr")
        p.wait("vector", "t", tbc, tag=f"{tag}:norm")
        p.wait("vector", "xhw", tag=f"{tag}:xbuf-war")
        tmpf = self.t("tmpf", 200, 512, F32)
        for cc in range(Dc):
            sl = sta[:, cc * TQ:(cc + 1) * TQ]
            p.op("vector", lambda e, s=sl:
                 e.tensor_tensor(tmpf, s, PMB[:, 0:TQ], ALU.subtract),
                 inc="v", tag=f"{tag}:sub{cc}")
            p.op("vector", lambda e, d=xbuf[:, cc * TQ:(cc + 1) * TQ]:
                 e.tensor_tensor(d, tmpf, PRB[:, 0:TQ], ALU.mult),
                 inc="v", tag=f"{tag}:mul{cc}")
        vn = p.chans["v"].n
        p.wait("tensor", "v", vn, tag=f"{tag}:bank-free")
        p.wait("sync", "v", vn, tag=f"{tag}:st-wait")
        out3 = self.dram["xh8"].rearrange("(cc p) t -> p cc t", p=128)[
            :, :, tt * c["TQ"]:(tt + 1) * c["TQ"]]
        p.dma(out3, xbuf.rearrange("p (cc t) -> p cc t", cc=Dc), "xhw",
              tag=f"{tag}:st")

    def proj(self, name, wsrc, n_oc, n_kc, NTn, rhs_fn, evac_fn, woff_kb,
             oc0=0, drain=False):
        p, c = self.p, self.c
        TQ = c["TQ"]
        wst = [self.t(f"{name}wst{i}", woff_kb + i * (n_kc * 128 * 4) / KB,
                      n_kc * 128, F32) for i in range(2)]
        bfo = woff_kb + 2 * (n_kc * 128 * 4) / KB
        wbf = [self.t(f"{name}wbf{i}", bfo + i * (n_kc * 128 * 2) / KB,
                      n_kc * 128, BF16) for i in range(2)]
        wd = [p.chan("wd0"), p.chan("wd1")]
        free_marks = {}
        cast_marks = {}
        p.wait("tensor", "s", tag=f"{name}:banks-quiet")
        p.wait("tensor", "v", tag=f"{name}:banks-quiet-v")
        for oc in range(n_oc):
            sl = oc % 2
            src3 = wsrc.rearrange("(kc p) o -> p kc o", p=128)[
                :, :, (oc0 + oc) * 128:(oc0 + oc + 1) * 128]
            dst3 = wst[sl].rearrange("p (kc j) -> p kc j", kc=n_kc)
            if oc < 2:
                p.wait("sync", "t", tag=f"{name}:fresh{oc}")
                p.wait("sync", "v", tag=f"{name}:fresh-v{oc}")
            else:
                p.wait("sync", "v", cast_marks[oc - 2],
                       tag=f"{name}:wstfree{oc}")
            p.dma(dst3, src3, wd[sl], tag=f"{name}:wld{oc}")
            p.wait("vector", wd[sl], tag=f"{name}:wcast-wait{oc}")
            if oc >= 2:
                chn, v = free_marks[(oc - 2, NTn - 1)]
                p.wait("vector", chn, v, tag=f"{name}:wbffree{oc}")
            cast_marks[oc] = p.op(
                "vector", lambda e, d=wbf[sl], s=wst[sl]: e.tensor_copy(d, s),
                inc="v", tag=f"{name}:wcast{oc}")
            if drain:
                self.bg_step(3)
            p.wait("tensor", "v", cast_marks[oc], tag=f"{name}:mm-wait{oc}")
            for n in range(NTn):
                g = oc * NTn + n
                bank = self.ps[g % 2]
                if g >= 2:
                    pk = ((g - 2) // NTn, (g - 2) % NTn)
                    chn, v = free_marks[pk]
                    p.wait("tensor", chn, v, tag=f"{name}:bankfree{oc},{n}")
                tmk = None
                for kc in range(n_kc):
                    tmk = p.op("tensor", lambda e, b=bank, w=wbf[sl], k=kc,
                               nn=n, s=(kc == 0), st=(kc == n_kc - 1):
                               e.matmul(b[:, 0:TQ], w[:, k * 128:(k + 1) * 128],
                                        rhs_fn(k, nn), start=s, stop=st),
                               inc="t", tag=f"{name}:mm{oc},{n},{kc}")
                ev = evac_fn(oc, n, bank[:, 0:TQ], tmk)
                free_marks[(oc, n)] = ev
        return free_marks

    def phase_A(self):
        p, c = self.p, self.c
        TQ, NT, Dc, T = c["TQ"], c["NT"], c["Dc"], c["T"]
        xhat = self.t("xhatA", 84, Dc * T, BF16)
        sta = self.t("staA", 148, Dc * TQ, F32)
        for tt in range(NT):
            self.ln_in(self.dram["x"], tt, xhat, T, sta, f"A:ln{tt}")

        ctq = self.t("ctq", 20, c["DQc"] * T, BF16)
        ctkv = self.t("ctkv", 68, c["DKVc"] * T, BF16)

        def rhs(kc, n):
            return xhat[:, kc * T + n * TQ: kc * T + (n + 1) * TQ]

        def mk_evac(dst_slab, bias0):
            def evac(oc, n, ps, tmk):
                p.wait("scalar", "t", tmk, tag=f"ct:ev-wait{oc},{n}")
                sv = p.op("scalar", lambda e,
                          d=dst_slab[:, oc * T + n * TQ:oc * T + (n + 1) * TQ],
                          s=ps, b=self.bias_col(bias0 + oc):
                          e.activation(d, s, AF.Identity, bias=b),
                          inc="s", tag=f"ct:ev{oc},{n}")
                return ("s", sv)
            return evac

        self.proj("ctq", self.dram["wdq"], c["DQc"], Dc, NT, rhs,
                  mk_evac(ctq, self.b_dq()), woff_kb=148)
        self.proj("ctkv", self.dram["wdkv"], c["DKVc"], Dc, NT, rhs,
                  mk_evac(ctkv, self.b_dkv()), woff_kb=148)

    def phase_B(self):
        p, c = self.p, self.c
        TQ, NT, Dc, KVc, T = c["TQ"], c["NT"], c["Dc"], c["DKVc"], c["T"]
        ctkv = self.t("ctkv", 68, KVc * T, BF16)
        wst = self.t("Bwst", 84, KVc * c["D"], F32)
        wukb = self.t("Bwuk", 116, KVc * c["D"], BF16)
        wkrb = self.t("Bwkr", 132, KVc * c["D"], BF16)
        wd = p.chan("wd0")
        p.wait("sync", "t", tag="B:fresh")
        p.wait("sync", "v", tag="B:fresh-v")
        p.dma(wst.rearrange("p (kc o) -> p kc o", kc=KVc),
              self.dram["wuk"].rearrange("(kc p) o -> p kc o", p=128),
              wd, tag="B:lduk")
        p.wait("vector", wd, tag="B:castuk")
        vuk = p.op("vector", lambda e: e.tensor_copy(wukb, wst), inc="v",
                   tag="B:castuk")
        p.wait("sync", "v", vuk, tag="B:wst-free1")
        p.dma(wst.rearrange("p (kc o) -> p kc o", kc=KVc),
              self.dram["wkr"].rearrange("(kc p) o -> p kc o", p=128),
              wd, tag="B:ldkr")
        p.wait("vector", wd, tag="B:castkr")
        vkr = p.op("vector", lambda e: e.tensor_copy(wkrb, wst), inc="v",
                   tag="B:castkr")
        csl = [self.t(f"Bcs{i}", 148 + i * (2 * T * 2) / KB, 2 * T, BF16)
               for i in range(2)]
        csd = [p.chan("csd0"), p.chan("csd1")]
        ssd = [p.chan("ssd0"), p.chan("ssd1")]
        krS = self.t("BkrS", 180, T, BF16)
        kslab = self.t("Bkslab", 184, T, BF16)
        vslab = self.t("Bvslab", 188, c["D"], BF16)
        tmpf = self.t("tmpf", 200, 512, F32)
        tmpf2 = self.t("tmpf2", 202, 512, F32)
        p.wait("tensor", "v", vkr, tag="B:wait-casts")
        kw = p.chan("kvw")
        PK, PR, PP = self.ps[0], self.ps[1], self.ps[2]
        vdone = {}
        for oc in range(Dc):
            sl = oc % 2
            if oc >= 1:
                p.wait("vector", kw, tag=f"B:kslab-war{oc}")
            if oc >= 2:
                p.wait("sync", "v", vdone[oc - 2], tag=f"B:csfree{oc}")
            p.dma(csl[sl][:, 0:T],
                  self.dram["cexp"][oc * 128:(oc + 1) * 128, :], csd[sl],
                  tag=f"B:ldc{oc}")
            p.dma(csl[sl][:, T:2 * T],
                  self.dram["sexp"][oc * 128:(oc + 1) * 128, :], ssd[sl],
                  tag=f"B:lds{oc}")
            for n in range(NT):
                tk = None
                for kc in range(KVc):
                    tk = p.op("tensor", lambda e, k=kc, nn=n, o=oc,
                              s=(kc == 0), st=(kc == KVc - 1):
                              e.matmul(PK[:, 0:TQ],
                                       wukb[:, k * c["D"] + o * 128:
                                            k * c["D"] + (o + 1) * 128],
                                       ctkv[:, k * T + nn * TQ:
                                            k * T + (nn + 1) * TQ],
                                       start=s, stop=st),
                              inc="t", tag=f"B:mmk{oc},{n},{kc}")
                tr = None
                for kc in range(KVc):
                    tr = p.op("tensor", lambda e, k=kc, nn=n, o=oc,
                              s=(kc == 0), st=(kc == KVc - 1):
                              e.matmul(PR[:, 0:TQ],
                                       wkrb[:, k * c["D"] + o * 128:
                                            k * c["D"] + (o + 1) * 128],
                                       ctkv[:, k * T + nn * TQ:
                                            k * T + (nn + 1) * TQ],
                                       start=s, stop=st),
                              inc="t", tag=f"B:mmr{oc},{n},{kc}")
                p.wait("scalar", "t", tr, tag=f"B:evr{oc},{n}")
                sv = p.op("scalar", lambda e, nn=n,
                          bb_=self.bias_col(self.b_kr() + oc):
                          e.activation(krS[:, nn * TQ:(nn + 1) * TQ],
                                       PR[:, 0:TQ], AF.Identity, bias=bb_),
                          inc="s", tag=f"B:evr{oc},{n}")
                p.wait("tensor", "s", sv, tag=f"B:pm{oc},{n}")
                tp = p.op("tensor", lambda e, nn=n:
                          e.matmul(PP[:, 0:TQ], self.t_perm("ptrope"),
                                   krS[:, nn * TQ:(nn + 1) * TQ],
                                   start=True, stop=True),
                          inc="t", tag=f"B:pmm{oc},{n}")
                p.wait("vector", "t", tp, tag=f"B:rope{oc},{n}")
                p.wait("vector", csd[sl], tag=f"B:cs{oc},{n}")
                p.wait("vector", ssd[sl], tag=f"B:ss{oc},{n}")
                ch_ = csl[sl][:, n * TQ:(n + 1) * TQ]
                sh_ = csl[sl][:, T + n * TQ:T + (n + 1) * TQ]
                p.op("vector", lambda e, nn=n, cc=ch_:
                     e.tensor_tensor(tmpf, krS[:, nn * TQ:(nn + 1) * TQ], cc,
                                     ALU.mult), inc="v", tag=f"B:r1{oc},{n}")
                p.op("vector", lambda e, ss=sh_:
                     e.tensor_tensor(tmpf2, PP[:, 0:TQ], ss, ALU.mult),
                     inc="v", tag=f"B:r2{oc},{n}")
                p.op("vector", lambda e:
                     e.tensor_tensor(tmpf, tmpf, tmpf2, ALU.add),
                     inc="v", tag=f"B:r2b{oc},{n}")
                vv = p.op("vector", lambda e, nn=n,
                          bb_=self.bias_col(self.b_uk() + oc):
                          e.scalar_tensor_tensor(
                              kslab[:, nn * TQ:(nn + 1) * TQ], PK[:, 0:TQ],
                              bb_, tmpf, ALU.add, ALU.add),
                          inc="v", tag=f"B:r3{oc},{n}")
                p.wait("tensor", "v", vv, tag=f"B:bank{oc},{n}")
            vdone[oc] = vv
            p.wait("sync", "v", vv, tag=f"B:stk{oc}")
            p.dma(self.dram["k0"][oc * 128:(oc + 1) * 128, :], kslab[:, 0:T],
                  kw, tag=f"B:stk{oc}")
        wuvb = self.t("Bwuv", 116, KVc * c["D"], BF16)
        p.wait("sync", "t", tag="B:uk-done")
        p.dma(wst.rearrange("p (kc o) -> p kc o", kc=KVc),
              self.dram["wuv"].rearrange("(kc p) o -> p kc o", p=128),
              wd, tag="B:lduv")
        p.wait("vector", wd, tag="B:castuv")
        p.wait("vector", "t", tag="B:castuv-t")
        vuv = p.op("vector", lambda e: e.tensor_copy(wuvb, wst), inc="v",
                   tag="B:castuv")
        bvb = self.t_perm("bvb")
        brow = self.t("Bbrow", 132, c["D"], F32)
        ad = p.chan("ad0")
        p.dma(brow[0:1, 0:c["D"]], self.dram["bvrow"], ad, tag="B:ldbv")
        p.wait("tensor", ad, tag="B:bvb")
        p.wait("tensor", "v", vuv, tag="B:v-ready")
        nD = max(1, c["D"] // TQ)
        for n in range(nD):
            if n >= 1:
                p.wait("tensor", "s", tag=f"B:bvbank{n}")
            tb = p.op("tensor", lambda e, nn=n:
                      e.matmul(self.ps[4][:, 0:TQ],
                               self.t_perm("onesf")[0:1, 0:128],
                               brow[0:1, nn * TQ:(nn + 1) * TQ],
                               start=True, stop=True),
                      inc="t", tag=f"B:bvmm{n}")
            p.wait("scalar", "t", tb, tag=f"B:bvev{n}")
            p.op("scalar", lambda e, nn=n:
                 e.activation(bvb[:, nn * TQ:(nn + 1) * TQ],
                              self.ps[4][:, 0:TQ], AF.Copy),
                 inc="s", tag=f"B:bvev{n}")
        sbv = p.chans["s"].n
        p.wait("tensor", "s", sbv, tag="B:bv-done")
        for ttk in range(c["NKT"]):
            if ttk >= 1:
                p.wait("vector", kw, tag=f"B:vslab-war{ttk}")
            for n in range(nD):
                b = self.ps[n % 4]
                tmk = None
                for kc in range(KVc):
                    tmk = p.op("tensor", lambda e, k=kc, nn=n, t2=ttk, P=b,
                               s=(kc == 0), st=(kc == KVc - 1):
                               e.matmul(P[:, 0:TQ],
                                        ctkv[:, k * T + t2 * 128:
                                             k * T + (t2 + 1) * 128],
                                        wuvb[:, k * c["D"] + nn * TQ:
                                             k * c["D"] + (nn + 1) * TQ],
                                        start=s, stop=st),
                               inc="t", tag=f"B:vmm{ttk},{n},{kc}")
                p.wait("vector", "t", tmk, tag=f"B:vev{ttk},{n}")
                vv = p.op("vector", lambda e, nn=n, P=b:
                          e.tensor_tensor(vslab[:, nn * TQ:(nn + 1) * TQ],
                                          P[:, 0:TQ],
                                          bvb[:, nn * TQ:(nn + 1) * TQ],
                                          ALU.add),
                          inc="v", tag=f"B:vev{ttk},{n}")
            p.wait("sync", "v", vv, tag=f"B:stv{ttk}")
            p.dma(self.dram["v0"][ttk * 128:(ttk + 1) * 128, :],
                  vslab[:, 0:c["D"]], kw, tag=f"B:stv{ttk}")
            p.wait("tensor", "v", vv, tag=f"B:vbank{ttk}")

    def head_bound(self, qh, kh_slice, mh_col, tag):
        p, c = self.p, self.c
        TQ, NT = c["TQ"], c["NT"]
        statr = self.t_perm("statr")
        mhs = statr[0:1, 1002:1003]
        kcast = self.t_perm("kcast")
        tmpf = self.t("tmpf", 200, 512, F32)
        PN = self.ps[6]
        p.wait("tensor", "s", tag=f"{tag}:quiet")
        p.wait("tensor", "v", tag=f"{tag}:quiet-v")
        tmm = p.op("tensor", lambda e: e.matmul(
            PN[:, 0:TQ], kh_slice, qh[:, (NT - 1) * TQ:NT * TQ],
            start=True, stop=True), inc="t", tag=f"{tag}:smm")
        p.wait("vector", "t", tmm, tag=f"{tag}:red")
        p.op("vector", lambda e: e.reduce_max(
            tmpf[:, 0:1], PN[:, 0:TQ], AX.X), inc="v", tag=f"{tag}:red")
        vv = p.op("vector", lambda e: e.tensor_copy(kcast[:, 0:1],
                                                    tmpf[:, 0:1]),
                  inc="v", tag=f"{tag}:cast")
        p.wait("tensor", "v", vv, tag=f"{tag}:tr")
        tt_ = p.op("tensor", lambda e: e.transpose(
            self.ps["bf"][0:1, 0:128], kcast[:, 0:1], self.t_perm("identb")),
            inc="t", tag=f"{tag}:tr")
        p.wait("vector", "t", tt_, tag=f"{tag}:red2")
        p.op("vector", lambda e: e.reduce_max(
            mhs, self.ps["bf"][0:1, 0:128], AX.X), inc="v", tag=f"{tag}:red2")
        vmh = p.op("vector", lambda e: e.tensor_scalar(
            mhs, mhs, self.col("negsm")[0:1, :], None, ALU.mult), inc="v",
            tag=f"{tag}:neg")
        p.wait("tensor", "v", vmh, tag=f"{tag}:bc")
        tmh = p.op("tensor", lambda e: e.matmul(
            PN[:, 0:1], self.t_perm("onesf")[0:1, 0:128], mhs,
            start=True, stop=True), inc="t", tag=f"{tag}:bc")
        p.wait("vector", "t", tmh, tag=f"{tag}:cp")
        vmh = p.op("vector", lambda e: e.tensor_copy(
            mh_col, PN[:, 0:1]), inc="v", tag=f"{tag}:cp")
        p.wait("scalar", "v", vmh, tag=f"{tag}:ready")
        p.wait("tensor", "v", vmh, tag=f"{tag}:bankfree")

    def phase_C(self):
        p, c = self.p, self.c
        TQ, NT, T, Dc = c["TQ"], c["NT"], c["T"], c["Dc"]
        DQc, NH, NKT = c["DQc"], c["NH"], c["NKT"]
        ctq = self.t("ctq", 20, DQc * T, BF16)
        kh = [self.t(f"Ckh{i}", 84 + i * (T * 2) / KB, T, BF16)
              for i in range(2)]
        vh = [self.t(f"Cvh{i}", 92 + i * (T * 2) / KB, T, BF16)
              for i in range(2)]
        qh = self.t("Cqh", 100, T, BF16)
        ctxh = self.t("Cctxh", 104, T, BF16)
        qrS = self.t("CqrS", 108, T, BF16)
        csl = [self.t(f"Ccs{i}", 112 + i * (2 * T * 2) / KB, 2 * T, BF16)
               for i in range(2)]
        pT = [self.t(f"CpT{i}", 164 + i * 2, 2 * TQ, BF16) for i in range(2)]
        rbf = self.t("Crbf", 168, TQ, F32)
        wstq = [self.t(f"Cwstq{i}", 128 + i * (DQc * 128 * 4) / KB,
                       DQc * 128, F32) for i in range(2)]
        wstr = [self.t(f"Cwstr{i}", 140 + i * (DQc * 128 * 4) / KB,
                       DQc * 128, F32) for i in range(2)]
        wqb = [self.t(f"Cwqb{i}", 152 + i * (DQc * 128 * 2) / KB,
                      DQc * 128, BF16) for i in range(2)]
        wrb = [self.t(f"Cwrb{i}", 158 + i * (DQc * 128 * 2) / KB,
                      DQc * 128, BF16) for i in range(2)]
        tmpf = self.t("tmpf", 200, 512, F32)
        tmpf2 = self.t("tmpf2", 202, 512, F32)
        statr = self.t_perm("statr")
        wd = [p.chan("wd0"), p.chan("wd1")]
        wdr = [p.chan("wdr0"), p.chan("wdr1")]
        khd = [p.chan("khd0"), p.chan("khd1")]
        vhd = [p.chan("vhd0"), p.chan("vhd1")]
        csd = [p.chan("csd0"), p.chan("csd1")]
        ssd = [p.chan("ssd0"), p.chan("ssd1")]
        abw = p.chan("abw")
        p.wait("sync", "kvw", tag="C:kv-landed")
        mh_col = self.col("mh0")
        self._smark = {}
        self._rbe_prev = None
        self._ctx_marks = {}
        cast_marks = {}
        head_done = {}
        rope_done = {}
        for h in range(NH):
            sl = h % 2
            if h >= 2:
                p.wait("sync", "s", cast_marks[h - 2], tag=f"C:wstfree{h}")
            src3 = self.dram["wuq"].rearrange("(kc p) o -> p kc o", p=128)[
                :, :, h * 128:(h + 1) * 128]
            p.dma(wstq[sl].rearrange("p (kc j) -> p kc j", kc=DQc), src3,
                  wd[sl], tag=f"C:wldq{h}")
            src3 = self.dram["wqr"].rearrange("(kc p) o -> p kc o", p=128)[
                :, :, h * 128:(h + 1) * 128]
            p.dma(wstr[sl].rearrange("p (kc j) -> p kc j", kc=DQc), src3,
                  wdr[sl], tag=f"C:wldr{h}")
            if h >= 2:
                p.wait("sync", "t", head_done[h - 2], tag=f"C:khfree{h}")
            p.dma(kh[sl], self.dram["k0"][h * 128:(h + 1) * 128, :],
                  khd[sl], tag=f"C:ldkh{h}")
            vsrc = bass.AP(self.dram["v0"].tensor, h * 128,
                           [[c["D"], 128], [128 * c["D"], NKT], [1, 128]])
            p.dma(vh[sl].rearrange("p (t j) -> p t j", t=NKT), vsrc,
                  vhd[sl], tag=f"C:ldvh{h}")
            if h >= 2:
                p.wait("sync", "v", rope_done[h - 2], tag=f"C:csfree{h}")
            p.dma(csl[sl][:, 0:T],
                  self.dram["cexp"][h * 128:(h + 1) * 128, :], csd[sl],
                  tag=f"C:ldc{h}")
            p.dma(csl[sl][:, T:2 * T],
                  self.dram["sexp"][h * 128:(h + 1) * 128, :], ssd[sl],
                  tag=f"C:lds{h}")
            self.bg_step(1, allow_collapse=True)
            p.wait("scalar", wd[sl], tag=f"C:wcast{h}")
            p.wait("scalar", wdr[sl], tag=f"C:wcastr-dma{h}")
            p.op("scalar", lambda e, s=sl: e.copy(wqb[s], wstq[s]),
                 inc="s", tag=f"C:wcastq{h}")
            cast_marks[h] = p.op(
                "scalar", lambda e, s=sl: e.copy(wrb[s], wstr[s]),
                inc="s", tag=f"C:wcastr{h}")
            p.wait("tensor", "s", cast_marks[h], tag=f"C:q-wait{h}")
            PQm, PRm, PPm = self.ps[0], self.ps[1], self.ps[2]
            for n in range(NT):
                for kc in range(DQc):
                    p.op("tensor", lambda e, k=kc, nn=n, s=sl,
                         st=(kc == DQc - 1), s0=(kc == 0):
                         e.matmul(PQm[:, 0:TQ],
                                  wqb[s][:, k * 128:(k + 1) * 128],
                                  ctq[:, k * T + nn * TQ:k * T + (nn + 1) * TQ],
                                  start=s0, stop=st),
                         inc="t", tag=f"C:qmm{h},{n},{kc}")
                tr = None
                for kc in range(DQc):
                    tr = p.op("tensor", lambda e, k=kc, nn=n, s=sl,
                              st=(kc == DQc - 1), s0=(kc == 0):
                              e.matmul(PRm[:, 0:TQ],
                                       wrb[s][:, k * 128:(k + 1) * 128],
                                       ctq[:, k * T + nn * TQ:
                                            k * T + (nn + 1) * TQ],
                                       start=s0, stop=st),
                              inc="t", tag=f"C:rmm{h},{n},{kc}")
                p.wait("scalar", "t", tr, tag=f"C:evr{h},{n}")
                sv = p.op("scalar", lambda e, nn=n,
                          bb_=self.bias_col(self.b_qr() + h):
                          e.activation(qrS[:, nn * TQ:(nn + 1) * TQ],
                                       PRm[:, 0:TQ], AF.Identity, bias=bb_),
                          inc="s", tag=f"C:evr{h},{n}")
                p.wait("tensor", "s", sv, tag=f"C:pm{h},{n}")
                tp = p.op("tensor", lambda e, nn=n:
                          e.matmul(PPm[:, 0:TQ], self.t_perm("ptrope"),
                                   qrS[:, nn * TQ:(nn + 1) * TQ],
                                   start=True, stop=True),
                          inc="t", tag=f"C:pmm{h},{n}")
                p.wait("vector", "t", tp, tag=f"C:rope{h},{n}")
                p.wait("vector", csd[sl], tag=f"C:cs{h},{n}")
                p.wait("vector", ssd[sl], tag=f"C:ss{h},{n}")
                ch_ = csl[sl][:, n * TQ:(n + 1) * TQ]
                sh_ = csl[sl][:, T + n * TQ:T + (n + 1) * TQ]
                p.op("vector", lambda e, nn=n, cc=ch_:
                     e.tensor_tensor(tmpf, qrS[:, nn * TQ:(nn + 1) * TQ], cc,
                                     ALU.mult), inc="v", tag=f"C:r1{h},{n}")
                p.op("vector", lambda e, ss=sh_:
                     e.tensor_tensor(tmpf2, PPm[:, 0:TQ], ss, ALU.mult),
                     inc="v", tag=f"C:r2{h},{n}")
                p.op("vector", lambda e:
                     e.tensor_tensor(tmpf, tmpf, tmpf2, ALU.add),
                     inc="v", tag=f"C:r2b{h},{n}")
                vq = p.op("vector", lambda e, nn=n,
                          bb_=self.bias_col(self.b_uq() + h):
                          e.scalar_tensor_tensor(
                              qh[:, nn * TQ:(nn + 1) * TQ], PQm[:, 0:TQ],
                              bb_, tmpf, ALU.add, ALU.add),
                          inc="v", tag=f"C:r3{h},{n}")
                p.wait("tensor", "v", vq, tag=f"C:qbank{h},{n}")
            rope_done[h] = vq
            p.wait("vector", "abw", tag=f"C:ctx-war{h}")
            p.wait("tensor", khd[sl], tag=f"C:kvh-ready{h}")
            p.wait("tensor", vhd[sl], tag=f"C:vh-ready{h}")
            self.head_bound(qh[:, 0:T], kh[sl][:, 0:128], mh_col, f"C:hb{h}")
            SP = [self.ps["A"], self.ps["B"]]
            for qc in range(NT):
                PC = self.ps[4] if qc % 2 == 0 else self.ps[5]
                PS6 = self.ps[6]
                exp_marks = {}
                npair = NKT // 2

                def smm(pi2, qv=qc):
                    if pi2 >= npair:
                        return
                    if pi2 >= 2:
                        chn, vv_ = exp_marks[pi2 - 2]
                        p.wait("tensor", chn, vv_, tag=f"C:Sfree{h},{qv},{pi2}")
                    mk = None
                    for half in range(2):
                        mk = p.op("tensor", lambda e, k=2 * pi2 + half, s=sl,
                                  q=qv, bb=pi2 % 2, hh=half:
                                  e.matmul(SP[bb][:, hh * TQ:(hh + 1) * TQ],
                                           kh[s][:, k * 128:(k + 1) * 128],
                                           qh[:, q * TQ:(q + 1) * TQ],
                                           start=True, stop=True),
                                  inc="t", tag=f"C:smm{h},{qv},{pi2},{half}")
                    self._smark[(h, qv, pi2)] = mk

                smm(0)
                for pi in range(npair):
                    b = pi % 2
                    smm(pi + 1)
                    p.wait("scalar", "t", self._smark[(h, qc, pi)],
                           tag=f"C:exp{h},{qc},{pi}")
                    sv = p.op("scalar", lambda e, bb=b, pp=pi:
                              e.activation(pT[pp % 2], SP[bb][:, 0:2 * TQ],
                                           AF.Exp, bias=mh_col,
                                           scale=self.col("sm")),
                              inc="s", tag=f"C:exp{h},{qc},{pi}")
                    exp_marks[pi] = ("s", sv)
                    p.wait("tensor", "s", sv, tag=f"C:pv{h},{qc},{pi}")
                    if pi == 0 and self._rbe_prev is not None:
                        p.wait("tensor", "s", self._rbe_prev,
                               tag=f"C:sumfree{h},{qc}")
                    if pi == 0 and self._ctx_marks.get(qc % 2) is not None:
                        p.wait("tensor", "v", self._ctx_marks[qc % 2],
                               tag=f"C:pcfree{h},{qc}")
                    for half in range(2):
                        kt = 2 * pi + half
                        p.op("tensor", lambda e, k=kt, s=sl, pp=pi, hh=half,
                             P=PC, s0=(kt == 0), st=(kt == NKT - 1):
                             e.matmul(P[:, 0:TQ],
                                      vh[s][:, k * 128:(k + 1) * 128],
                                      pT[pp % 2][:, hh * TQ:(hh + 1) * TQ],
                                      start=s0, stop=st),
                             inc="t", tag=f"C:pv{h},{qc},{kt}")
                        tsum = p.op("tensor", lambda e, pp=pi, hh=half,
                                    s0=(kt == 0), st=(kt == NKT - 1):
                                    e.matmul(PS6[0:1, 0:TQ],
                                             self.t_perm("onesb")[:, 0:1],
                                             pT[pp % 2][:, hh * TQ:(hh + 1) * TQ],
                                             start=s0, stop=st),
                                    inc="t", tag=f"C:sum{h},{qc},{kt}")
                p.wait("vector", "t", tsum, tag=f"C:recip{h},{qc}")
                srow = statr[0:1, 0:TQ]
                vv = p.op("vector", lambda e: e.reciprocal(
                    srow, PS6[0:1, 0:TQ]), inc="v", tag=f"C:recip{h},{qc}")
                p.wait("tensor", "v", vv, tag=f"C:rb{h},{qc}")
                tb = p.op("tensor", lambda e: e.matmul(
                    PS6[:, 0:TQ], self.t_perm("onesf")[0:1, 0:128], srow,
                    start=True, stop=True), inc="t", tag=f"C:rb{h},{qc}")
                p.wait("scalar", "t", tb, tag=f"C:rbe{h},{qc}")
                self._rbe_prev = p.op("scalar", lambda e: e.activation(
                    rbf, PS6[:, 0:TQ], AF.Copy), inc="s",
                    tag=f"C:rbe{h},{qc}")
                p.wait("vector", "s", self._rbe_prev, tag=f"C:ctx{h},{qc}")
                self._ctx_marks[qc % 2] = p.op(
                    "vector", lambda e, q=qc, P=PC: e.tensor_tensor(
                        ctxh[:, q * TQ:(q + 1) * TQ], P[:, 0:TQ], rbf,
                        ALU.mult),
                    inc="v", tag=f"C:ctx{h},{qc}")
                self.bg_step(12)
            head_done[h] = p.chans["t"].n
            p.wait("sync", "v", tag=f"C:stab{h}")
            p.dma(self.dram["ab"][h * 128:(h + 1) * 128, :], ctxh[:, 0:T],
                  abw, tag=f"C:stab{h}")

    def phase_D(self, wo_name, bias0, res_src, res_dst, out_ch_name, tag):
        p, c = self.p, self.c
        TQ, Dc, T = c["TQ"], c["Dc"], c["T"]
        T2 = max(c["T2"], TQ)
        nhalves = T // T2
        p.wait("sync", "abw", tag=f"{tag}:ab-landed")
        asb = self.t("Dasb", 20, Dc * T2, BF16)
        aout = self.t("Daout", 52, Dc * T2, BF16)
        ad = p.chan("ad0")
        out_ch = p.chan(out_ch_name)
        NTh = T2 // TQ
        for half in range(nhalves):
            t0 = half * T2
            src3 = self.dram["ab"].rearrange("(cc p) t -> p cc t", p=128)[
                :, :, t0:t0 + T2]
            p.wait("sync", "t", tag=f"{tag}:asb-free{half}")
            p.wait("scalar", "v", tag=f"{tag}:aout-free{half}")
            p.dma(asb.rearrange("p (cc t) -> p cc t", cc=Dc), src3, ad,
                  tag=f"{tag}:lda{half}")
            p.wait("vector", ad, tag=f"{tag}:a-ready{half}")
            vr = p.op("vector", lambda e: e.engine_nop(), inc="v",
                      tag=f"{tag}:a-ready{half}")
            p.wait("tensor", "v", vr, tag=f"{tag}:a-ready-t{half}")
            self.bg_step(2, allow_collapse=True)

            def rhs(kc, n):
                return asb[:, kc * T2 + n * TQ: kc * T2 + (n + 1) * TQ]

            def evac(oc, n, ps, tmk):
                p.wait("scalar", "t", tmk, tag=f"{tag}:ev-wait{oc},{n}")
                sv = p.op("scalar", lambda e,
                          d=aout[:, oc * T2 + n * TQ:oc * T2 + (n + 1) * TQ],
                          s=ps, b=self.bias_col(bias0 + oc):
                          e.activation(d, s, AF.Identity, bias=b),
                          inc="s", tag=f"{tag}:ev{oc},{n}")
                return ("s", sv)

            self.proj(f"{tag}wo{half}", self.dram[wo_name], Dc, Dc, NTh,
                      rhs, evac, woff_kb=116, drain=True)
            sev = p.chans["s"].n
            p.wait("vector", "s", sev, tag=f"{tag}:ln-wait{half}")
            xbuf = self.t("Dxbuf", 148, Dc * TQ, BF16)
            for ttl in range(NTh):
                self.ln_post_add(aout, T2, half * NTh + ttl, res_src, res_dst,
                                 out_ch, f"{tag}:pa{half}.{ttl}",
                                 make_xhat=xbuf)
        return out_ch

    def bg_bufs(self, ring=2):
        b = getattr(self, "bg_region", 170)
        if ring == 4:
            gst = [self.t(f"bgr{i}_{b}", b + 4 * i, 1024, F32)
                   for i in range(4)]
            return gst, None, None, None
        gst = [self.t(f"bgst0_{b}", b, 1024, F32),
               self.t(f"bgst1_{b}", b + 4, 1024, F32)]
        pos = self.t(f"bgpos_{b}", b + 8, 1024, BF16)
        neg = self.t(f"bgneg_{b}", b + 10, 1024, BF16)
        bft = [self.t(f"bgbf0_{b}", b + 12, 1024, BF16),
               self.t(f"bgbf1_{b}", b + 14, 1024, BF16)]
        return gst, pos, neg, bft

    def gamma_tiles_gen(self, wname, gname, tag):
        p, c = self.p, self.c
        ap = self.dram[wname]
        K, O = ap.tensor.shape
        if O <= 1024:
            CW = O
        else:
            CW = next(w for w in (1024, 512, 256, 128) if O % w == 0)
        ncw = O // CW
        gd = [p.chan(f"g{i}") for i in range(4)]
        gacc = self.col(gname)
        gtmp = self.col(gname + "_t")
        p.op("vector", lambda e: e.memset(gacc, 0.0), inc="v", tag=f"{tag}:0")
        vprev = {}
        i = 0
        for rc in range(K // 128):
            for j in range(ncw):
                sl = i % 4
                if i == 0:
                    p.wait("gpsimd", "v", tag=f"{tag}:prev-user")
                    for chn in ("tzw0", "tzw1"):
                        if chn in p.chans:
                            p.wait("gpsimd", chn, tag=f"{tag}:tern-quiet")
                if i >= 4:
                    p.wait("gpsimd", "v", vprev[i - 4], tag=f"{tag}:free{i}")
                gst, _, _, _ = self.bg_bufs(ring=4)
                p.dma(gst[sl][:, 0:CW],
                      ap[rc * 128:(rc + 1) * 128, j * CW:(j + 1) * CW],
                      gd[sl], q="gpsimd", tag=f"{tag}:ld{i}")
                p.wait("vector", gd[sl], tag=f"{tag}:red{i}")
                p.op("vector", lambda e, g_=gst[sl], w=CW: e.tensor_reduce(
                    gtmp, g_[:, 0:w], AX.X, ALU.add,
                    apply_absolute_value=True), inc="v", tag=f"{tag}:red{i}")
                vprev[i] = p.op("vector", lambda e: e.tensor_tensor(
                    gacc, gacc, gtmp, ALU.add), inc="v", tag=f"{tag}:acc{i}")
                self._g_last_acc = vprev[i]
                i += 1
                yield

    def gamma_collapse_gen(self, gname, invN_name, tag):
        p = self.p
        gacc = self.col(gname)
        p.wait("tensor", "v", self._g_last_acc, tag=f"{tag}:tr")
        p.wait("tensor", "s", tag=f"{tag}:tr-s")
        tt_ = p.op("tensor", lambda e: e.transpose(
            self.ps[0][0:1, 0:128], gacc, self.t_perm("identf")),
            inc="t", tag=f"{tag}:tr")
        statr = self.t_perm("statr")
        g1 = statr[0:1, 1003:1004]
        g2 = statr[0:1, 1004:1005]
        p.wait("vector", "t", tt_, tag=f"{tag}:col")
        p.op("vector", lambda e: e.tensor_reduce(
            g1, self.ps[0][0:1, 0:128], AX.X, ALU.add), inc="v",
            tag=f"{tag}:col")
        p.op("vector", lambda e: e.tensor_scalar(
            g1, g1, self.col(invN_name)[0:1, :], self.col("eps8")[0:1, :],
            ALU.mult, ALU.add), inc="v", tag=f"{tag}:gam")
        vv = p.op("vector", lambda e: e.reciprocal(g2, g1), inc="v",
                  tag=f"{tag}:rec")
        p.wait("tensor", "v", vv, tag=f"{tag}:bc")
        p.op("tensor", lambda e: e.matmul(
            self.ps[0][:, 0:1], self.t_perm("onesf")[0:1, 0:128], g1,
            start=True, stop=True), inc="t", tag=f"{tag}:bc1")
        tb = p.op("tensor", lambda e: e.matmul(
            self.ps[1][:, 0:1], self.t_perm("onesf")[0:1, 0:128], g2,
            start=True, stop=True), inc="t", tag=f"{tag}:bc2")
        p.wait("vector", "t", tb, tag=f"{tag}:bcv")
        p.op("vector", lambda e: e.tensor_copy(
            self.col(gname + "_s"), self.ps[0][:, 0:1]), inc="v",
            tag=f"{tag}:bcv1")
        vv = p.op("vector", lambda e: e.tensor_copy(
            self.col(gname + "_r"), self.ps[1][:, 0:1]), inc="v",
            tag=f"{tag}:bcv2")
        p.wait("tensor", "v", vv, tag=f"{tag}:done")
        p.wait("scalar", "v", vv, tag=f"{tag}:done-s")
        yield

    def tern_tiles_gen(self, wname, cache, gname, tag):
        p, c = self.p, self.c
        ap = self.dram[wname]
        cap = self.dram[cache]
        K, O = ap.tensor.shape
        if O <= 1024:
            CW = O
        else:
            CW = next(w for w in (1024, 512, 256, 128) if O % w == 0)
        ncw = O // CW
        grecip_col = self.col(gname + "_r")
        gd = [p.chan("g0"), p.chan("g1")]
        tw = [p.chan("tzw0"), p.chan("tzw1")]
        sprev = {}
        vprev = {}
        i = 0
        for rc in range(K // 128):
            for j in range(ncw):
                sl = i % 2
                if i == 0:
                    p.wait("gpsimd", "v", tag=f"{tag}:prev-user")
                if i >= 2:
                    p.wait("gpsimd", "v", sprev[i - 2], tag=f"{tag}:free{i}")
                gst, pos, neg, bft = self.bg_bufs()
                p.dma(gst[sl][:, 0:CW],
                      ap[rc * 128:(rc + 1) * 128, j * CW:(j + 1) * CW],
                      gd[sl], q="gpsimd", tag=f"{tag}:ld{i}")
                p.wait("vector", gd[sl], tag=f"{tag}:cmp{i}")
                p.op("vector", lambda e, g_=gst[sl], po=pos, w=CW:
                     e.tensor_scalar(po[:, 0:w], g_[:, 0:w], grecip_col,
                                     self.col("half")[:, 0:1],
                                     ALU.mult, ALU.is_ge),
                     inc="v", tag=f"{tag}:pos{i}")
                sprev[i] = p.op("vector", lambda e, g_=gst[sl], ne=neg, w=CW:
                                e.tensor_scalar(ne[:, 0:w], g_[:, 0:w],
                                                grecip_col,
                                                self.col("neghalf")[:, 0:1],
                                                ALU.mult, ALU.is_le),
                                inc="v", tag=f"{tag}:neg{i}")
                p.wait("vector", tw[sl], tag=f"{tag}:bffree{i}")
                vprev[i] = p.op("vector", lambda e, b_=bft[sl], po=pos,
                                ne=neg, w=CW:
                                e.tensor_tensor(b_[:, 0:w], po[:, 0:w],
                                                ne[:, 0:w], ALU.subtract),
                                inc="v", tag=f"{tag}:sub{i}")
                p.wait("gpsimd", "v", vprev[i], tag=f"{tag}:st{i}")
                p.dma(cap[rc * 128:(rc + 1) * 128, j * CW:(j + 1) * CW],
                      bft[sl][:, 0:CW], tw[sl], q="gpsimd", tag=f"{tag}:st{i}")
                i += 1
                yield

    def bg_enqueue(self):
        q = []
        for wname, gname, invN in (("wup", "gup", "invE0"),
                                   ("wgt", "ggt", "invE0"),
                                   ("wdn", "gdn", "invE0")):
            q.append((self.gamma_tiles_gen(wname, gname, "G" + gname), False))
            q.append((self.gamma_collapse_gen(gname, invN, "Gc" + gname), True))
        q.append((self.tern_tiles_gen("wdn", "tdn", "gdn", "Tdn"), False))
        for wname, gname, invN in (("ww1", "gw1", "invE1"),
                                   ("ww2", "gw2", "invE1"),
                                   ("ww3", "gw3", "invE1")):
            q.append((self.gamma_tiles_gen(wname, gname, "G" + gname), False))
            q.append((self.gamma_collapse_gen(gname, invN, "Gc" + gname), True))
        q.append((self.tern_tiles_gen("ww3", "tw3", "gw3", "Tw3"), False))
        self.bg = q
        self.bg_total = len(q)

    def bg_step(self, n, allow_collapse=False):
        while n > 0 and self.bg:
            gen, is_col = self.bg[0]
            if is_col and not allow_collapse:
                return
            try:
                next(gen)
                n -= 1
            except StopIteration:
                self.bg.pop(0)

    def bg_finish(self, keep_last):
        while len(self.bg) > keep_last:
            gen, is_col = self.bg[0]
            try:
                next(gen)
            except StopIteration:
                self.bg.pop(0)

    def ffn(self, layer, res_src, res_dst, out_ch_name, tag):
        p, c = self.p, self.c
        TQ, Dc, T, NT = c["TQ"], c["Dc"], c["T"], c["NT"]
        Ec = c["E0c"] if layer == 0 else c["E1c"]
        wu_name, wg_name = (("wup", "wgt") if layer == 0 else ("ww1", "ww2"))
        dn_cache = "tdn" if layer == 0 else "tw3"
        gu = "gup" if layer == 0 else "gw1"
        gg = "ggt" if layer == 0 else "gw2"
        gd_ = "gdn" if layer == 0 else "gw3"
        out_ch = p.chan(out_ch_name)
        xh = self.t("Fxh", 20, Dc * T, BF16)
        sta = self.t("staD", 84, Dc * TQ, F32)
        wst = [[self.t(f"Fw{a}{i}", 116 + (a * 3 + i) * 8, Dc * 128, F32)
                for i in range(3)] for a in range(2)]
        pos = self.t("Fpos", 164, 2048, BF16)
        neg = self.t("Fneg", 168, 2048, BF16)
        wub = [self.t(f"Fwub{i}", 172 + i * 4, Dc * 128, BF16)
               for i in range(2)]
        wgb = [self.t(f"Fwgb{i}", 180 + i * 4, Dc * 128, BF16)
               for i in range(2)]
        hbuf = [self.t(f"Fhbuf{i}", 188 + i * 4, T, BF16) for i in range(2)]
        tsil = self.t("Ftsil", 202, 512, F32)
        wd = [p.chan(f"wd{i}") for i in range(2)]
        wdg = [p.chan(f"wdg{i}") for i in range(2)]
        wd3 = [p.chan("wd0"), p.chan("wd1"), p.chan("wd2")]
        wg3 = [p.chan("wdg0"), p.chan("wdg1"), p.chan("wdg2")]
        hw = [p.chan("hw0"), p.chan("hw1")]
        p.wait("sync", "xhw", tag=f"{tag}:xh-landed")
        xld = [p.chan("ad0"), p.chan("ad1")]
        for tt in range(NT):
            p.wait("sync", "t", tag=f"{tag}:xh-free{tt}")
            p.dma(xh.rearrange("p (cc t) -> p cc t", cc=Dc)[
                      :, :, tt * TQ:(tt + 1) * TQ],
                  self.dram["xh8"].rearrange("(cc p) t -> p cc t", p=128)[
                      :, :, tt * TQ:(tt + 1) * TQ],
                  xld[tt % 2], tag=f"{tag}:ldxh{tt}")
        hid = self.dram["hd0"]
        for chn in ("g0", "g1", "g2", "g3", "tzw0", "tzw1", "xhw"):
            if chn in p.chans:
                p.wait("vector", chn, tag=f"{tag}:bg-quiet-start")

        def w_dma(ec):
            if ec >= Ec:
                return
            s3 = ec % 3
            p.dma(wst[0][s3].rearrange("p (kc j) -> p kc j", kc=Dc),
                  self.dram[wu_name].rearrange("(kc p) o -> p kc o", p=128)[
                      :, :, ec * 128:(ec + 1) * 128],
                  wd3[s3], tag=f"{tag}:ldu{ec}")
            p.dma(wst[1][s3].rearrange("p (kc j) -> p kc j", kc=Dc),
                  self.dram[wg_name].rearrange("(kc p) o -> p kc o", p=128)[
                      :, :, ec * 128:(ec + 1) * 128],
                  wg3[s3], tag=f"{tag}:ldg{ec}")

        tern_marks = {}

        def w_tern(ec):
            if ec >= Ec:
                return
            s3, sl = ec % 3, ec % 2
            n_ = Dc * 128
            p.wait("vector", wd3[s3], tag=f"{tag}:tu{ec}")
            p.wait("vector", wg3[s3], tag=f"{tag}:tg{ec}")
            if ec >= 2:
                p.wait("vector", "t", mm_marks[ec - 2], tag=f"{tag}:wbf{ec}")
            vv = None
            for a, (wb, gcol) in enumerate(((wub, gu), (wgb, gg))):
                p.op("vector", lambda e, s=s3, aa=a, w=n_,
                     rc=self.col(gcol + "_r"):
                     e.tensor_scalar(pos[:, 0:w], wst[aa][s], rc,
                                     self.col("half")[:, 0:1],
                                     ALU.mult, ALU.is_ge),
                     inc="v", tag=f"{tag}:pos{ec},{a}")
                p.op("vector", lambda e, s=s3, aa=a, w=n_,
                     rc=self.col(gcol + "_r"):
                     e.tensor_scalar(neg[:, 0:w], wst[aa][s], rc,
                                     self.col("neghalf")[:, 0:1],
                                     ALU.mult, ALU.is_le),
                     inc="v", tag=f"{tag}:neg{ec},{a}")
                vv = p.op("vector", lambda e, s=sl, w=n_, wb_=wb:
                          e.tensor_tensor(wb_[s][:, 0:w], pos[:, 0:w],
                                          neg[:, 0:w], ALU.subtract),
                          inc="v", tag=f"{tag}:sub{ec},{a}")
            tern_marks[ec] = vv

        mm_marks = {}
        prev_stt = None
        w_dma(0)
        w_dma(1)
        w_tern(0)
        p.wait("tensor", "ad0", tag=f"{tag}:xh-rdy0")
        p.wait("tensor", "ad1", tag=f"{tag}:xh-rdy1")
        for ec in range(Ec):
            sl = ec % 2
            w_dma(ec + 2)
            w_tern(ec + 1)
            p.wait("tensor", "v", tern_marks[ec], tag=f"{tag}:mm{ec}")
            vv = None
            for tt in range(NT):
                g = ec * NT + tt
                PU = self.ps[0] if g % 2 == 0 else self.ps[2]
                PG = self.ps[1] if g % 2 == 0 else self.ps[3]
                if g >= 2:
                    key = ((g - 2) // NT, (g - 2) % NT)
                    if key in mm_marks:
                        chn, v_ = mm_marks[key]
                        p.wait("tensor", chn, v_, tag=f"{tag}:bank{ec},{tt}")
                tg = None
                for kc in range(Dc):
                    p.op("tensor", lambda e, k=kc, s=sl, P=PU, n2=tt,
                         s0=(kc == 0), st=(kc == Dc - 1):
                         e.matmul(P[:, 0:TQ], wub[s][:, k * 128:(k + 1) * 128],
                                  xh[:, k * T + n2 * TQ:k * T + (n2 + 1) * TQ],
                                  start=s0, stop=st),
                         inc="t", tag=f"{tag}:umm{ec},{tt},{kc}")
                for kc in range(Dc):
                    tg = p.op("tensor", lambda e, k=kc, s=sl, P=PG, n2=tt,
                              s0=(kc == 0), st=(kc == Dc - 1):
                              e.matmul(P[:, 0:TQ],
                                       wgb[s][:, k * 128:(k + 1) * 128],
                                       xh[:, k * T + n2 * TQ:
                                          k * T + (n2 + 1) * TQ],
                                       start=s0, stop=st),
                              inc="t", tag=f"{tag}:gmm{ec},{tt},{kc}")
                p.wait("scalar", "t", tg, tag=f"{tag}:act{ec},{tt}")
                if prev_stt is not None:
                    p.wait("scalar", "v", prev_stt,
                           tag=f"{tag}:tsil-war{ec},{tt}")
                if layer == 0:
                    sv = p.op("scalar", lambda e, P=PU:
                              e.activation(tsil, P[:, 0:TQ], AF.Silu,
                                           scale=self.col(gu + "_s")),
                              inc="s", tag=f"{tag}:silu{ec},{tt}")
                    gscol = self.col(gg + "_s")
                else:
                    sv = p.op("scalar", lambda e, P=PU:
                              e.activation(tsil, P[:, 0:TQ], AF.Sigmoid),
                              inc="s", tag=f"{tag}:sig{ec},{tt}")
                    gscol = self.col("one")
                p.wait("vector", "s", sv, tag=f"{tag}:hid{ec},{tt}")
                if tt == 0:
                    p.wait("vector", hw[sl], tag=f"{tag}:hbuf-war{ec}")
                vv = p.op("vector", lambda e, s=sl, n2=tt, P=PG, g_=gscol:
                          e.scalar_tensor_tensor(
                              hbuf[s][:, n2 * TQ:(n2 + 1) * TQ], P[:, 0:TQ],
                              g_, tsil, ALU.mult, ALU.mult),
                          inc="v", tag=f"{tag}:hid{ec},{tt}")
                prev_stt = vv
                mm_marks[(ec, tt)] = ("v", vv)
            mm_marks[ec] = p.chans["t"].n
            p.wait("sync", "v", vv, tag=f"{tag}:sth{ec}")
            p.dma(hid[ec * 128:(ec + 1) * 128, :], hbuf[sl][:, 0:T], hw[sl],
                  tag=f"{tag}:sth{ec}")
        wdn = [self.t(f"Fwd{layer}_{i}", 116 + i * 16, Ec * 128, BF16)
               for i in range(2)]
        fout = self.t("Ffout", 148, Dc * TQ, BF16)
        htile = self.t(f"Fht{layer}", 20, Ec * TQ, BF16)
        p.wait("sync", "tzw0", tag=f"{tag}:cache0")
        p.wait("sync", "tzw1", tag=f"{tag}:cache1")
        p.wait("sync", "hw0", tag=f"{tag}:hid-landed0")
        p.wait("sync", "hw1", tag=f"{tag}:hid-landed1")
        ad = [p.chan("ad0"), p.chan("ad1")]
        ev_marks = {}
        nhb = max(1, Ec // 8)
        for tt in range(NT):
            p.wait("sync", "t", tag=f"{tag}:ht-free{tt}")
            p.wait("sync", "v", tag=f"{tag}:ht-free-v{tt}")
            hmarks = {}
            for chn in ("resw", "ody"):
                if chn in p.chans:
                    p.wait("sync", chn, tag=f"{tag}:res-landed{tt}")
            p.wait("sync", "v", tag=f"{tag}:sta-free{tt}")
            rp = p.chan("rpf")
            src3r = res_src.rearrange("(cc p) t -> p cc t", p=128)[
                :, :, tt * TQ:(tt + 1) * TQ]
            p.dma(sta.rearrange("p (cc t) -> p cc t", cc=Dc), src3r, rp,
                  tag=f"{tag}:ldres{tt}")
            for hb in range(0, Ec, nhb):
                nb = min(nhb, Ec - hb)
                ch = ad[(hb // nhb) % 2]
                p.dma(htile.rearrange("p (ec t) -> p ec t", ec=Ec)[
                          :, hb:hb + nb, :],
                      hid.rearrange("(ec p) t -> p ec t", p=128)[
                          :, hb:hb + nb, tt * TQ:(tt + 1) * TQ],
                      ch, tag=f"{tag}:ldh{tt},{hb}")
                for k in range(hb, hb + nb):
                    hmarks[k] = (ch, ch.n)
            self.bg_region = 164
            p.wait("gpsimd", "t", tag=f"{tag}:bg-gate{tt}")
            for dc in range(Dc):
                sl = dc % 2
                PD = self.ps[4] if sl == 0 else self.ps[5]
                if dc >= 2:
                    chn, v_ = ev_marks[dc - 2]
                    p.wait("sync", chn, v_, tag=f"{tag}:wdfree{tt},{dc}")
                p.dma(wdn[sl].rearrange("p (kc j) -> p kc j", kc=Ec),
                      self.dram[dn_cache].rearrange(
                          "(kc p) o -> p kc o", p=128)[
                          :, :, dc * 128:(dc + 1) * 128],
                      wd[sl], tag=f"{tag}:ldd{tt},{dc}")
                p.wait("tensor", wd[sl], tag=f"{tag}:dmm{tt},{dc}")
                if dc >= 2:
                    chn, v_ = ev_marks[dc - 2]
                    p.wait("tensor", chn, v_, tag=f"{tag}:dbank{tt},{dc}")
                tmk = None
                for ec in range(Ec):
                    if dc == 0:
                        chh, vh2 = hmarks[ec]
                        p.wait("tensor", chh, vh2,
                               tag=f"{tag}:htw{tt},{ec}")
                    tmk = p.op("tensor", lambda e, k=ec, s=sl, P=PD,
                               s0=(ec == 0), st=(ec == Ec - 1):
                               e.matmul(P[:, 0:TQ],
                                        wdn[s][:, k * 128:(k + 1) * 128],
                                        htile[:, k * TQ:(k + 1) * TQ],
                                        start=s0, stop=st),
                               inc="t", tag=f"{tag}:dmm{tt},{dc},{ec}")
                p.wait("scalar", "t", tmk, tag=f"{tag}:dev{tt},{dc}")
                if layer == 0:
                    sv = p.op("scalar", lambda e, d=dc, P=PD:
                              e.activation(fout[:, d * TQ:(d + 1) * TQ],
                                           P[:, 0:TQ], AF.Identity,
                                           scale=self.col(gd_ + "_s")),
                              inc="s", tag=f"{tag}:dev{tt},{dc}")
                else:
                    sv = p.op("scalar", lambda e, d=dc, P=PD:
                              e.activation(fout[:, d * TQ:(d + 1) * TQ],
                                           P[:, 0:TQ], AF.Copy),
                              inc="s", tag=f"{tag}:dev{tt},{dc}")
                ev_marks[dc] = ("s", sv)
                self.bg_step(3, allow_collapse=True)
            self.bg_region = 170
            for chn in ("g0", "g1", "g2", "g3", "tzw0", "tzw1"):
                if chn in p.chans:
                    p.wait("vector", chn, tag=f"{tag}:bg-quiet{tt}")
                    p.wait("scalar", chn, tag=f"{tag}:bg-quiet-s{tt}")
            sev = p.chans["s"].n
            p.wait("vector", "s", sev, tag=f"{tag}:ln4{tt}")
            p.wait("vector", rp, tag=f"{tag}:sta-rdy{tt}")
            xb = self.t("Fxbuf", 180, Dc * TQ, BF16) if layer == 0 else None
            self.ln_post_add(fout, TQ, tt, res_src, res_dst, out_ch,
                             f"{tag}:pa{tt}", sta_ready=sta, make_xhat=xb)
        return out_ch

    def phase_L1A(self):
        p, c = self.p, self.c
        TQ, NT, Dc, T = c["TQ"], c["NT"], c["Dc"], c["T"]
        NKT, KVD = c["NKT"], c["KVD"]
        xhat = self.t("xh1", 20, Dc * T, BF16)
        p.wait("sync", "xhw", tag="1A:xh-landed")
        xld = [p.chan("ad0"), p.chan("ad1")]
        for tt in range(NT):
            p.wait("sync", "t", tag=f"1A:xh-free{tt}")
            p.dma(xhat.rearrange("p (cc t) -> p cc t", cc=Dc)[
                      :, :, tt * TQ:(tt + 1) * TQ],
                  self.dram["xh8"].rearrange("(cc p) t -> p cc t", p=128)[
                      :, :, tt * TQ:(tt + 1) * TQ],
                  xld[tt % 2], tag=f"1A:ldxh{tt}")
        p.wait("tensor", "ad0", tag="1A:xh-rdy0")
        p.wait("tensor", "ad1", tag="1A:xh-rdy1")
        wst = self.t("1Awst", 148, Dc * KVD, F32)
        wkb = self.t("1Awkb", 180, Dc * KVD, BF16)
        wd = p.chan("wd0")
        p.wait("sync", "t", tag="1A:fresh")
        p.wait("sync", "v", tag="1A:fresh-v")
        for chn in ("g0", "g1"):
            if chn in p.chans:
                p.wait("sync", chn, tag="1A:bg-quiet")
        p.dma(wst.rearrange("p (kc o) -> p kc o", kc=Dc),
              self.dram["wk1"].rearrange("(kc p) o -> p kc o", p=128),
              wd, tag="1A:ldk")
        p.wait("vector", wd, tag="1A:castk")
        vk = p.op("vector", lambda e: e.tensor_copy(wkb, wst), inc="v",
                  tag="1A:castk")
        p.wait("tensor", "v", vk, tag="1A:kmm")
        k1 = self.t("k1", 116, c["KVG"] * T, BF16)
        v1 = self.t("v1", 132, NKT * KVD, BF16)
        tmpf = self.t("tmpf", 200, 512, F32)
        for g in range(c["KVG"]):
            for n in range(NT):
                b = self.ps[n % 2]
                tmk = None
                for kc in range(Dc):
                    tmk = p.op("tensor", lambda e, k=kc, nn=n, gg=g, P=b,
                               s0=(kc == 0), st=(kc == Dc - 1):
                               e.matmul(P[:, 0:TQ],
                                        wkb[:, k * KVD + gg * 128:
                                            k * KVD + (gg + 1) * 128],
                                        xhat[:, k * T + nn * TQ:
                                             k * T + (nn + 1) * TQ],
                                        start=s0, stop=st),
                               inc="t", tag=f"1A:kmm{g},{n},{kc}")
                p.wait("vector", "t", tmk, tag=f"1A:kev{g},{n}")
                vv = p.op("vector", lambda e, gg=g, nn=n, P=b:
                          e.tensor_copy(k1[:, gg * T + nn * TQ:
                                           gg * T + (nn + 1) * TQ],
                                        P[:, 0:TQ]),
                          inc="v", tag=f"1A:kev{g},{n}")
                p.wait("tensor", "v", vv, tag=f"1A:kb{g},{n}")
        vlast = p.chans["v"].n
        p.wait("sync", "v", vlast, tag="1A:wst-free2")
        p.dma(wst.rearrange("p (kc o) -> p kc o", kc=Dc),
              self.dram["wv1"].rearrange("(kc p) o -> p kc o", p=128),
              wd, tag="1A:ldv")
        p.wait("vector", wd, tag="1A:castv")
        p.wait("vector", "t", tag="1A:castv-t")
        vk = p.op("vector", lambda e: e.tensor_copy(wkb, wst), inc="v",
                  tag="1A:castv")
        p.wait("tensor", "v", vk, tag="1A:vmm")
        VW = min(TQ, KVD)
        nV = KVD // VW
        for ttk in range(NKT):
            for n in range(nV):
                b = self.ps[n % 2]
                tmk = None
                for kc in range(Dc):
                    tmk = p.op("tensor", lambda e, k=kc, nn=n, t2=ttk, P=b,
                               s0=(kc == 0), st=(kc == Dc - 1):
                               e.matmul(P[:, 0:VW],
                                        xhat[:, k * T + t2 * 128:
                                             k * T + (t2 + 1) * 128],
                                        wkb[:, k * KVD + nn * VW:
                                            k * KVD + (nn + 1) * VW],
                                        start=s0, stop=st),
                               inc="t", tag=f"1A:vmm{ttk},{n},{kc}")
                p.wait("vector", "t", tmk, tag=f"1A:vev{ttk},{n}")
                vv = p.op("vector", lambda e, nn=n, t2=ttk, P=b, w=VW:
                          e.tensor_copy(v1[:, t2 * KVD + nn * w:
                                           t2 * KVD + (nn + 1) * w],
                                        P[:, 0:w]),
                          inc="v", tag=f"1A:vev{ttk},{n}")
                p.wait("tensor", "v", vv, tag=f"1A:vb{ttk},{n}")

    def phase_L1C(self):
        p, c = self.p, self.c
        TQ, NT, T, Dc = c["TQ"], c["NT"], c["T"], c["Dc"]
        NH, NKT, NOFF, KVD = c["NH"], c["NKT"], c["NOFF"], c["KVD"]
        xhat = self.t("xh1", 20, Dc * T, BF16)
        k1 = self.t("k1", 116, c["KVG"] * T, BF16)
        v1 = self.t("v1", 132, NKT * KVD, BF16)
        qh = self.t("1Cqh", 108, T, BF16)
        ctxh = self.t("1Cctxh", 112, T, BF16)
        pT = [self.t(f"1CpT{i}", 148 + i * 2, 2 * TQ, BF16) for i in range(2)]
        rbf = self.t("1Crbf", 152, TQ, F32)
        wstq = [self.t(f"1Cwstq{i}", 84 + i * (Dc * 128 * 4) / KB,
                       Dc * 128, F32) for i in range(2)]
        wqb = [self.t(f"1Cwqb{i}", 100 + i * (Dc * 128 * 2) / KB,
                      Dc * 128, BF16) for i in range(2)]
        statr = self.t_perm("statr")
        masks = self.t_perm("masks")
        wd = [p.chan("wd0"), p.chan("wd1")]
        abw = p.chan("abw")
        for chn in ("resw", "ody", "rpf"):
            if chn in p.chans:
                p.wait("sync", chn, tag="1C:res-quiet")
        mh_col = self.col("mh1")
        self._smark = {}
        self._rbe_prev = None
        self._ctx_marks = {}
        cast_marks = {}
        for h in range(NH):
            g = h // (NH // c["KVG"])
            sl = h % 2
            if h < 2:
                p.wait("sync", "t", tag=f"1C:fresh{h}")
                p.wait("sync", "v", tag=f"1C:fresh-v{h}")
                p.wait("sync", "s", tag=f"1C:fresh-s{h}")
            else:
                p.wait("sync", "s", cast_marks[h - 2], tag=f"1C:wstfree{h}")
            src3 = self.dram["wq1"].rearrange("(kc p) o -> p kc o", p=128)[
                :, :, h * 128:(h + 1) * 128]
            p.dma(wstq[sl].rearrange("p (kc j) -> p kc j", kc=Dc), src3,
                  wd[sl], tag=f"1C:wldq{h}")
            self.bg_step(1, allow_collapse=True)
            p.wait("scalar", wd[sl], tag=f"1C:wcast{h}")
            cast_marks[h] = p.op(
                "scalar", lambda e, s=sl: e.copy(wqb[s], wstq[s]),
                inc="s", tag=f"1C:wcast{h}")
            p.wait("tensor", "s", cast_marks[h], tag=f"1C:q-wait{h}")
            PQm = self.ps[0]
            for n in range(NT):
                tmk = None
                for kc in range(Dc):
                    tmk = p.op("tensor", lambda e, k=kc, nn=n, s=sl,
                               s0=(kc == 0), st=(kc == Dc - 1):
                               e.matmul(PQm[:, 0:TQ],
                                        wqb[s][:, k * 128:(k + 1) * 128],
                                        xhat[:, k * T + nn * TQ:
                                             k * T + (nn + 1) * TQ],
                                        start=s0, stop=st),
                               inc="t", tag=f"1C:qmm{h},{n},{kc}")
                p.wait("vector", "t", tmk, tag=f"1C:qev{h},{n}")
                vv = p.op("vector", lambda e, nn=n:
                          e.tensor_copy(qh[:, nn * TQ:(nn + 1) * TQ],
                                        PQm[:, 0:TQ]),
                          inc="v", tag=f"1C:qev{h},{n}")
                p.wait("tensor", "v", vv, tag=f"1C:qb{h},{n}")
            p.wait("vector", "abw", tag=f"1C:ctx-war{h}")
            self.head_bound(qh[:, 0:T], k1[:, g * T:g * T + 128], mh_col,
                            f"1C:hb{h}")
            SP = [self.ps["A"], self.ps["B"]]
            for qc in range(NT):
                PC = self.ps[4] if qc % 2 == 0 else self.ps[5]
                PS6 = self.ps[6]
                nkt = (qc + 1) * NOFF
                npair = nkt // 2
                exp_marks = {}

                def smm(pi2, qv=qc, nk=nkt, np_=npair):
                    if pi2 >= np_:
                        return
                    if pi2 >= 2:
                        chn, vv_ = exp_marks[pi2 - 2]
                        p.wait("tensor", chn, vv_,
                               tag=f"1C:Sfree{h},{qv},{pi2}")
                    vm = None
                    mk = None
                    for half in range(2):
                        kt = 2 * pi2 + half
                        mk = p.op("tensor", lambda e, k=kt, q=qv,
                                  bb=pi2 % 2, gg=g, hh=half:
                                  e.matmul(SP[bb][:, hh * TQ:(hh + 1) * TQ],
                                           k1[:, gg * T + k * 128:
                                              gg * T + (k + 1) * 128],
                                           qh[:, q * TQ:(q + 1) * TQ],
                                           start=True, stop=True),
                                  inc="t", tag=f"1C:smm{h},{qv},{kt}")
                        diag = kt - qv * NOFF
                        if diag >= 0:
                            p.wait("vector", "t", mk,
                                   tag=f"1C:mask{h},{qv},{kt}")
                            vm = p.op("vector", lambda e, bb=pi2 % 2,
                                      dd=diag, hh=half:
                                      e.tensor_tensor(
                                          SP[bb][:, hh * TQ:(hh + 1) * TQ],
                                          SP[bb][:, hh * TQ:(hh + 1) * TQ],
                                          masks[:, dd * TQ:(dd + 1) * TQ],
                                          ALU.add),
                                      inc="v", tag=f"1C:mask{h},{qv},{kt}")
                    self._smark[(1, h, qv, pi2)] = (mk, vm)

                smm(0)
                for pi in range(npair):
                    b = pi % 2
                    smm(pi + 1)
                    mk, vm = self._smark[(1, h, qc, pi)]
                    if vm is not None:
                        p.wait("scalar", "v", vm, tag=f"1C:expm{h},{qc},{pi}")
                    p.wait("scalar", "t", mk, tag=f"1C:exp{h},{qc},{pi}")
                    sv = p.op("scalar", lambda e, bb=b, pp=pi:
                              e.activation(pT[pp % 2], SP[bb][:, 0:2 * TQ],
                                           AF.Exp, bias=mh_col,
                                           scale=self.col("sm")),
                              inc="s", tag=f"1C:exp{h},{qc},{pi}")
                    exp_marks[pi] = ("s", sv)
                    p.wait("tensor", "s", sv, tag=f"1C:pv{h},{qc},{pi}")
                    if pi == 0 and self._rbe_prev is not None:
                        p.wait("tensor", "s", self._rbe_prev,
                               tag=f"1C:sumfree{h},{qc}")
                    if pi == 0 and self._ctx_marks.get(qc % 2) is not None:
                        p.wait("tensor", "v", self._ctx_marks[qc % 2],
                               tag=f"1C:pcfree{h},{qc}")
                    for half in range(2):
                        kt = 2 * pi + half
                        p.op("tensor", lambda e, k=kt, gg=g, pp=pi, hh=half,
                             P=PC, s0=(kt == 0), st=(kt == nkt - 1):
                             e.matmul(P[:, 0:TQ],
                                      v1[:, k * KVD + gg * 128:
                                         k * KVD + (gg + 1) * 128],
                                      pT[pp % 2][:, hh * TQ:(hh + 1) * TQ],
                                      start=s0, stop=st),
                             inc="t", tag=f"1C:pv{h},{qc},{kt}")
                        tsum = p.op("tensor", lambda e, pp=pi, hh=half,
                                    s0=(kt == 0), st=(kt == nkt - 1):
                                    e.matmul(PS6[0:1, 0:TQ],
                                             self.t_perm("onesb")[:, 0:1],
                                             pT[pp % 2][:, hh * TQ:(hh + 1) * TQ],
                                             start=s0, stop=st),
                                    inc="t", tag=f"1C:sum{h},{qc},{kt}")
                p.wait("vector", "t", tsum, tag=f"1C:recip{h},{qc}")
                srow = statr[0:1, 0:TQ]
                vv = p.op("vector", lambda e: e.reciprocal(
                    srow, PS6[0:1, 0:TQ]), inc="v", tag=f"1C:recip{h},{qc}")
                p.wait("tensor", "v", vv, tag=f"1C:rb{h},{qc}")
                tb = p.op("tensor", lambda e: e.matmul(
                    PS6[:, 0:TQ], self.t_perm("onesf")[0:1, 0:128], srow,
                    start=True, stop=True), inc="t", tag=f"1C:rb{h},{qc}")
                p.wait("scalar", "t", tb, tag=f"1C:rbe{h},{qc}")
                self._rbe_prev = p.op("scalar", lambda e: e.activation(
                    rbf, PS6[:, 0:TQ], AF.Copy), inc="s",
                    tag=f"1C:rbe{h},{qc}")
                p.wait("vector", "s", self._rbe_prev, tag=f"1C:ctx{h},{qc}")
                self._ctx_marks[qc % 2] = p.op(
                    "vector", lambda e, q=qc, P=PC: e.tensor_tensor(
                        ctxh[:, q * TQ:(q + 1) * TQ], P[:, 0:TQ], rbf,
                        ALU.mult),
                    inc="v", tag=f"1C:ctx{h},{qc}")
                self.bg_step(12)
            p.wait("sync", "v", tag=f"1C:stab{h}")
            p.dma(self.dram["ab"][h * 128:(h + 1) * 128, :], ctxh[:, 0:T],
                  abw, tag=f"1C:stab{h}")

    def barrier(self, name):
        p = self.p
        ph = p.chan("ph")
        for e in ("tensor", "scalar", "sync", "gpsimd"):
            p.op(e, lambda eng: eng.nop(), inc=ph, tag=f"bar:{name}:{e}")
        p.op("vector", lambda eng: eng.engine_nop(), inc=ph,
             tag=f"bar:{name}:vector")
        tgt = ph.n
        for e in ("tensor", "vector", "scalar", "sync", "gpsimd"):
            p.wait(e, ph, tgt, tag=f"bar:{name}")

    def build(self, upto=None):
        self.declare()
        self.bg = []
        steps = [
            ("prelude", lambda: (self.prelude(), self.bg_enqueue())),
            ("A", self.phase_A),
            ("B", self.phase_B),
            ("C", self.phase_C),
            ("D0", lambda: self.phase_D("wo0", self.b_o0(), self.dram["x"],
                                        self.dram["res"], "resw", "D0")),
            ("T0fin", lambda: self.bg_finish(7)),
            ("F0", lambda: self.ffn(0, self.dram["res"], self.dram["res"],
                                    "resw", "F0")),
            ("L1A", self.phase_L1A),
            ("L1C", self.phase_L1C),
            ("D1", lambda: self.phase_D("wo1", self.b_o1(), self.dram["res"],
                                        self.dram["res"], "resw", "D1")),
            ("T1fin", lambda: self.bg_finish(0)),
            ("F1", lambda: self.ffn(1, self.dram["res"], self.dram["y"],
                                    "ody", "F1")),
        ]
        for name, fn in steps:
            fn()
            self.barrier(name)
            if upto is not None and name == upto:
                break
        for chn in ("resw", "kvw", "abw", "tzw0", "tzw1", "ody"):
            if chn in self.p.chans:
                self.p.wait("sync", chn, tag=f"final:{chn}")
        self.p.emit()
        return self.nc


def build_nc(cfg, upto=None, debug_outputs=()):
    mk = MK(cfg, debug_outputs)
    nc = mk.build(upto)
    mk.stack.close()
    return nc, mk



def kernel(**inp):
    cfg = FULL
    c = derive(cfg)
    nc, _ = build_nc(cfg)
    x = np.asarray(inp["L0_attn_x"], np.float32)
    common = host_inputs(c, inp)
    in_maps = []
    for b in range(8):
        m = dict(common)
        m["x"] = np.ascontiguousarray(x[b].T)
        in_maps.append(m)
    res = run_bass_kernel_spmd(nc, in_maps, core_ids=list(range(8)))
    out = np.stack([np.asarray(r["y"], np.float32).T for r in res.results], 0)
    return np.ascontiguousarray(out)
